# version 15
# baseline (speedup 1.0000x reference)
import contextlib
import numpy as np
import ml_dtypes
import concourse.bass as bass
import concourse.mybir as mybir
from concourse.bass_utils import run_bass_kernel_spmd
from concourse.alu_op_type import AluOpType as Op

F32 = mybir.dt.float32
BF16 = mybir.dt.bfloat16
F16 = mybir.dt.float16
F8 = mybir.dt.float8e4
AF = mybir.ActivationFunctionType
AX = mybir.AxisListType

P = 128
T = 4096
D_MODEL = 1024
DC = 8
H_FFN = 4096
HC = 32
E = 4
D_INNER = 2048
DINC = 16
NH = 32
HD = 64
CONV_DIM = 2304
CVC = 18
D_IN_PROJ = 4384
NST = 128
CH = 128
NCH = T // CH
FPH = 256
NFPH = T // FPH
MG = 1536.0
NEG = -1e30

ENGINES = ("sync", "tensor", "vector", "scalar", "gpsimd")
SEM_OF = {"tensor": "s_te", "vector": "s_ve", "scalar": "s_ac", "gpsimd": "s_gp"}
DMA_POOL = {"sync": ("s_ld0", "s_ld1", "s_ld2", "s_ld3"),
            "gpsimd": ("s_gd0", "s_gd1")}
SEMS = ("s_ld0", "s_ld1", "s_ld2", "s_ld3", "s_gd0", "s_gd1",
        "s_te", "s_ve", "s_ac", "s_gp")


class Mark:
    __slots__ = ("sem", "seq", "val", "used", "step", "vc")

    def __init__(self, sem, seq, step, vc):
        self.sem, self.seq, self.step, self.vc = sem, seq, step, vc
        self.val = None
        self.used = False


class _NoMark:
    used = False
    val = None


_NOMARK = _NoMark()


class PB:

    def __init__(self):
        self.prog = []
        self.seq = {s: 0 for s in SEMS}
        self.vc = {e: {} for e in ENGINES}
        self.bufs = {}
        self._marks = {s: [] for s in SEMS}
        self.dmacnt = {q: 0 for q in DMA_POOL}

    def _buf(self, name):
        b = self.bufs.get(name)
        if b is None:
            b = {"w": None, "r": []}
            self.bufs[name] = b
        return b

    def op(self, eng, fn, reads=(), writes=(), dma=False, force_mark=False):
        need = {}
        if dma:
            qsem = DMA_POOL[eng][self.dmacnt[eng] % len(DMA_POOL[eng])]
            self.dmacnt[eng] += 1
            if self.seq[qsem] > 0:
                need[qsem] = self.seq[qsem] - 1
        for r in reads:
            b = self._buf(r)
            if b["w"] is not None:
                m = b["w"]
                if need.get(m.sem, -1) < m.seq:
                    need[m.sem] = m.seq
        for w in writes:
            b = self._buf(w)
            if b["w"] is not None:
                m = b["w"]
                if need.get(m.sem, -1) < m.seq:
                    need[m.sem] = m.seq
            for m in b["r"]:
                if need.get(m.sem, -1) < m.seq:
                    need[m.sem] = m.seq
        waits = []
        vc = self.vc[eng]
        for sem, seqv in need.items():
            if vc.get(sem, -1) >= seqv:
                continue
            m = self._marks[sem][seqv]
            m.used = True
            waits.append(m)
            for s2, q2 in m.vc.items():
                if vc.get(s2, -1) < q2:
                    vc[s2] = q2
            vc[sem] = max(vc.get(sem, -1), seqv)
        sem = qsem if dma else SEM_OF[eng]
        myvc = dict(vc)
        mark = Mark(sem, self.seq[sem], 16 if dma else 1, myvc)
        myvc[sem] = max(myvc.get(sem, -1), mark.seq)
        if force_mark or dma:
            mark.used = True
        self._marks[sem].append(mark)
        self.seq[sem] += 1
        for w in writes:
            b = self._buf(w)
            b["w"] = mark
            b["r"] = []
        for r in reads:
            self._buf(r)["r"].append(mark)
        self.prog.append((eng, fn, mark, waits))
        return mark

    def barrier(self):
        for eng in ENGINES:
            waits = []
            vc = self.vc[eng]
            for sem in SEMS:
                seqv = self.seq[sem] - 1
                if seqv < 0 or vc.get(sem, -1) >= seqv:
                    continue
                m = self._marks[sem][seqv]
                m.used = True
                waits.append(m)
                for s2, q2 in m.vc.items():
                    if vc.get(s2, -1) < q2:
                        vc[s2] = q2
                vc[sem] = seqv
            if waits:
                self.prog.append((eng, None, _NOMARK, waits))

    def resolve(self):
        self.final = {}
        for sem, marks in self._marks.items():
            c = 0
            for m in marks:
                if m.used:
                    c += m.step
                    m.val = c
            self.final[sem] = c

    def emit(self, eng_name, e, sems):
        for eng, fn, mark, waits in self.prog:
            if eng != eng_name:
                continue
            for m in waits:
                e.wait_ge(sems[m.sem], m.val)
            if fn is None:
                continue
            ins = fn(e)
            if mark.used:
                assert ins is not None, f"op on {eng} needs a sync-bearing instr"
                ins.then_inc(sems[mark.sem], mark.step)


STAGES = ("F1", "M0", "M1", "M2", "F2")
DEBUG_FFN = False


def set_T(t):
    global T, NCH, NFPH
    T = t
    NCH = T // CH
    NFPH = T // FPH


def build_nc():
    nc = bass.Bass("TRN2", target_bir_lowering=False)
    pb = PB()

    xT = nc.dram_tensor("xT", [D_MODEL, T], F32, kind="ExternalInput").ap()
    w1T = nc.dram_tensor("w1T", [D_MODEL, H_FFN], F32, kind="ExternalInput").ap()
    w2T = nc.dram_tensor("w2T", [H_FFN, D_MODEL], F32, kind="ExternalInput").ap()
    cwf = nc.dram_tensor("cwf", [H_FFN, 3], F32, kind="ExternalInput").ap()
    cbf = nc.dram_tensor("cbf", [H_FFN], F32, kind="ExternalInput").ap()
    rwT = nc.dram_tensor("rwT", [D_MODEL, E], F32, kind="ExternalInput").ap()
    inw = nc.dram_tensor("inw", [E, D_MODEL, D_IN_PROJ], F32, kind="ExternalInput").ap()
    cwm = nc.dram_tensor("cwm", [E, CONV_DIM, 4], F32, kind="ExternalInput").ap()
    cbm = nc.dram_tensor("cbm", [E, CONV_DIM], F32, kind="ExternalInput").ap()
    dtb = nc.dram_tensor("dtb", [E, NH], F32, kind="ExternalInput").ap()
    alog = nc.dram_tensor("alog", [E, NH], F32, kind="ExternalInput").ap()
    dskp = nc.dram_tensor("dskp", [E, NH], F32, kind="ExternalInput").ap()
    nrmw = nc.dram_tensor("nrmw", [E, D_INNER], F32, kind="ExternalInput").ap()
    outw = nc.dram_tensor("outw", [E, D_INNER, D_MODEL], F32, kind="ExternalInput").ap()
    out = nc.dram_tensor("out", [D_MODEL, T], F32, kind="ExternalOutput").ap()
    if DEBUG_FFN:
        dxq = nc.dram_tensor("dxq", [P, DC, 258], F32, kind="ExternalOutput").ap()
        dcs = nc.dram_tensor("dcs", [P, HC, 256], F32, kind="ExternalOutput").ap()
        dhq = nc.dram_tensor("dhq", [P, HC, 256], F32, kind="ExternalOutput").ap()
        dhh = nc.dram_tensor("dhh", [P, 4, 258], F32, kind="ExternalOutput").ap()
        drw = nc.dram_tensor("drw", [8, T], F32, kind="ExternalOutput").ap()
        dsw = nc.dram_tensor("dsw", [1, 4], F32, kind="ExternalOutput").ap()

    x1 = nc.dram_tensor("x1", [D_MODEL, T], F32, kind="Internal").ap()
    x2 = nc.dram_tensor("x2", [D_MODEL, T], F32, kind="Internal").ap()
    ln2hi = nc.dram_tensor("ln2hi", [D_MODEL, T], BF16, kind="Internal").ap()
    ln2lo = nc.dram_tensor("ln2lo", [D_MODEL, T], BF16, kind="Internal").ap()
    zx = nc.dram_tensor("zx", [35 * P, T], F32, kind="Internal").ap()
    dtsfl = nc.dram_tensor("dtsfl", [P, T], F32, kind="Internal").ap()
    decfl = nc.dram_tensor("decfl", [NCH, NH], F32, kind="Internal").ap()
    rowfl = nc.dram_tensor("rowfl", [8, T], F32, kind="Internal").ap()

    def fmv(ap):
        return ap.rearrange("(c p) t -> p c t", p=P)

    stack = contextlib.ExitStack()
    with stack:
        ident = stack.enter_context(nc.sbuf_tensor("ident", [P, P], F32))
        masktile = stack.enter_context(nc.sbuf_tensor("masktile", [P, P], F32))
        onescol = stack.enter_context(nc.sbuf_tensor("onescol", [P, 1], F32))
        onesrow = stack.enter_context(nc.sbuf_tensor("onesrow", [1, P], F32))
        r_sb = stack.enter_context(nc.sbuf_tensor("r_sb", [P, NCH * E], F32))
        ps = [stack.enter_context(nc.psum_tensor(f"ps{i}", [P, 512], F32))
              for i in range(8)]

        pb.op("gpsimd", lambda e: e.memset(ident[:], 0.0), writes=("ident",))
        pb.op("gpsimd", (lambda e: e.affine_select(
            out=ident[:], in_=ident[:], compare_op=Op.not_equal, fill=1.0,
            base=0, pattern=[[-1, P]], channel_multiplier=1)),
            reads=("ident",), writes=("ident",))
        pb.op("gpsimd", lambda e: e.memset(masktile[:], 0.0), writes=("masktile",))
        pb.op("gpsimd", (lambda e: e.affine_select(
            out=masktile[:], in_=masktile[:], compare_op=Op.is_ge, fill=NEG,
            base=0, pattern=[[1, P]], channel_multiplier=-1)),
            reads=("masktile",), writes=("masktile",))
        pb.op("gpsimd", lambda e: e.memset(onescol[:], 1.0), writes=("onescol",))
        pb.op("gpsimd", lambda e: e.memset(onesrow[:], 1.0), writes=("onesrow",))
        epscol = stack.enter_context(nc.sbuf_tensor("epscol", [P, 1], F32))
        pb.op("gpsimd", lambda e: e.memset(epscol[:], 1e-5), writes=("epscol",))

        def bcast(tag2, rowap, n, bank, reads):
            pb.op("tensor", (lambda e, r=rowap, n=n, bank=bank: e.matmul(
                ps[bank][0:P, 0:n], onesrow[:], r, start=True, stop=True)),
                reads=tuple(reads) + ("onesrow",), writes=(f"ps{bank}",))

        trow = stack.enter_context(nc.sbuf_tensor("trow", [8, P], F32))

        def amax_row(tag2, absbuf, absname, nch, n, rowslice, rowname, psA, psB):
            w = nch // 2
            while w >= 1:
                pb.op("vector", (lambda e, w=w: e.tensor_tensor(
                    out=absbuf[:, 0:w, 0:n], in0=absbuf[:, 0:w, 0:n],
                    in1=absbuf[:, w:2 * w, 0:n], op=Op.max)),
                    reads=(absname,), writes=(absname,))
                w //= 2
            ng = n // P
            def trs(e):
                for g in range(ng):
                    ins = e.transpose(ps[psA][0:P, g * P:(g + 1) * P],
                                      absbuf[:, 0, g * P:(g + 1) * P], ident[:])
                return ins
            pb.op("tensor", trs, reads=(absname, "ident"), writes=(f"ps{psA}",))
            def reds(e):
                for g in range(ng):
                    ins = e.tensor_reduce(out=absbuf[0:P, 0, g:g + 1],
                                          in_=ps[psA][0:P, g * P:(g + 1) * P],
                                          axis=AX.X, op=Op.max)
                return ins
            pb.op("vector", reds, reads=(f"ps{psA}",), writes=(absname,))
            pb.op("tensor", (lambda e: e.transpose(ps[psB][0:ng, 0:P],
                                                   absbuf[:, 0, 0:ng], ident[:])),
                  reads=(absname, "ident"), writes=(f"ps{psB}",))
            pb.op("vector", (lambda e: e.tensor_copy(trow[0:ng, :], ps[psB][0:ng, 0:P])),
                  reads=(f"ps{psB}",), writes=("trow",))
            pb.op("gpsimd", (lambda e: e.dma_start(out=rowslice, in_=trow[0:ng, :])),
                  reads=("trow",), writes=(rowname,), dma=True)

        def ffn_call(tag, src, dst, half, is_out):
            rowsd = f"{tag}rowsd"
            with contextlib.ExitStack() as st1:
                w1q = st1.enter_context(nc.sbuf_tensor(f"{tag}w1q", [P, DC, H_FFN], F8))
                w2q = st1.enter_context(nc.sbuf_tensor(f"{tag}w2q", [P, HC, D_MODEL], F8))
                cw = st1.enter_context(nc.sbuf_tensor(f"{tag}cw", [P, HC, 3], F32))
                cb = st1.enter_context(nc.sbuf_tensor(f"{tag}cb", [P, HC], F32))
                swr = st1.enter_context(nc.sbuf_tensor(f"{tag}swr", [1, 4], F32))

                with contextlib.ExitStack() as st2:
                    wbuf = st2.enter_context(
                        nc.sbuf_tensor(f"{tag}wbuf", [P, 2, 4096], F32))
                    wtmp = st2.enter_context(nc.sbuf_tensor(f"{tag}wtmp", [1 * P, 4096], F16))
                    wsc = st2.enter_context(nc.sbuf_tensor(f"{tag}wsc", [P, 4], F32))
                    for wi, (wsrc, nchunk, wid) in enumerate(
                            ((w1T, DC, H_FFN), (w2T, HC, D_MODEL))):
                        wn = f"{tag}w1q" if wi == 0 else f"{tag}w2q"
                        wdst = w1q if wi == 0 else w2q
                        view = wsrc.rearrange("(c p) w -> p c w", p=P)
                        npc = 4096 // wid
                        npieces = nchunk // npc
                        for pc in range(npieces):
                            pp = pc % 2
                            bw = f"{tag}wbuf{pp}"
                            pb.op("sync", (lambda e, pc=pc, pp=pp, v=view, npc=npc, w=wid:
                                           e.dma_start(out=wbuf[:, pp].rearrange(
                                               "p (a b) -> p a b", a=npc),
                                               in_=v[:, pc * npc:(pc + 1) * npc, :])),
                                  writes=(bw,), dma=True)
                            if pc == 0:
                                pb.op("vector", (lambda e, pp=pp: e.tensor_reduce(
                                    out=wsc[:, 0:1], in_=wbuf[:, pp], axis=AX.X,
                                    op=Op.add, apply_absolute_value=True)),
                                    reads=(bw,), writes=(f"{tag}wsc",))
                            else:
                                pb.op("vector", (lambda e, pp=pp: e.tensor_reduce(
                                    out=wsc[:, 1:2], in_=wbuf[:, pp], axis=AX.X,
                                    op=Op.add, apply_absolute_value=True)),
                                    reads=(bw,), writes=(f"{tag}wsc",))
                                pb.op("vector", (lambda e: e.tensor_tensor(
                                    out=wsc[:, 0:1], in0=wsc[:, 0:1], in1=wsc[:, 1:2],
                                    op=Op.add)), reads=(f"{tag}wsc",), writes=(f"{tag}wsc",))
                        pb.op("tensor", (lambda e: e.matmul(
                            ps[7][0:1, 0:1], wsc[:, 0:1], onescol[:], start=True, stop=True)),
                            reads=(f"{tag}wsc", "onescol"), writes=("ps7",))
                        pb.op("vector", (lambda e, wi=wi, n=nchunk, w=wid: e.tensor_scalar(
                            out=swr[0:1, wi + 2:wi + 3], in0=ps[7][0:1, 0:1],
                            scalar1=1.0 / (n * P * w), scalar2=1e-5,
                            op0=Op.mult, op1=Op.max)),
                            reads=("ps7",), writes=(f"{tag}swr",))
                        pb.op("vector", (lambda e, wi=wi: e.reciprocal(
                            out=swr[0:1, wi:wi + 1], in_=swr[0:1, wi + 2:wi + 3])),
                            reads=(f"{tag}swr",), writes=(f"{tag}swr",))
                        pb.op("tensor", (lambda e, wi=wi: e.matmul(
                            ps[7][0:P, 1:2], onesrow[:], swr[0:1, wi:wi + 1],
                            start=True, stop=True)),
                            reads=(f"{tag}swr", "onesrow"), writes=("ps7",))
                        pb.op("vector", (lambda e: e.tensor_copy(
                            wsc[:, 3:4], ps[7][0:P, 1:2])),
                            reads=("ps7",), writes=(f"{tag}wsc",))
                        for pc in range(npieces):
                            pp = pc % 2
                            bw = f"{tag}wbuf{pp}"
                            pb.op("sync", (lambda e, pc=pc, pp=pp, v=view, npc=npc, w=wid:
                                           e.dma_start(out=wbuf[:, pp].rearrange(
                                               "p (a b) -> p a b", a=npc),
                                               in_=v[:, pc * npc:(pc + 1) * npc, :])),
                                  writes=(bw,), dma=True)
                            pb.op("vector", (lambda e, pp=pp: e.tensor_scalar(
                                out=wbuf[:, pp], in0=wbuf[:, pp], scalar1=wsc[:, 3:4],
                                scalar2=1.4999, op0=Op.mult, op1=Op.min)),
                                reads=(bw, f"{tag}wsc"), writes=(bw,))
                            pb.op("vector", (lambda e, pp=pp: e.tensor_scalar(
                                out=wtmp[:, :], in0=wbuf[:, pp], scalar1=-1.4999,
                                scalar2=MG, op0=Op.max, op1=Op.add)),
                                reads=(bw,), writes=(f"{tag}wtmp",))
                            pb.op("scalar", (lambda e, wd=wdst, pc=pc, npc=npc, w=wid:
                                             e.activation(out=wd[:, pc * npc:(pc + 1) * npc, :]
                                                          .rearrange("p a b -> p (a b)"),
                                                          in_=wtmp[:, :], func=AF.Copy,
                                                          bias=-MG)),
                                  reads=(f"{tag}wtmp",), writes=(wn,))
                    pb.barrier()

                pb.op("sync", (lambda e: e.dma_start(
                    out=cw[:, :, :], in_=cwf.rearrange("(c p) k -> p c k", p=P))),
                    writes=(f"{tag}cw",), dma=True)
                pb.op("sync", (lambda e: e.dma_start(
                    out=cb[:, :], in_=cbf.rearrange("(c p) -> p c", p=P),
                    allow_slow_non_contiguous=True)),
                    writes=(f"{tag}cb",), dma=True)

                with contextlib.ExitStack() as st2:
                    xp = st2.enter_context(nc.sbuf_tensor(f"{tag}xp", [P, 2, DC, 512], F32))
                    sq = st2.enter_context(nc.sbuf_tensor(f"{tag}sq", [P, DC, 512], F32))
                    rt = st2.enter_context(nc.sbuf_tensor(f"{tag}rt", [1, 8, 512], F32))
                    mbc = st2.enter_context(nc.sbuf_tensor(f"{tag}mbc", [P, 512], F32))
                    for i in range(T // 512):
                        pp = i % 2
                        bx = f"{tag}xp{pp}"
                        sl = slice(i * 512, (i + 1) * 512)
                        pb.op("sync", (lambda e, sl=sl, pp=pp: e.dma_start(
                            out=xp[:, pp], in_=fmv(src)[:, :, sl])),
                            writes=(bx,), dma=True)
                        pb.op("scalar", (lambda e, pp=pp: e.activation(
                            out=sq[:], in_=xp[:, pp], func=AF.Square)),
                            reads=(bx,), writes=(f"{tag}sq",))
                        def mmsum(e, pp=pp):
                            for dc in range(DC):
                                ins = e.matmul(ps[6][0:1, 0:512], onescol[:],
                                               xp[:, pp, dc], start=(dc == 0),
                                               stop=(dc == DC - 1))
                            return ins
                        pb.op("tensor", mmsum, reads=(bx, "onescol"), writes=("ps6",))
                        def mmsq(e):
                            for dc in range(DC):
                                ins = e.matmul(ps[7][0:1, 0:512], onescol[:],
                                               sq[:, dc], start=(dc == 0),
                                               stop=(dc == DC - 1))
                            return ins
                        pb.op("tensor", mmsq, reads=(f"{tag}sq", "onescol"), writes=("ps7",))
                        pb.op("vector", (lambda e: e.tensor_scalar(
                            out=rt[0:1, 4], in0=ps[6][0:1, 0:512],
                            scalar1=1.0 / D_MODEL, scalar2=None, op0=Op.mult, op1=Op.bypass)),
                            reads=("ps6",), writes=(f"{tag}rt",))
                        pb.op("vector", (lambda e: e.tensor_tensor(
                            out=rt[0:1, 6], in0=rt[0:1, 4], in1=rt[0:1, 4], op=Op.mult)),
                            reads=(f"{tag}rt",), writes=(f"{tag}rt",))
                        pb.op("vector", (lambda e: e.scalar_tensor_tensor(
                            out=rt[0:1, 5], in0=ps[7][0:1, 0:512], scalar=1.0 / D_MODEL,
                            in1=rt[0:1, 6], op0=Op.mult, op1=Op.subtract)),
                            reads=("ps7", f"{tag}rt"), writes=(f"{tag}rt",))
                        pb.op("scalar", (lambda e: e.activation(
                            out=rt[0:1, 6], in_=rt[0:1, 5], func=AF.Ln,
                            bias=epscol[0:1, 0:1])),
                            reads=(f"{tag}rt", "epscol"), writes=(f"{tag}rt",))
                        pb.op("scalar", (lambda e: e.activation(
                            out=rt[0:1, 0], in_=rt[0:1, 6], func=AF.Exp, scale=-0.5)),
                            reads=(f"{tag}rt",), writes=(f"{tag}rt",))
                        pb.op("vector", (lambda e: e.scalar_tensor_tensor(
                            out=rt[0:1, 1], in0=rt[0:1, 4], scalar=-1.0,
                            in1=rt[0:1, 0], op0=Op.mult, op1=Op.mult)),
                            reads=(f"{tag}rt",), writes=(f"{tag}rt",))
                        bcast(tag, rt[0:1, 4], 512, 6, (f"{tag}rt",))
                        pb.op("vector", (lambda e: e.tensor_copy(mbc[:], ps[6][0:P, 0:512])),
                              reads=("ps6",), writes=(f"{tag}mbc",))
                        def subm(e, pp=pp):
                            rep = mbc[:, :].unsqueeze(1).broadcast_to((P, DC, 512))
                            return e.tensor_tensor(out=sq[:], in0=xp[:, pp], in1=rep,
                                                   op=Op.subtract)
                        pb.op("vector", subm, reads=(bx, f"{tag}mbc"), writes=(f"{tag}sq",))
                        pb.op("scalar", (lambda e: e.activation(out=sq[:], in_=sq[:],
                                                                func=AF.Abs)),
                              reads=(f"{tag}sq",), writes=(f"{tag}sq",))
                        amax_row(tag, sq, f"{tag}sq", DC, 512, rt[0:1, 7], f"{tag}rt", 6, 7)
                        pb.op("vector", (lambda e: e.scalar_tensor_tensor(
                            out=rt[0:1, 6], in0=rt[0:1, 7], scalar=1e-30,
                            in1=rt[0:1, 0], op0=Op.max, op1=Op.mult)),
                            reads=(f"{tag}rt",), writes=(f"{tag}rt",))
                        pb.op("vector", (lambda e: e.tensor_scalar(
                            out=rt[0:1, 6], in0=rt[0:1, 6], scalar1=1e-5,
                            scalar2=None, op0=Op.max, op1=Op.bypass)),
                            reads=(f"{tag}rt",), writes=(f"{tag}rt",))
                        pb.op("vector", (lambda e: e.reciprocal(
                            out=rt[0:1, 7], in_=rt[0:1, 6])),
                            reads=(f"{tag}rt",), writes=(f"{tag}rt",))
                        pb.op("vector", (lambda e: e.tensor_scalar(
                            out=rt[0:1, 2], in0=rt[0:1, 7], scalar1=127.0, scalar2=None,
                            op0=Op.mult, op1=Op.bypass)),
                            reads=(f"{tag}rt",), writes=(f"{tag}rt",))
                        pb.op("vector", (lambda e: e.tensor_scalar(
                            out=rt[0:1, 3], in0=rt[0:1, 6], scalar1=1.0 / 127.0,
                            scalar2=None, op0=Op.mult, op1=Op.bypass)),
                            reads=(f"{tag}rt",), writes=(f"{tag}rt",))
                        pb.op("gpsimd", (lambda e, sl=sl: e.dma_start(
                            out=rowfl[0:4, sl], in_=rt[0:1, 0:4, :])),
                            reads=(f"{tag}rt",), writes=(rowsd,), dma=True)
                    pb.barrier()

                with contextlib.ExitStack() as st2:
                    x0t = st2.enter_context(nc.sbuf_tensor(f"{tag}x0t", [P, 2, DC, 258], F32))
                    xoln = st2.enter_context(nc.sbuf_tensor(f"{tag}xoln", [P, DC, 258], F32))
                    q16 = st2.enter_context(nc.sbuf_tensor(f"{tag}q16", [P, DC, 258], F16))
                    xq = st2.enter_context(nc.sbuf_tensor(f"{tag}xq", [P, DC, 258], BF16))
                    hsb = st2.enter_context(nc.sbuf_tensor(f"{tag}hsb", [P, 2, 258], F32))
                    csb = st2.enter_context(nc.sbuf_tensor(f"{tag}csb", [P, HC, 256], F32))
                    h16 = st2.enter_context(nc.sbuf_tensor(f"{tag}h16", [P, HC, 256], F16))
                    amxb = st2.enter_context(nc.sbuf_tensor(f"{tag}amxb", [P, HC, 256], F32))
                    hq = st2.enter_context(nc.sbuf_tensor(f"{tag}hq", [P, HC, 256], BF16))
                    amx = st2.enter_context(nc.sbuf_tensor(f"{tag}amx", [P, 1, 258], F32))
                    sbc = st2.enter_context(nc.sbuf_tensor(f"{tag}sbc", [P, 3, 258], F32))
                    rw = st2.enter_context(nc.sbuf_tensor(f"{tag}rw", [1, 4, 258], F32))
                    rr = st2.enter_context(nc.sbuf_tensor(f"{tag}rr", [1, 2, 258], F32))

                    pb.op("gpsimd", (lambda e: e.memset(x0t[:], 0.0)),
                          writes=(f"{tag}x0t0", f"{tag}x0t1"))
                    pb.op("gpsimd", (lambda e: e.memset(rr[:], 0.0)),
                          writes=(f"{tag}rr",))
                    pb.op("gpsimd", (lambda e: e.memset(rw[:], 0.0)),
                          writes=(f"{tag}rw",))

                    for i in range(NFPH):
                        t0 = i * FPH
                        w0 = t0 - 1
                        lo, hi = max(w0, 0), min(t0 + FPH + 1, T)
                        off = lo - w0
                        pp = i % 2
                        bx = f"{tag}x0t{pp}"
                        pb.op("sync", (lambda e, lo=lo, hi=hi, off=off, pp=pp:
                                       e.dma_start(out=x0t[:, pp, :, off:off + (hi - lo)],
                                                   in_=fmv(src)[:, :, lo:hi])),
                              writes=(bx,), dma=True)
                        pb.op("sync", (lambda e, lo=lo, hi=hi, off=off: e.dma_start(
                            out=rw[0:1, :, off:off + (hi - lo)], in_=rowfl[0:4, lo:hi])),
                            reads=(rowsd,), writes=(f"{tag}rw",), dma=True)
                        if off == 1:
                            pb.op("gpsimd", (lambda e: e.memset(rw[0:1, :, 0:1], 0.0)),
                                  writes=(f"{tag}rw",))
                        if hi == T:
                            pb.op("gpsimd", (lambda e: e.memset(rw[0:1, :, 257:258], 0.0)),
                                  writes=(f"{tag}rw",))
                        def mkrr(e):
                            e.tensor_tensor(out=rr[0:1, 0, :], in0=rw[0:1, 0, :],
                                            in1=rw[0:1, 2, :], op=Op.mult)
                            return e.tensor_tensor(out=rr[0:1, 1, :], in0=rw[0:1, 1, :],
                                                   in1=rw[0:1, 2, :], op=Op.mult)
                        pb.op("vector", mkrr, reads=(f"{tag}rw",), writes=(f"{tag}rr",))
                        bcast(tag, rr[0:1, 0, 0:258], 258, 6, (f"{tag}rr",))
                        bcast(tag, rr[0:1, 1, 0:258], 258, 7, (f"{tag}rr",))
                        def cpsbc(e):
                            e.tensor_copy(sbc[:, 0, 0:258], ps[6][0:P, 0:258])
                            return e.tensor_copy(sbc[:, 1, 0:258], ps[7][0:P, 0:258])
                        pb.op("vector", cpsbc, reads=("ps6", "ps7"), writes=(f"{tag}sbc",))
                        def qa(e, pp=pp):
                            rep = sbc[:, 0, 0:258].unsqueeze(1).broadcast_to((P, DC, 258))
                            return e.tensor_tensor(out=xoln[:], in0=x0t[:, pp], in1=rep,
                                                   op=Op.mult)
                        pb.op("vector", qa, reads=(bx, f"{tag}sbc"), writes=(f"{tag}xoln",))
                        def qb(e):
                            rep = sbc[:, 1, 0:258].unsqueeze(1).broadcast_to((P, DC, 258))
                            return e.scalar_tensor_tensor(out=q16[:], in0=xoln[:], scalar=MG,
                                                          in1=rep, op0=Op.add, op1=Op.add)
                        pb.op("vector", qb, reads=(f"{tag}xoln", f"{tag}sbc"),
                              writes=(f"{tag}q16",))
                        pb.op("scalar", (lambda e: e.activation(out=xq[:], in_=q16[:],
                                                                func=AF.Copy, bias=-MG)),
                              reads=(f"{tag}q16",), writes=(f"{tag}xq",))
                        if off == 1:
                            pb.op("gpsimd", (lambda e: e.memset(xq[:, :, 0:1], 0.0)),
                                  writes=(f"{tag}xq",))
                        if hi == T:
                            pb.op("gpsimd", (lambda e: e.memset(xq[:, :, 257:258], 0.0)),
                                  writes=(f"{tag}xq",))
                        pb.op("vector", (lambda e: e.tensor_scalar(
                            out=rr[0:1, 0, :], in0=rw[0:1, 3, :],
                            scalar1=swr[0:1, 2:3], scalar2=None,
                            op0=Op.mult, op1=Op.bypass)),
                            reads=(f"{tag}rw", f"{tag}swr"), writes=(f"{tag}rr",))
                        bcast(tag, rr[0:1, 0, 0:258], 258, 6, (f"{tag}rr",))
                        pb.op("vector", (lambda e: e.tensor_copy(sbc[:, 1, 0:258],
                                                                 ps[6][0:P, 0:258])),
                              reads=("ps6",), writes=(f"{tag}sbc",))

                        for hc in range(HC):
                            bank = 4 + (hc % 2)
                            hr = hc % 2
                            def mm1(e, hc=hc, bank=bank):
                                for dc in range(DC):
                                    ins = e.matmul(ps[bank][0:P, 0:258],
                                                   w1q[:, dc, hc * P:(hc + 1) * P],
                                                   xq[:, dc], start=(dc == 0),
                                                   stop=(dc == DC - 1))
                                return ins
                            pb.op("tensor", mm1, reads=(f"{tag}w1q", f"{tag}xq"),
                                  writes=(f"ps{bank}",))
                            pb.op("vector", (lambda e, hr=hr, bank=bank: e.tensor_tensor(
                                out=hsb[:, hr], in0=ps[bank][0:P, 0:258],
                                in1=sbc[:, 1, 0:258], op=Op.mult)),
                                reads=(f"ps{bank}", f"{tag}sbc"), writes=(f"{tag}hsb{hr}",))
                            pb.op("scalar", (lambda e, hc=hc, hr=hr: e.activation(
                                out=csb[:, hc], in_=hsb[:, hr, 0:256], func=AF.Copy,
                                scale=cw[:, hc, 0:1])),
                                reads=(f"{tag}hsb{hr}", f"{tag}cw"), writes=(f"{tag}csb",))
                            pb.op("vector", (lambda e, hc=hc, hr=hr: e.scalar_tensor_tensor(
                                out=csb[:, hc], in0=hsb[:, hr, 1:257],
                                scalar=cw[:, hc, 1:2], in1=csb[:, hc],
                                op0=Op.mult, op1=Op.add)),
                                reads=(f"{tag}hsb{hr}", f"{tag}cw", f"{tag}csb"),
                                writes=(f"{tag}csb",))
                            pb.op("vector", (lambda e, hc=hc, hr=hr: e.scalar_tensor_tensor(
                                out=csb[:, hc], in0=hsb[:, hr, 2:258],
                                scalar=cw[:, hc, 2:3], in1=csb[:, hc],
                                op0=Op.mult, op1=Op.add)),
                                reads=(f"{tag}hsb{hr}", f"{tag}cw", f"{tag}csb"),
                                writes=(f"{tag}csb",))
                            pb.op("scalar", (lambda e, hc=hc: e.activation(
                                out=csb[:, hc], in_=csb[:, hc], func=AF.Silu,
                                bias=cb[:, hc:hc + 1])),
                                reads=(f"{tag}csb", f"{tag}cb"), writes=(f"{tag}csb",))
                        pb.op("scalar", (lambda e: e.activation(
                            out=amxb[:], in_=csb[:], func=AF.Abs)),
                            reads=(f"{tag}csb",), writes=(f"{tag}amxb",))
                        pb.op("vector", (lambda e: e.tensor_reduce(
                            out=amx[:, 0, 0:256], in_=amxb.transpose([0, 2, 1]),
                            axis=AX.X, op=Op.max)),
                            reads=(f"{tag}amxb",), writes=(f"{tag}amx",))
                        amax_row(tag, amx, f"{tag}amx", 1, 256, rr[0:1, 0, 0:256],
                                 f"{tag}rr", 6, 7)
                        pb.op("vector", (lambda e: e.tensor_scalar(
                            out=rr[0:1, 0, 0:256], in0=rr[0:1, 0, 0:256], scalar1=1e-5,
                            scalar2=None, op0=Op.max, op1=Op.bypass)),
                            reads=(f"{tag}rr",), writes=(f"{tag}rr",))
                        pb.op("vector", (lambda e: e.tensor_scalar(
                            out=rr[0:1, 1, 0:256], in0=rr[0:1, 0, 0:256],
                            scalar1=swr[0:1, 3:4], scalar2=half / 127.0,
                            op0=Op.mult, op1=Op.mult)),
                            reads=(f"{tag}rr", f"{tag}swr"), writes=(f"{tag}rr",))
                        pb.op("vector", (lambda e: e.reciprocal(
                            out=rr[0:1, 0, 0:256], in_=rr[0:1, 0, 0:256])),
                            reads=(f"{tag}rr",), writes=(f"{tag}rr",))
                        pb.op("vector", (lambda e: e.tensor_scalar(
                            out=rr[0:1, 0, 0:256], in0=rr[0:1, 0, 0:256], scalar1=127.0,
                            scalar2=None, op0=Op.mult, op1=Op.bypass)),
                            reads=(f"{tag}rr",), writes=(f"{tag}rr",))
                        bcast(tag, rr[0:1, 0, 0:256], 256, 6, (f"{tag}rr",))
                        bcast(tag, rr[0:1, 1, 0:256], 256, 7, (f"{tag}rr",))
                        def cph(e):
                            e.tensor_copy(sbc[:, 1, 0:256], ps[6][0:P, 0:256])
                            return e.tensor_copy(sbc[:, 2, 0:256], ps[7][0:P, 0:256])
                        pb.op("vector", cph, reads=("ps6", "ps7"), writes=(f"{tag}sbc",))
                        pb.op("vector", (lambda e: e.tensor_tensor(
                            out=amxb[:], in0=csb[:],
                            in1=sbc[:, 1, 0:256].unsqueeze(1).broadcast_to((P, HC, 256)),
                            op=Op.mult)),
                            reads=(f"{tag}csb", f"{tag}sbc"), writes=(f"{tag}amxb",))
                        pb.op("scalar", (lambda e: e.activation(
                            out=h16[:], in_=amxb[:], func=AF.Copy, bias=MG)),
                            reads=(f"{tag}amxb",), writes=(f"{tag}h16",))
                        pb.op("scalar", (lambda e: e.activation(
                            out=hq[:, :, :], in_=h16[:], func=AF.Copy, bias=-MG)),
                            reads=(f"{tag}h16",), writes=(f"{tag}hq",))
                        for oc in range(DC):
                            bank = 4 + (oc % 2)
                            def mm2(e, oc=oc, bank=bank):
                                for hc2 in range(HC):
                                    ins = e.matmul(ps[bank][0:P, 0:256],
                                                   w2q[:, hc2, oc * P:(oc + 1) * P],
                                                   hq[:, hc2], start=(hc2 == 0),
                                                   stop=(hc2 == HC - 1))
                                return ins
                            pb.op("tensor", mm2, reads=(f"{tag}w2q", f"{tag}hq"),
                                  writes=(f"ps{bank}",))
                            pb.op("vector", (lambda e, oc=oc, bank=bank: e.tensor_tensor(
                                out=xoln[:, oc, 0:256], in0=ps[bank][0:P, 0:256],
                                in1=sbc[:, 2, 0:256], op=Op.mult)),
                                reads=(f"ps{bank}", f"{tag}sbc"), writes=(f"{tag}xoln",))
                        pb.op("vector", (lambda e, pp=pp: e.tensor_tensor(
                            out=xoln[:, :, 0:256], in0=xoln[:, :, 0:256],
                            in1=x0t[:, pp, :, 1:257], op=Op.add)),
                            reads=(f"{tag}xoln", bx), writes=(f"{tag}xoln",))
                        pb.op("gpsimd", (lambda e, t0=t0: e.dma_start(
                            out=fmv(dst)[:, :, t0:t0 + FPH], in_=xoln[:, :, 0:256])),
                            reads=(f"{tag}xoln",), writes=(f"dram{id(dst)}",),
                            dma=True, force_mark=is_out)
                        if DEBUG_FFN and tag == "f1" and i == 0:
                            pb.op("gpsimd", (lambda e: e.dma_start(out=dxq[:, :, :], in_=xq[:])),
                                  reads=(f"{tag}xq",), dma=True, force_mark=True)
                            pb.op("gpsimd", (lambda e: e.dma_start(out=dcs[:, :, :], in_=csb[:])),
                                  reads=(f"{tag}csb",), dma=True, force_mark=True)
                            pb.op("gpsimd", (lambda e: e.dma_start(out=dhq[:, :, :], in_=hq[:])),
                                  reads=(f"{tag}hq",), dma=True, force_mark=True)
                            pb.op("gpsimd", (lambda e: e.dma_start(out=dhh[:, :, :], in_=hsb[:])),
                                  reads=(f"{tag}hsb0", f"{tag}hsb1", f"{tag}hsb2", f"{tag}hsb3"),
                                  dma=True, force_mark=True)
                            pb.op("gpsimd", (lambda e: e.dma_start(out=dsw[:, :], in_=swr[:])),
                                  reads=(f"{tag}swr",), dma=True, force_mark=True)
                    if DEBUG_FFN and tag == "f1":
                        pb.op("gpsimd", (lambda e: e.dma_start(out=drw[:, :], in_=rowfl[:, :])),
                              reads=(rowsd,), dma=True, force_mark=True)
                    pb.barrier()

        def m0():
            with contextlib.ExitStack() as st1:
                xp = st1.enter_context(nc.sbuf_tensor("m0xp", [P, 2, DC, 512], F32))
                sq = st1.enter_context(nc.sbuf_tensor("m0sq", [P, DC, 512], F32))
                rt = st1.enter_context(nc.sbuf_tensor("m0rt", [1, 4, 512], F32))
                bc2 = st1.enter_context(nc.sbuf_tensor("m0bc", [P, 2, 512], F32))
                lnf = st1.enter_context(nc.sbuf_tensor("m0lnf", [P, DC, 512], F32))
                lhi = st1.enter_context(nc.sbuf_tensor("m0lhi", [P, DC, 512], BF16))
                llo = st1.enter_context(nc.sbuf_tensor("m0llo", [P, DC, 512], BF16))
                rwh = st1.enter_context(nc.sbuf_tensor("m0rwh", [P, DC, E], BF16))
                rwl = st1.enter_context(nc.sbuf_tensor("m0rwl", [P, DC, E], BF16))
                rwf = st1.enter_context(nc.sbuf_tensor("m0rwf", [P, DC, E], F32))
                sm = st1.enter_context(nc.sbuf_tensor("m0sm", [P, E], F32))
                smc = st1.enter_context(nc.sbuf_tensor("m0smc", [P, 2], F32))
                pb.op("sync", (lambda e: e.dma_start(
                    out=rwf[:, :, :], in_=rwT.rearrange("(c p) e -> p c e", p=P))),
                    writes=("m0rwf",), dma=True)
                pb.op("vector", (lambda e: e.tensor_copy(rwh[:], rwf[:])),
                      reads=("m0rwf",), writes=("m0rwh",))
                pb.op("vector", (lambda e: e.tensor_tensor(
                    out=rwl[:], in0=rwf[:], in1=rwh[:], op=Op.subtract)),
                    reads=("m0rwf", "m0rwh"), writes=("m0rwl",))
                for i in range(T // 512):
                    pp = i % 2
                    bx = f"m0xp{pp}"
                    sl = slice(i * 512, (i + 1) * 512)
                    pb.op("sync", (lambda e, sl=sl, pp=pp: e.dma_start(
                        out=xp[:, pp], in_=fmv(x1)[:, :, sl])),
                        reads=("dramx1",), writes=(bx,), dma=True)
                    pb.op("scalar", (lambda e, pp=pp: e.activation(
                        out=sq[:], in_=xp[:, pp], func=AF.Square)),
                        reads=(bx,), writes=("m0sq",))
                    def mmsum(e, pp=pp):
                        for dc in range(DC):
                            ins = e.matmul(ps[6][0:1, 0:512], onescol[:],
                                           xp[:, pp, dc], start=(dc == 0),
                                           stop=(dc == DC - 1))
                        return ins
                    pb.op("tensor", mmsum, reads=(bx, "onescol"), writes=("ps6",))
                    def mmsq(e):
                        for dc in range(DC):
                            ins = e.matmul(ps[7][0:1, 0:512], onescol[:],
                                           sq[:, dc], start=(dc == 0),
                                           stop=(dc == DC - 1))
                        return ins
                    pb.op("tensor", mmsq, reads=("m0sq", "onescol"), writes=("ps7",))
                    pb.op("vector", (lambda e: e.tensor_scalar(
                        out=rt[0:1, 0], in0=ps[6][0:1, 0:512],
                        scalar1=1.0 / D_MODEL, scalar2=None, op0=Op.mult, op1=Op.bypass)),
                        reads=("ps6",), writes=("m0rt",))
                    pb.op("vector", (lambda e: e.tensor_tensor(
                        out=rt[0:1, 1], in0=rt[0:1, 0], in1=rt[0:1, 0], op=Op.mult)),
                        reads=("m0rt",), writes=("m0rt",))
                    pb.op("vector", (lambda e: e.scalar_tensor_tensor(
                        out=rt[0:1, 1], in0=ps[7][0:1, 0:512], scalar=1.0 / D_MODEL,
                        in1=rt[0:1, 1], op0=Op.mult, op1=Op.subtract)),
                        reads=("ps7", "m0rt"), writes=("m0rt",))
                    pb.op("scalar", (lambda e: e.activation(
                        out=rt[0:1, 1], in_=rt[0:1, 1], func=AF.Ln,
                        bias=epscol[0:1, 0:1])),
                        reads=("m0rt", "epscol"), writes=("m0rt",))
                    pb.op("scalar", (lambda e: e.activation(
                        out=rt[0:1, 2], in_=rt[0:1, 1], func=AF.Exp, scale=-0.5)),
                        reads=("m0rt",), writes=("m0rt",))
                    pb.op("vector", (lambda e: e.scalar_tensor_tensor(
                        out=rt[0:1, 3], in0=rt[0:1, 0], scalar=-1.0,
                        in1=rt[0:1, 2], op0=Op.mult, op1=Op.mult)),
                        reads=("m0rt",), writes=("m0rt",))
                    bcast("m0", rt[0:1, 2], 512, 6, ("m0rt",))
                    bcast("m0", rt[0:1, 3], 512, 7, ("m0rt",))
                    def cpbc(e):
                        e.tensor_copy(bc2[:, 0], ps[6][0:P, 0:512])
                        return e.tensor_copy(bc2[:, 1], ps[7][0:P, 0:512])
                    pb.op("vector", cpbc, reads=("ps6", "ps7"), writes=("m0bc",))
                    def lna(e, pp=pp):
                        rep = bc2[:, 0].unsqueeze(1).broadcast_to((P, DC, 512))
                        return e.tensor_tensor(out=lnf[:], in0=xp[:, pp], in1=rep,
                                               op=Op.mult)
                    pb.op("vector", lna, reads=(bx, "m0bc"), writes=("m0lnf",))
                    def lnb(e):
                        rep = bc2[:, 1].unsqueeze(1).broadcast_to((P, DC, 512))
                        return e.tensor_tensor(out=lnf[:], in0=lnf[:], in1=rep, op=Op.add)
                    pb.op("vector", lnb, reads=("m0lnf", "m0bc"), writes=("m0lnf",))
                    pb.op("vector", (lambda e: e.tensor_copy(lhi[:], lnf[:])),
                          reads=("m0lnf",), writes=("m0lhi",))
                    pb.op("vector", (lambda e: e.tensor_tensor(
                        out=llo[:], in0=lnf[:], in1=lhi[:], op=Op.subtract)),
                        reads=("m0lnf", "m0lhi"), writes=("m0llo",))
                    pb.op("gpsimd", (lambda e, sl=sl: e.dma_start(
                        out=fmv(ln2hi)[:, :, sl], in_=lhi[:])),
                        reads=("m0lhi",), writes=("dramlnhi",), dma=True)
                    pb.op("gpsimd", (lambda e, sl=sl: e.dma_start(
                        out=fmv(ln2lo)[:, :, sl], in_=llo[:])),
                        reads=("m0llo",), writes=("dramlnlo",), dma=True)
                    for g in range(4):
                        G = i * 4 + g
                        t0g = g * 128
                        def rmm(e, t0g=t0g):
                            for dc in range(DC):
                                e.matmul(ps[5][0:P, 0:E], lhi[:, dc, t0g:t0g + 128],
                                         rwh[:, dc], start=(dc == 0), stop=False)
                                e.matmul(ps[5][0:P, 0:E], lhi[:, dc, t0g:t0g + 128],
                                         rwl[:, dc], start=False, stop=False)
                                ins = e.matmul(ps[5][0:P, 0:E], llo[:, dc, t0g:t0g + 128],
                                               rwh[:, dc], start=False,
                                               stop=(dc == DC - 1))
                            return ins
                        pb.op("tensor", rmm, reads=("m0lhi", "m0llo", "m0rwh", "m0rwl"),
                              writes=("ps5",))
                        pb.op("vector", (lambda e: e.tensor_reduce(
                            out=smc[:, 0:1], in_=ps[5][0:P, 0:E], axis=AX.X, op=Op.max)),
                            reads=("ps5",), writes=("m0smc",))
                        pb.op("vector", (lambda e: e.tensor_scalar(
                            out=sm[:], in0=ps[5][0:P, 0:E], scalar1=smc[:, 0:1],
                            scalar2=None, op0=Op.subtract, op1=Op.bypass)),
                            reads=("ps5", "m0smc"), writes=("m0sm",))
                        pb.op("scalar", (lambda e: e.activation(
                            out=sm[:], in_=sm[:], func=AF.Exp)),
                            reads=("m0sm",), writes=("m0sm",))
                        pb.op("vector", (lambda e: e.tensor_reduce(
                            out=smc[:, 0:1], in_=sm[:], axis=AX.X, op=Op.add)),
                            reads=("m0sm",), writes=("m0smc",))
                        pb.op("vector", (lambda e: e.reciprocal(
                            out=smc[:, 1:2], in_=smc[:, 0:1])),
                            reads=("m0smc",), writes=("m0smc",))
                        pb.op("vector", (lambda e, G=G: e.tensor_scalar(
                            out=r_sb[:, G * E:(G + 1) * E], in0=sm[:],
                            scalar1=smc[:, 1:2], scalar2=None,
                            op0=Op.mult, op1=Op.bypass)),
                            reads=("m0sm", "m0smc"), writes=("r_sb",))
                pb.barrier()

        def m1(ei):
            with contextlib.ExitStack() as st1:
                lhi = st1.enter_context(nc.sbuf_tensor(f"e{ei}lhi", [P, DC, T], BF16))
                llo = st1.enter_context(nc.sbuf_tensor(f"e{ei}llo", [P, DC, T], BF16))
                wf = st1.enter_context(nc.sbuf_tensor(f"e{ei}wf", [P, 2, DC, P], F32))
                whi = st1.enter_context(nc.sbuf_tensor(f"e{ei}whi", [P, 2, DC, P], BF16))
                wlo = st1.enter_context(nc.sbuf_tensor(f"e{ei}wlo", [P, 2, DC, P], BF16))
                pb.op("gpsimd", (lambda e: e.memset(whi[:], 0.0)),
                      writes=(f"e{ei}whi0", f"e{ei}whi1"))
                pb.op("gpsimd", (lambda e: e.memset(wlo[:], 0.0)),
                      writes=(f"e{ei}wlo0", f"e{ei}wlo1"))
                zst = st1.enter_context(nc.sbuf_tensor(f"e{ei}zst", [P, 2, T], F32))
                pb.op("sync", (lambda e: e.dma_start(out=lhi[:], in_=fmv(ln2hi))),
                      reads=("dramlnhi",), writes=(f"e{ei}lhi",), dma=True)
                pb.op("sync", (lambda e: e.dma_start(out=llo[:], in_=fmv(ln2lo))),
                      reads=("dramlnlo",), writes=(f"e{ei}llo",), dma=True)
                inwv = inw[ei].rearrange("(c p) f -> p c f", p=P)
                NFC = 35
                for fc in range(NFC):
                    fw = 128 if fc < 34 else 32
                    f0 = fc * 128
                    pp = fc % 2
                    bw = f"e{ei}wf{pp}"
                    bz = f"e{ei}zst{pp}"
                    pb.op("sync", (lambda e, f0=f0, fw=fw, pp=pp: e.dma_start(
                        out=wf[:, pp, :, 0:fw], in_=inwv[:, :, f0:f0 + fw])),
                        writes=(bw,), dma=True)
                    pb.op("vector", (lambda e, pp=pp, fw=fw: e.tensor_copy(
                        whi[:, pp, :, 0:fw], wf[:, pp, :, 0:fw])),
                        reads=(bw,), writes=(f"e{ei}whi{pp}",))
                    pb.op("vector", (lambda e, pp=pp, fw=fw: e.tensor_tensor(
                        out=wlo[:, pp, :, 0:fw], in0=wf[:, pp, :, 0:fw],
                        in1=whi[:, pp, :, 0:fw], op=Op.subtract)),
                        reads=(bw, f"e{ei}whi{pp}"), writes=(f"e{ei}wlo{pp}",))
                    for tg in range(T // 512):
                        sl = slice(tg * 512, (tg + 1) * 512)
                        bank = 4 + (tg % 2)
                        def pmm(e, pp=pp, sl=sl, bank=bank):
                            for dc in range(DC):
                                e.matmul(ps[bank][0:P, 0:512], whi[:, pp, dc, :],
                                         lhi[:, dc, sl], start=(dc == 0), stop=False)
                                e.matmul(ps[bank][0:P, 0:512], whi[:, pp, dc, :],
                                         llo[:, dc, sl], start=False, stop=False)
                                ins = e.matmul(ps[bank][0:P, 0:512], wlo[:, pp, dc, :],
                                               lhi[:, dc, sl], start=False,
                                               stop=(dc == DC - 1))
                            return ins
                        pb.op("tensor", pmm,
                              reads=(f"e{ei}whi{pp}", f"e{ei}wlo{pp}",
                                     f"e{ei}lhi", f"e{ei}llo"),
                              writes=(f"ps{bank}",))
                        pb.op("vector", (lambda e, fw=fw, sl=sl, pp=pp, bank=bank:
                                         e.tensor_copy(zst[0:fw, pp, sl],
                                                       ps[bank][0:fw, 0:512])),
                              reads=(f"ps{bank}",), writes=(bz,))
                    pb.op("gpsimd", (lambda e, f0=f0, fw=fw, pp=pp: e.dma_start(
                        out=zx[f0:f0 + fw, :], in_=zst[0:fw, pp, :])),
                        reads=(bz,), writes=("dramzx",), dma=True)
                pb.barrier()

        def m2(ei):
            with contextlib.ExitStack() as st1:
                owhi = st1.enter_context(nc.sbuf_tensor(f"o{ei}owhi", [P, DINC, 1024], BF16))
                owlo = st1.enter_context(nc.sbuf_tensor(f"o{ei}owlo", [P, DINC, 1024], BF16))
                cw = st1.enter_context(nc.sbuf_tensor(f"o{ei}cw", [P, CVC, 4], F32))
                cb = st1.enter_context(nc.sbuf_tensor(f"o{ei}cb", [P, CVC], F32))
                nwcol = st1.enter_context(nc.sbuf_tensor(f"o{ei}nw", [P, DINC], F32))
                dskb = st1.enter_context(nc.sbuf_tensor(f"o{ei}dsk", [P, NH], F32))
                acol = st1.enter_context(nc.sbuf_tensor(f"o{ei}acol", [NH, 2], F32))
                onesh = st1.enter_context(nc.sbuf_tensor(f"o{ei}ones", [NH, CH], F32))
                state = st1.enter_context(nc.sbuf_tensor(f"o{ei}st", [P, D_INNER], F32))
                pb.op("sync", (lambda e: e.dma_start(
                    out=cw[:, :, :], in_=cwm[ei].rearrange("(c p) k -> p c k", p=P))),
                    writes=(f"o{ei}cw",), dma=True)
                pb.op("sync", (lambda e: e.dma_start(
                    out=cb[:, :], in_=cbm[ei].rearrange("(c p) -> p c", p=P),
                    allow_slow_non_contiguous=True)),
                    writes=(f"o{ei}cb",), dma=True)
                pb.op("sync", (lambda e: e.dma_start(
                    out=nwcol[:, :], in_=nrmw[ei].rearrange("(c p) -> p c", p=P),
                    allow_slow_non_contiguous=True)),
                    writes=(f"o{ei}nw",), dma=True)
                pb.op("sync", (lambda e: e.dma_start(
                    out=dskb[:, :], in_=dskp[ei:ei + 1, :].partition_broadcast(P))),
                    writes=(f"o{ei}dsk",), dma=True)
                pb.op("sync", (lambda e: e.dma_start(
                    out=acol[:, 0:1], in_=alog[ei].unsqueeze(-1))),
                    writes=(f"o{ei}acol",), dma=True)
                pb.op("sync", (lambda e: e.dma_start(
                    out=acol[:, 1:2], in_=dtb[ei].unsqueeze(-1))),
                    writes=(f"o{ei}acol",), dma=True)
                pb.op("gpsimd", (lambda e: e.memset(onesh[:], 1.0)), writes=(f"o{ei}ones",))
                pb.op("gpsimd", (lambda e: e.memset(state[:], 0.0)), writes=(f"o{ei}st",))
                pb.op("scalar", (lambda e: e.activation(
                    out=acol[:, 0:1], in_=acol[:, 0:1], func=AF.Exp)),
                    reads=(f"o{ei}acol",), writes=(f"o{ei}acol",))
                owv = outw[ei].rearrange("(c p) d -> p c d", p=P)
                with contextlib.ExitStack() as st2:
                    owf = st2.enter_context(nc.sbuf_tensor(f"o{ei}owf", [P, 8, 1024], F32))
                    for pc2 in range(2):
                        pb.op("sync", (lambda e, pc2=pc2: e.dma_start(
                            out=owf[:, :, :], in_=owv[:, pc2 * 8:(pc2 + 1) * 8, :])),
                            writes=(f"o{ei}owf",), dma=True)
                        def nwsc(e, pc2=pc2):
                            for j in range(8):
                                ins = e.tensor_scalar(
                                    out=owf[:, j, :], in0=owf[:, j, :],
                                    scalar1=nwcol[:, pc2 * 8 + j:pc2 * 8 + j + 1],
                                    scalar2=None, op0=Op.mult, op1=Op.bypass)
                            return ins
                        pb.op("vector", nwsc, reads=(f"o{ei}owf", f"o{ei}nw"),
                              writes=(f"o{ei}owf",))
                        pb.op("vector", (lambda e, pc2=pc2: e.tensor_copy(
                            owhi[:, pc2 * 8:(pc2 + 1) * 8, :], owf[:, :, :])),
                            reads=(f"o{ei}owf",), writes=(f"o{ei}owhi",))
                        pb.op("vector", (lambda e, pc2=pc2: e.tensor_tensor(
                            out=owlo[:, pc2 * 8:(pc2 + 1) * 8, :], in0=owf[:, :, :],
                            in1=owhi[:, pc2 * 8:(pc2 + 1) * 8, :], op=Op.subtract)),
                            reads=(f"o{ei}owf", f"o{ei}owhi"), writes=(f"o{ei}owlo",))
                    dts = st2.enter_context(nc.sbuf_tensor(f"o{ei}dts", [NH, 6, 512], F32))
                    for ph in range(T // 512):
                        sl = slice(ph * 512, (ph + 1) * 512)
                        pb.op("sync", (lambda e, sl=sl: e.dma_start(
                            out=dts[:, 0, :], in_=zx[4352:4384, sl])),
                            reads=("dramzx",), writes=(f"o{ei}dts",), dma=True)
                        pb.op("vector", (lambda e: e.tensor_scalar(
                            out=dts[:, 1, :], in0=dts[:, 0, :], scalar1=acol[:, 1:2],
                            scalar2=0.0, op0=Op.add, op1=Op.max)),
                            reads=(f"o{ei}dts", f"o{ei}acol"), writes=(f"o{ei}dts",))
                        pb.op("scalar", (lambda e: e.activation(
                            out=dts[:, 2, :], in_=dts[:, 0, :], func=AF.Abs,
                            bias=acol[:, 1:2])),
                            reads=(f"o{ei}dts", f"o{ei}acol"), writes=(f"o{ei}dts",))
                        pb.op("scalar", (lambda e: e.activation(
                            out=dts[:, 2, :], in_=dts[:, 2, :], func=AF.Exp, scale=-1.0)),
                            reads=(f"o{ei}dts",), writes=(f"o{ei}dts",))
                        pb.op("scalar", (lambda e: e.activation(
                            out=dts[:, 2, :], in_=dts[:, 2, :], func=AF.Ln,
                            bias=onescol[0:NH, 0:1])),
                            reads=(f"o{ei}dts", "onescol"), writes=(f"o{ei}dts",))
                        pb.op("vector", (lambda e: e.tensor_tensor(
                            out=dts[:, 1, :], in0=dts[:, 1, :], in1=dts[:, 2, :],
                            op=Op.add)),
                            reads=(f"o{ei}dts",), writes=(f"o{ei}dts",))
                        pb.op("vector", (lambda e: e.tensor_scalar(
                            out=dts[:, 2, :], in0=dts[:, 1, :], scalar1=acol[:, 0:1],
                            scalar2=-1.0, op0=Op.mult, op1=Op.mult)),
                            reads=(f"o{ei}dts", f"o{ei}acol"), writes=(f"o{ei}dts",))
                        def csscan(e):
                            for cc in range(4):
                                slc = slice(cc * CH, (cc + 1) * CH)
                                ins = e.tensor_tensor_scan(
                                    out=dts[:, 3, slc], data0=onesh[:],
                                    data1=dts[:, 2, slc], initial=0.0,
                                    op0=Op.mult, op1=Op.add)
                            return ins
                        pb.op("vector", csscan, reads=(f"o{ei}dts", f"o{ei}ones"),
                              writes=(f"o{ei}dts",))
                        pb.op("scalar", (lambda e: e.activation(
                            out=dts[:, 5, :], in_=dts[:, 3, :], func=AF.Exp)),
                            reads=(f"o{ei}dts",), writes=(f"o{ei}dts",))
                        def dsx(e):
                            for cc in range(4):
                                slc = slice(cc * CH, (cc + 1) * CH)
                                ins = e.activation(
                                    out=dts[:, 4, slc], in_=dts[:, 3, slc],
                                    func=AF.Exp, scale=-1.0,
                                    bias=dts[:, 3, (cc + 1) * CH - 1:(cc + 1) * CH])
                            return ins
                        pb.op("scalar", dsx, reads=(f"o{ei}dts",), writes=(f"o{ei}dts",))
                        pb.op("vector", (lambda e: e.tensor_tensor(
                            out=dts[:, 2, :], in0=dts[:, 1, :], in1=dts[:, 4, :],
                            op=Op.mult)),
                            reads=(f"o{ei}dts",), writes=(f"o{ei}dts",))
                        for ri, row in ((0, 1), (1, 2), (2, 5), (3, 3)):
                            pb.op("gpsimd", (lambda e, ri=ri, row=row, sl=sl: e.dma_start(
                                out=dtsfl[ri * NH:(ri + 1) * NH, sl],
                                in_=dts[:, row, :])),
                                reads=(f"o{ei}dts",), writes=("dramdts",), dma=True)
                        pb.op("gpsimd", (lambda e, ph=ph: e.dma_start(
                            out=decfl[ph * 4:(ph + 1) * 4, :].transpose([1, 0]),
                            in_=dts[:, 5, CH - 1::CH],
                            allow_slow_non_contiguous=True)),
                            reads=(f"o{ei}dts",), writes=("dramdec",), dma=True)
                    pb.barrier()

                xbc = st1.enter_context(nc.sbuf_tensor(f"o{ei}xbc", [P, 2, CVC, 131], F32))
                zt = st1.enter_context(nc.sbuf_tensor(f"o{ei}zt", [P, 8, CH], F32))
                csbc = st1.enter_context(nc.sbuf_tensor(f"o{ei}csbc", [P, 16, CH], F32))
                decb = st1.enter_context(nc.sbuf_tensor(f"o{ei}decb", [P, NH], F32))
                cvo = st1.enter_context(nc.sbuf_tensor(f"o{ei}cvo", [P, CVC, CH], F32))
                dtsl = st1.enter_context(nc.sbuf_tensor(f"o{ei}dtsl", [P, CH], F32))
                tsml = st1.enter_context(nc.sbuf_tensor(f"o{ei}tsml", [P, 2, 4, NH], F32))
                xdtT = st1.enter_context(nc.sbuf_tensor(f"o{ei}xdtT", [P, 2, DINC, P], F32))
                xqT = st1.enter_context(nc.sbuf_tensor(f"o{ei}xqT", [P, DINC, P], F32))
                xskT = st1.enter_context(nc.sbuf_tensor(f"o{ei}xskT", [P, DINC, P], F32))
                bmT = st1.enter_context(nc.sbuf_tensor(f"o{ei}bmT", [P, P], F32))
                szT = st1.enter_context(nc.sbuf_tensor(f"o{ei}szT", [P, D_INNER], F32))
                cbt = st1.enter_context(nc.sbuf_tensor(f"o{ei}cbt", [P, 2, P], F32))
                mh = st1.enter_context(nc.sbuf_tensor(f"o{ei}mh", [P, 2, 8, P], F32))
                ytok = st1.enter_context(nc.sbuf_tensor(f"o{ei}ytok", [P, 2, D_INNER], F32))
                yfh = st1.enter_context(nc.sbuf_tensor(f"o{ei}yfh", [P, DINC, 2, P], BF16))
                yfl = st1.enter_context(nc.sbuf_tensor(f"o{ei}yfl", [P, DINC, 2, P], BF16))
                sscl = st1.enter_context(nc.sbuf_tensor(f"o{ei}sscl", [P, 4], F32))
                xnew = st1.enter_context(nc.sbuf_tensor(f"o{ei}xnew", [P, DC, 2 * CH], F32))

                pb.op("gpsimd", (lambda e: e.memset(xbc[:, :, :, 0:3], 0.0)),
                      writes=(f"o{ei}xbc0", f"o{ei}xbc1"))
                zxv = zx.rearrange("(c p) t -> p c t", p=P)

                def m2_front(c):
                    pp = c % 2
                    t0 = c * CH
                    sl = slice(t0, t0 + CH)
                    bxb = f"o{ei}xbc{pp}"
                    btm, bct, bxd, byt = (f"o{ei}tsml{pp}", f"o{ei}cbt{pp}",
                                          f"o{ei}xdtT{pp}", f"o{ei}ytok{pp}")
                    if c == 0:
                        pb.op("sync", (lambda e, pp=pp: e.dma_start(
                            out=xbc[:, pp, :, 3:131], in_=zxv[:, 16:34, 0:CH])),
                            reads=("dramzx",), writes=(bxb,), dma=True)
                    else:
                        pb.op("sync", (lambda e, pp=pp, t0=t0: e.dma_start(
                            out=xbc[:, pp, :, 0:131], in_=zxv[:, 16:34, t0 - 3:t0 + CH])),
                            reads=("dramzx",), writes=(bxb,), dma=True)
                    pb.op("sync", (lambda e, sl=sl: e.dma_start(
                        out=dtsl[:, :], in_=dtsfl[0:P, sl])),
                        reads=("dramdts",), writes=(f"o{ei}dtsl",), dma=True)
                    pb.op("sync", (lambda e, c=c: e.dma_start(
                        out=decb[:, :],
                        in_=decfl[c, :].partition_broadcast(P))),
                        reads=("dramdec",), writes=(f"o{ei}decb",), dma=True)

                    def tap0(e, pp=pp):
                        for cc in range(CVC):
                            ins = e.activation(out=cvo[:, cc], in_=xbc[:, pp, cc, 0:CH],
                                               func=AF.Copy, scale=cw[:, cc, 0:1])
                        return ins
                    pb.op("scalar", tap0, reads=(bxb, f"o{ei}cw"), writes=(f"o{ei}cvo",))
                    for k in (1, 2, 3):
                        def tapk(e, k=k, pp=pp):
                            for cc in range(CVC):
                                ins = e.scalar_tensor_tensor(
                                    out=cvo[:, cc], in0=xbc[:, pp, cc, k:k + CH],
                                    scalar=cw[:, cc, k:k + 1], in1=cvo[:, cc],
                                    op0=Op.mult, op1=Op.add)
                            return ins
                        pb.op("vector", tapk, reads=(bxb, f"o{ei}cw", f"o{ei}cvo"),
                              writes=(f"o{ei}cvo",))
                    def addcb(e):
                        rep = cb[:, :].unsqueeze(-1).broadcast_to((P, CVC, CH))
                        return e.tensor_tensor(out=cvo[:], in0=cvo[:], in1=rep, op=Op.add)
                    pb.op("gpsimd", addcb, reads=(f"o{ei}cvo", f"o{ei}cb"),
                          writes=(f"o{ei}cvo",))
                    pb.op("scalar", (lambda e: e.activation(
                        out=cvo[:], in_=cvo[:], func=AF.Silu)),
                        reads=(f"o{ei}cvo",), writes=(f"o{ei}cvo",))

                    pb.op("tensor", (lambda e: e.transpose(
                        ps[2][0:P, 0:128], dtsl[:, :], ident[:])),
                        reads=(f"o{ei}dtsl", "ident"), writes=("ps2",))
                    pb.op("scalar", (lambda e, pp=pp: e.activation(
                        out=tsml[:, pp, :, :], in_=ps[2][0:P, 0:128].rearrange(
                            "p (a b) -> p a b", a=4), func=AF.Copy, bias=0.0)),
                        reads=("ps2",), writes=(btm,))
                    pb.op("tensor", (lambda e: e.transpose(
                        ps[2][0:P, 128:256], cvo[:, 16, :], ident[:])),
                        reads=(f"o{ei}cvo", "ident"), writes=("ps2",))
                    pb.op("scalar", (lambda e: e.activation(
                        out=bmT[:, :], in_=ps[2][0:P, 128:256],
                        func=AF.Copy, bias=0.0)),
                        reads=("ps2",), writes=(f"o{ei}bmT",))
                    pb.op("tensor", (lambda e: e.matmul(
                        ps[2][0:P, 256:384], cvo[:, 16, :], cvo[:, 17, :],
                        start=True, stop=True)),
                        reads=(f"o{ei}cvo",), writes=("ps2",))
                    pb.op("scalar", (lambda e, pp=pp: e.activation(
                        out=cbt[:, pp, :], in_=ps[2][0:P, 256:384],
                        func=AF.Copy, bias=0.0)),
                        reads=("ps2",), writes=(bct,))
                    for r4 in range(4):
                        bank = 6 + (r4 % 2)
                        def xtr(e, r4=r4, bank=bank):
                            for j in range(4):
                                cc = r4 * 4 + j
                                ins = e.transpose(ps[bank][0:P, j * 128:(j + 1) * 128],
                                                  cvo[:, cc, :], ident[:])
                            return ins
                        pb.op("tensor", xtr, reads=(f"o{ei}cvo", "ident"),
                              writes=(f"ps{bank}",))
                        def xev(e, r4=r4, pp=pp, bank=bank):
                            pss = ps[bank][0:P, 0:512].rearrange("p (a b) -> p a b", a=8)
                            rep = tsml[:, pp, 0, 8 * r4:8 * r4 + 8].unsqueeze(-1) \
                                .broadcast_to((P, 8, 64))
                            e.tensor_tensor(out=xdtT[:, pp, 4 * r4:4 * r4 + 4].rearrange(
                                "p a (c b) -> p (a c) b", b=64), in0=pss, in1=rep,
                                op=Op.mult)
                            rep2 = tsml[:, pp, 1, 8 * r4:8 * r4 + 8].unsqueeze(-1) \
                                .broadcast_to((P, 8, 64))
                            e.tensor_tensor(out=xqT[:, 4 * r4:4 * r4 + 4].rearrange(
                                "p a (c b) -> p (a c) b", b=64), in0=pss, in1=rep2,
                                op=Op.mult)
                            rep3 = dskb[:, 8 * r4:8 * r4 + 8].unsqueeze(-1) \
                                .broadcast_to((P, 8, 64))
                            return e.tensor_tensor(out=xskT[:, 4 * r4:4 * r4 + 4].rearrange(
                                "p a (c b) -> p (a c) b", b=64), in0=pss, in1=rep3,
                                op=Op.mult)
                        pb.op("vector", xev, reads=(f"ps{bank}", btm, f"o{ei}dsk"),
                              writes=(bxd, f"o{ei}xqT", f"o{ei}xskT"))

                    for g in range(4):
                        pb.op("tensor", (lambda e, g=g: e.matmul(
                            ps[g][0:P, 0:512], cvo[:, 17, :],
                            state[:, g * 512:(g + 1) * 512], start=True, stop=True)),
                            reads=(f"o{ei}cvo", f"o{ei}st"), writes=(f"ps{g}",))
                        def yev(e, g=g, pp=pp):
                            rep = tsml[:, pp, 2, 8 * g:8 * g + 8].unsqueeze(-1) \
                                .broadcast_to((P, 8, 64))
                            return e.tensor_tensor(
                                out=ytok[:, pp, g * 512:(g + 1) * 512].rearrange(
                                    "p (a b) -> p a b", a=8),
                                in0=ps[g][0:P, 0:512].rearrange("p (a b) -> p a b", a=8),
                                in1=rep, op=Op.mult)
                        pb.op("vector", yev, reads=(f"ps{g}", btm), writes=(byt,))
                    for g in range(4):
                        bank = 4 + (g % 2)
                        pb.op("tensor", (lambda e, g=g, bank=bank: e.matmul(
                            ps[bank][0:P, 0:512], bmT[:, :],
                            xqT[:, 4 * g:4 * g + 4, :].rearrange("p a b -> p (a b)"),
                            start=True, stop=True)),
                            reads=(f"o{ei}bmT", f"o{ei}xqT"), writes=(f"ps{bank}",))
                        def srec1(e, g=g):
                            rep = decb[:, 8 * g:8 * g + 8].unsqueeze(-1) \
                                .broadcast_to((P, 8, 64))
                            return e.tensor_tensor(
                                out=state[:, g * 512:(g + 1) * 512]
                                .rearrange("p (a b) -> p a b", a=8),
                                in0=state[:, g * 512:(g + 1) * 512]
                                .rearrange("p (a b) -> p a b", a=8),
                                in1=rep, op=Op.mult)
                        pb.op("vector", srec1, reads=(f"o{ei}decb", f"o{ei}st"),
                              writes=(f"o{ei}st",))
                        pb.op("vector", (lambda e, g=g, bank=bank: e.tensor_tensor(
                            out=state[:, g * 512:(g + 1) * 512],
                            in0=state[:, g * 512:(g + 1) * 512],
                            in1=ps[bank][0:P, 0:512], op=Op.add)),
                            reads=(f"ps{bank}", f"o{ei}st"), writes=(f"o{ei}st",))
                    pb.op("vector", (lambda e, pp=pp: e.tensor_tensor(
                        out=ytok[:, pp], in0=ytok[:, pp],
                        in1=xskT[:, :, :].rearrange("p a b -> p (a b)"), op=Op.add)),
                        reads=(f"o{ei}xskT", byt), writes=(byt,))

                def m2_back(c):
                    pp = c % 2
                    t0 = c * CH
                    sl = slice(t0, t0 + CH)
                    btm, bct, bxd, byt = (f"o{ei}tsml{pp}", f"o{ei}cbt{pp}",
                                          f"o{ei}xdtT{pp}", f"o{ei}ytok{pp}")
                    for half in range(2):
                        pb.op("sync", (lambda e, sl=sl, half=half: e.dma_start(
                            out=zt[:, :, :], in_=zxv[:, 8 * half:8 * half + 8, sl])),
                            reads=("dramzx",), writes=(f"o{ei}zt",), dma=True)
                        pb.op("scalar", (lambda e: e.activation(
                            out=zt[:], in_=zt[:], func=AF.Silu)),
                            reads=(f"o{ei}zt",), writes=(f"o{ei}zt",))
                        for r2 in range(2):
                            r4 = half * 2 + r2
                            bank = 6 + (r4 % 2)
                            def ztr(e, r2=r2, bank=bank):
                                for j in range(4):
                                    cc = r2 * 4 + j
                                    ins = e.transpose(
                                        ps[bank][0:P, j * 128:(j + 1) * 128],
                                        zt[:, cc, :], ident[:])
                                return ins
                            pb.op("tensor", ztr, reads=(f"o{ei}zt", "ident"),
                                  writes=(f"ps{bank}",))
                            pb.op("scalar", (lambda e, r4=r4, bank=bank: e.activation(
                                out=szT[:, r4 * 512:(r4 + 1) * 512],
                                in_=ps[bank][0:P, 0:512],
                                func=AF.Copy, bias=0.0)),
                                reads=(f"ps{bank}",), writes=(f"o{ei}szT",))
                    for half in range(2):
                        pb.op("sync", (lambda e, sl=sl, half=half: e.dma_start(
                            out=csbc[:, :, :],
                            in_=dtsfl[96 + 16 * half:112 + 16 * half, sl]
                            .partition_broadcast(P))),
                            reads=("dramdts",), writes=(f"o{ei}csbc",), dma=True)
                        pb.op("vector", (lambda e: e.tensor_tensor(
                            out=csbc[:], in0=csbc[:],
                            in1=masktile[:, :].unsqueeze(1).broadcast_to((P, 16, CH)),
                            op=Op.add)),
                            reads=(f"o{ei}csbc", "masktile"), writes=(f"o{ei}csbc",))
                        for g2 in range(2):
                            g = half * 2 + g2
                            gp = g % 2
                            bmh = f"o{ei}mh{gp}"
                            def msub(e, g=g, g2=g2, gp=gp, pp=pp):
                                rep = tsml[:, pp, 3, 8 * g:8 * g + 8].unsqueeze(-1) \
                                    .broadcast_to((P, 8, CH))
                                return e.tensor_tensor(
                                    out=mh[:, gp], in0=csbc[:, 8 * g2:8 * g2 + 8, :],
                                    in1=rep, op=Op.subtract)
                            pb.op("vector", msub, reads=(f"o{ei}csbc", btm),
                                  writes=(bmh,))
                            pb.op("scalar", (lambda e, gp=gp: e.activation(
                                out=mh[:, gp], in_=mh[:, gp], func=AF.Exp)),
                                reads=(bmh,), writes=(bmh,))
                            def mmul(e, gp=gp, pp=pp):
                                rep = cbt[:, pp, :].unsqueeze(1).broadcast_to((P, 8, CH))
                                return e.tensor_tensor(out=mh[:, gp], in0=mh[:, gp],
                                                       in1=rep, op=Op.mult)
                            pb.op("vector", mmul, reads=(bmh, bct), writes=(bmh,))
                            bank = 4 + (g % 2)
                            def ydmm(e, g=g, gp=gp, pp=pp, bank=bank):
                                for hcol in range(8):
                                    h = g * 8 + hcol
                                    ins = e.matmul(
                                        ps[bank][0:P, hcol * 64:(hcol + 1) * 64],
                                        mh[:, gp, hcol, :],
                                        xdtT[:, pp, h // 2,
                                             (h % 2) * 64:(h % 2) * 64 + 64],
                                        start=True, stop=True)
                                return ins
                            pb.op("tensor", ydmm, reads=(bmh, bxd),
                                  writes=(f"ps{bank}",))
                            pb.op("vector", (lambda e, g=g, pp=pp, bank=bank:
                                             e.tensor_tensor(
                                                 out=ytok[:, pp, g * 512:(g + 1) * 512],
                                                 in0=ytok[:, pp, g * 512:(g + 1) * 512],
                                                 in1=ps[bank][0:P, 0:512], op=Op.add)),
                                  reads=(f"ps{bank}", byt), writes=(byt,))
                    pb.op("vector", (lambda e, pp=pp: e.tensor_tensor(
                        out=ytok[:, pp], in0=ytok[:, pp], in1=szT[:], op=Op.mult)),
                        reads=(f"o{ei}szT", byt), writes=(byt,))
                    pb.op("scalar", (lambda e, pp=pp: e.activation(
                        out=xskT[:, :, :].rearrange("p a b -> p (a b)"), in_=ytok[:, pp],
                        func=AF.Square)),
                        reads=(byt,), writes=(f"o{ei}xskT",))
                    pb.op("vector", (lambda e: e.tensor_reduce(
                        out=sscl[:, 0:1], in_=xskT[:, :, :].rearrange("p a b -> p (a b)"),
                        axis=AX.X, op=Op.add)),
                        reads=(f"o{ei}xskT",), writes=(f"o{ei}sscl",))
                    pb.op("scalar", (lambda e: e.activation(
                        out=sscl[:, 1:2], in_=sscl[:, 0:1], func=AF.Ln,
                        scale=1.0 / D_INNER, bias=epscol[0:P, 0:1])),
                        reads=(f"o{ei}sscl", "epscol"), writes=(f"o{ei}sscl",))
                    pb.op("scalar", (lambda e: e.activation(
                        out=sscl[:, 2:3], in_=sscl[:, 1:2], func=AF.Exp, scale=-0.5)),
                        reads=(f"o{ei}sscl",), writes=(f"o{ei}sscl",))
                    pb.op("vector", (lambda e, c=c: e.tensor_tensor(
                        out=sscl[:, 3:4], in0=sscl[:, 2:3],
                        in1=r_sb[:, c * E + ei:c * E + ei + 1], op=Op.mult)),
                        reads=(f"o{ei}sscl", "r_sb"), writes=(f"o{ei}sscl",))
                    pb.op("vector", (lambda e, pp=pp: e.tensor_scalar(
                        out=ytok[:, pp], in0=ytok[:, pp], scalar1=sscl[:, 3:4],
                        scalar2=None, op0=Op.mult, op1=Op.bypass)),
                        reads=(byt, f"o{ei}sscl"), writes=(byt,))
                    for r4 in range(4):
                        bank = 6 + (r4 % 2)
                        def ytr(e, r4=r4, bank=bank, pp=pp):
                            for j in range(4):
                                cc = r4 * 4 + j
                                ins = e.transpose(ps[bank][0:P, j * 128:(j + 1) * 128],
                                                  ytok[:, pp, cc * 128:(cc + 1) * 128],
                                                  ident[:])
                            return ins
                        pb.op("tensor", ytr, reads=(byt, "ident"), writes=(f"ps{bank}",))
                        pb.op("scalar", (lambda e, r4=r4, bank=bank, pp=pp: e.activation(
                            out=yfh[:, 4 * r4:4 * r4 + 4, pp, :],
                            in_=ps[bank][0:P, 0:512].rearrange("p (a b) -> p a b", a=4),
                            func=AF.Copy, bias=0.0)),
                            reads=(f"ps{bank}",), writes=(f"o{ei}yfh{pp}",))
                        pb.op("vector", (lambda e, r4=r4, bank=bank, pp=pp: e.tensor_tensor(
                            out=yfl[:, 4 * r4:4 * r4 + 4, pp, :],
                            in0=ps[bank][0:P, 0:512].rearrange("p (a b) -> p a b", a=4),
                            in1=yfh[:, 4 * r4:4 * r4 + 4, pp, :], op=Op.subtract)),
                            reads=(f"ps{bank}", f"o{ei}yfh{pp}"),
                            writes=(f"o{ei}yfl{pp}",))
                    if pp == 1:
                        xsrc = x1 if ei == 0 else x2
                        for hf in range(2):
                            pb.op("sync", (lambda e, t0=t0, hf=hf, xs=xsrc: e.dma_start(
                                out=xnew[:, :, hf * CH:(hf + 1) * CH],
                                in_=fmv(xs)[:, :, t0 - CH + hf * CH:t0 + hf * CH])),
                                reads=("dramx1" if ei == 0 else "dramx2",),
                                writes=(f"o{ei}xnew",), dma=True)
                        for oc in range(DC):
                            bank = 4 + (oc % 4) // 2
                            slot = oc % 2
                            def omm(e, oc=oc, bank=bank, slot=slot):
                                out = ps[bank][0:P, slot * 256:slot * 256 + 256]
                                mvh = yfh.rearrange("p a b c -> p a (b c)")
                                mvl = yfl.rearrange("p a b c -> p a (b c)")
                                for dic in range(DINC):
                                    e.matmul(out, owhi[:, dic, oc * 128:(oc + 1) * 128],
                                             mvh[:, dic, :], start=(dic == 0), stop=False)
                                    e.matmul(out, owhi[:, dic, oc * 128:(oc + 1) * 128],
                                             mvl[:, dic, :], start=False, stop=False)
                                    ins = e.matmul(
                                        out, owlo[:, dic, oc * 128:(oc + 1) * 128],
                                        mvh[:, dic, :], start=False,
                                        stop=(dic == DINC - 1))
                                return ins
                            pb.op("tensor", omm,
                                  reads=(f"o{ei}owhi", f"o{ei}owlo", f"o{ei}yfh0",
                                         f"o{ei}yfh1", f"o{ei}yfl0", f"o{ei}yfl1"),
                                  writes=(f"ps{bank}",))
                            pb.op("vector", (lambda e, oc=oc, bank=bank, slot=slot:
                                             e.tensor_tensor(
                                                 out=xnew[:, oc, :],
                                                 in0=xnew[:, oc, :],
                                                 in1=ps[bank][0:P, slot * 256:
                                                              slot * 256 + 256],
                                                 op=Op.add)),
                                  reads=(f"ps{bank}", f"o{ei}xnew"),
                                  writes=(f"o{ei}xnew",))
                        for hf in range(2):
                            pb.op("gpsimd", (lambda e, t0=t0, hf=hf: e.dma_start(
                                out=fmv(x2)[:, :, t0 - CH + hf * CH:t0 + hf * CH],
                                in_=xnew[:, :, hf * CH:(hf + 1) * CH])),
                                reads=(f"o{ei}xnew",), writes=("dramx2",), dma=True)

                m2_front(0)
                for c in range(NCH):
                    if c + 1 < NCH:
                        m2_front(c + 1)
                    m2_back(c)
                pb.barrier()

        if "F1" in STAGES:
            ffn_call("f1", xT, x1, 0.5, False)
        else:
            pb.op("sync", (lambda e: e.dma_start(out=x1[:, :], in_=xT[:, :])), dma=True)
            pb.barrier()

        src_final = x1
        if "M0" in STAGES:
            m0()
        if "M1" in STAGES:
            for ei in range(E):
                m1(ei)
                if "M2" in STAGES:
                    m2(ei)
            src_final = x2

        if "F2" in STAGES:
            ffn_call("f2", src_final, out, 0.5, True)
        else:
            pb.op("sync", (lambda e, s=src_final: e.dma_start(out=out[:, :], in_=s[:, :])),
                  dma=True, force_mark=True)
            pb.barrier()

        pb.resolve()
        semctx = {s: stack.enter_context(nc.semaphore(s)) for s in SEMS}
        with nc.Block() as block:
            @block.sync
            def _(e):
                pb.emit("sync", e, semctx)

            @block.tensor
            def _(e):
                pb.emit("tensor", e, semctx)

            @block.vector
            def _(e):
                pb.emit("vector", e, semctx)

            @block.scalar
            def _(e):
                pb.emit("scalar", e, semctx)

            @block.gpsimd
            def _(e):
                pb.emit("gpsimd", e, semctx)
                for s in SEMS:
                    if s.startswith("s_l") or s.startswith("s_g"):
                        if pb.final.get(s, 0) > 0:
                            e.wait_ge(semctx[s], pb.final[s])
    return nc


def kernel(L0_attn_x, L0_attn_router_w, L0_attn_in_w, L0_attn_conv_w,
           L0_attn_conv_b, L0_attn_dt_bias, L0_attn_A_log, L0_attn_D_skip,
           L0_attn_norm_w, L0_attn_out_w, L0_ffn_w1, L0_ffn_conv_w,
           L0_ffn_conv_b, L0_ffn_w2):
    x = np.asarray(L0_attn_x)
    shared = {
        "w1T": np.ascontiguousarray(np.asarray(L0_ffn_w1).T),
        "w2T": np.ascontiguousarray(np.asarray(L0_ffn_w2).T),
        "cwf": np.ascontiguousarray(np.asarray(L0_ffn_conv_w)[:, 0, :]),
        "cbf": np.ascontiguousarray(np.asarray(L0_ffn_conv_b)),
        "rwT": np.ascontiguousarray(np.asarray(L0_attn_router_w)),
        "inw": np.ascontiguousarray(np.asarray(L0_attn_in_w)),
        "cwm": np.ascontiguousarray(np.asarray(L0_attn_conv_w)),
        "cbm": np.ascontiguousarray(np.asarray(L0_attn_conv_b)),
        "dtb": np.ascontiguousarray(np.asarray(L0_attn_dt_bias)),
        "alog": np.ascontiguousarray(np.asarray(L0_attn_A_log)),
        "dskp": np.ascontiguousarray(np.asarray(L0_attn_D_skip)),
        "nrmw": np.ascontiguousarray(np.asarray(L0_attn_norm_w)),
        "outw": np.ascontiguousarray(np.asarray(L0_attn_out_w)),
    }
    nc = build_nc()
    in_maps = []
    for b in range(8):
        m = dict(shared)
        m["xT"] = np.ascontiguousarray(x[b].T)
        in_maps.append(m)
    res = run_bass_kernel_spmd(nc, in_maps, core_ids=list(range(8)))
    outs = [np.ascontiguousarray(r["out"].T) for r in res.results]
    return np.stack(outs, axis=0)


# version 17
# speedup vs baseline: 1.2029x; 1.2029x over previous
import contextlib
import numpy as np
import ml_dtypes
import concourse.bass as bass
import concourse.mybir as mybir
from concourse.bass_utils import run_bass_kernel_spmd
from concourse.alu_op_type import AluOpType as Op

F32 = mybir.dt.float32
BF16 = mybir.dt.bfloat16
F16 = mybir.dt.float16
F8 = mybir.dt.float8e4
AF = mybir.ActivationFunctionType
AX = mybir.AxisListType

P = 128
T = 4096
D_MODEL = 1024
DC = 8
H_FFN = 4096
HC = 32
E = 4
D_INNER = 2048
DINC = 16
NH = 32
HD = 64
CONV_DIM = 2304
CVC = 18
D_IN_PROJ = 4384
NST = 128
CH = 128
NCH = T // CH
FPH = 256
NFPH = T // FPH
MG = 1536.0
NEG = -1e30

ENGINES = ("sync", "tensor", "vector", "scalar", "gpsimd")
SEM_OF = {"tensor": "s_te", "vector": "s_ve", "scalar": "s_ac", "gpsimd": "s_gp"}
DMA_POOL = {"sync": ("s_ld0", "s_ld1", "s_ld2", "s_ld3"),
            "gpsimd": ("s_gd0", "s_gd1")}
SEMS = ("s_ld0", "s_ld1", "s_ld2", "s_ld3", "s_gd0", "s_gd1",
        "s_te", "s_ve", "s_ac", "s_gp")


class Mark:
    __slots__ = ("sem", "seq", "val", "used", "step", "vc")

    def __init__(self, sem, seq, step, vc):
        self.sem, self.seq, self.step, self.vc = sem, seq, step, vc
        self.val = None
        self.used = False


class _NoMark:
    used = False
    val = None


_NOMARK = _NoMark()


class PB:

    def __init__(self):
        self.prog = []
        self.seq = {s: 0 for s in SEMS}
        self.vc = {e: {} for e in ENGINES}
        self.bufs = {}
        self._marks = {s: [] for s in SEMS}
        self.dmacnt = {q: 0 for q in DMA_POOL}

    def _buf(self, name):
        b = self.bufs.get(name)
        if b is None:
            b = {"w": None, "r": []}
            self.bufs[name] = b
        return b

    def op(self, eng, fn, reads=(), writes=(), dma=False, force_mark=False):
        need = {}
        if dma:
            qsem = DMA_POOL[eng][self.dmacnt[eng] % len(DMA_POOL[eng])]
            self.dmacnt[eng] += 1
            if self.seq[qsem] > 0:
                need[qsem] = self.seq[qsem] - 1
        for r in reads:
            b = self._buf(r)
            if b["w"] is not None:
                m = b["w"]
                if need.get(m.sem, -1) < m.seq:
                    need[m.sem] = m.seq
        for w in writes:
            b = self._buf(w)
            if b["w"] is not None:
                m = b["w"]
                if need.get(m.sem, -1) < m.seq:
                    need[m.sem] = m.seq
            for m in b["r"]:
                if need.get(m.sem, -1) < m.seq:
                    need[m.sem] = m.seq
        waits = []
        vc = self.vc[eng]
        for sem, seqv in need.items():
            if vc.get(sem, -1) >= seqv:
                continue
            m = self._marks[sem][seqv]
            m.used = True
            waits.append(m)
            for s2, q2 in m.vc.items():
                if vc.get(s2, -1) < q2:
                    vc[s2] = q2
            vc[sem] = max(vc.get(sem, -1), seqv)
        sem = qsem if dma else SEM_OF[eng]
        myvc = dict(vc)
        mark = Mark(sem, self.seq[sem], 16 if dma else 1, myvc)
        myvc[sem] = max(myvc.get(sem, -1), mark.seq)
        if force_mark or dma:
            mark.used = True
        self._marks[sem].append(mark)
        self.seq[sem] += 1
        for w in writes:
            b = self._buf(w)
            b["w"] = mark
            b["r"] = []
        for r in reads:
            self._buf(r)["r"].append(mark)
        self.prog.append((eng, fn, mark, waits))
        return mark

    def barrier(self):
        for eng in ENGINES:
            waits = []
            vc = self.vc[eng]
            for sem in SEMS:
                seqv = self.seq[sem] - 1
                if seqv < 0 or vc.get(sem, -1) >= seqv:
                    continue
                m = self._marks[sem][seqv]
                m.used = True
                waits.append(m)
                for s2, q2 in m.vc.items():
                    if vc.get(s2, -1) < q2:
                        vc[s2] = q2
                vc[sem] = seqv
            if waits:
                self.prog.append((eng, None, _NOMARK, waits))

    def resolve(self):
        self.final = {}
        for sem, marks in self._marks.items():
            c = 0
            for m in marks:
                if m.used:
                    c += m.step
                    m.val = c
            self.final[sem] = c

    def emit(self, eng_name, e, sems):
        for eng, fn, mark, waits in self.prog:
            if eng != eng_name:
                continue
            for m in waits:
                e.wait_ge(sems[m.sem], m.val)
            if fn is None:
                continue
            ins = fn(e)
            if mark.used:
                assert ins is not None, f"op on {eng} needs a sync-bearing instr"
                ins.then_inc(sems[mark.sem], mark.step)


STAGES = ("F1", "M0", "M1", "M2", "F2")
DEBUG_FFN = False


def set_T(t):
    global T, NCH, NFPH
    T = t
    NCH = T // CH
    NFPH = T // FPH


def build_nc():
    nc = bass.Bass("TRN2", target_bir_lowering=False)
    pb = PB()

    xT = nc.dram_tensor("xT", [D_MODEL, T], F32, kind="ExternalInput").ap()
    w1T = nc.dram_tensor("w1T", [D_MODEL, H_FFN], F32, kind="ExternalInput").ap()
    w2T = nc.dram_tensor("w2T", [H_FFN, D_MODEL], F32, kind="ExternalInput").ap()
    cwf = nc.dram_tensor("cwf", [H_FFN, 3], F32, kind="ExternalInput").ap()
    cbf = nc.dram_tensor("cbf", [H_FFN], F32, kind="ExternalInput").ap()
    rwT = nc.dram_tensor("rwT", [D_MODEL, E], F32, kind="ExternalInput").ap()
    inw = nc.dram_tensor("inw", [E, D_MODEL, D_IN_PROJ], F32, kind="ExternalInput").ap()
    cwm = nc.dram_tensor("cwm", [E, CONV_DIM, 4], F32, kind="ExternalInput").ap()
    cbm = nc.dram_tensor("cbm", [E, CONV_DIM], F32, kind="ExternalInput").ap()
    dtb = nc.dram_tensor("dtb", [E, NH], F32, kind="ExternalInput").ap()
    alog = nc.dram_tensor("alog", [E, NH], F32, kind="ExternalInput").ap()
    dskp = nc.dram_tensor("dskp", [E, NH], F32, kind="ExternalInput").ap()
    nrmw = nc.dram_tensor("nrmw", [E, D_INNER], F32, kind="ExternalInput").ap()
    outw = nc.dram_tensor("outw", [E, D_INNER, D_MODEL], F32, kind="ExternalInput").ap()
    out = nc.dram_tensor("out", [D_MODEL, T], F32, kind="ExternalOutput").ap()
    if DEBUG_FFN:
        dxq = nc.dram_tensor("dxq", [P, DC, 258], F32, kind="ExternalOutput").ap()
        dcs = nc.dram_tensor("dcs", [P, HC, 256], F32, kind="ExternalOutput").ap()
        dhq = nc.dram_tensor("dhq", [P, HC, 256], F32, kind="ExternalOutput").ap()
        dhh = nc.dram_tensor("dhh", [P, 4, 258], F32, kind="ExternalOutput").ap()
        drw = nc.dram_tensor("drw", [8, T], F32, kind="ExternalOutput").ap()
        dsw = nc.dram_tensor("dsw", [1, 4], F32, kind="ExternalOutput").ap()

    x1 = nc.dram_tensor("x1", [D_MODEL, T], F32, kind="Internal").ap()
    x2 = nc.dram_tensor("x2", [D_MODEL, T], F32, kind="Internal").ap()
    ln2hi = nc.dram_tensor("ln2hi", [D_MODEL, T], BF16, kind="Internal").ap()
    ln2lo = nc.dram_tensor("ln2lo", [D_MODEL, T], BF16, kind="Internal").ap()
    zx = nc.dram_tensor("zx", [35 * P, T], F32, kind="Internal").ap()
    dtsfl = nc.dram_tensor("dtsfl", [P, T], F32, kind="Internal").ap()
    decfl = nc.dram_tensor("decfl", [NCH, NH], F32, kind="Internal").ap()
    rowfl = nc.dram_tensor("rowfl", [8, T], F32, kind="Internal").ap()

    def fmv(ap):
        return ap.rearrange("(c p) t -> p c t", p=P)

    stack = contextlib.ExitStack()
    with stack:
        ident = stack.enter_context(nc.sbuf_tensor("ident", [P, P], F32))
        masktile = stack.enter_context(nc.sbuf_tensor("masktile", [P, P], F32))
        onescol = stack.enter_context(nc.sbuf_tensor("onescol", [P, 1], F32))
        onesrow = stack.enter_context(nc.sbuf_tensor("onesrow", [1, P], F32))
        r_sb = stack.enter_context(nc.sbuf_tensor("r_sb", [P, NCH * E], F32))
        ps = [stack.enter_context(nc.psum_tensor(f"ps{i}", [P, 512], F32))
              for i in range(8)]

        pb.op("gpsimd", lambda e: e.memset(ident[:], 0.0), writes=("ident",))
        pb.op("gpsimd", (lambda e: e.affine_select(
            out=ident[:], in_=ident[:], compare_op=Op.not_equal, fill=1.0,
            base=0, pattern=[[-1, P]], channel_multiplier=1)),
            reads=("ident",), writes=("ident",))
        pb.op("gpsimd", lambda e: e.memset(masktile[:], 0.0), writes=("masktile",))
        pb.op("gpsimd", (lambda e: e.affine_select(
            out=masktile[:], in_=masktile[:], compare_op=Op.is_ge, fill=NEG,
            base=0, pattern=[[1, P]], channel_multiplier=-1)),
            reads=("masktile",), writes=("masktile",))
        pb.op("gpsimd", lambda e: e.memset(onescol[:], 1.0), writes=("onescol",))
        pb.op("gpsimd", lambda e: e.memset(onesrow[:], 1.0), writes=("onesrow",))
        epscol = stack.enter_context(nc.sbuf_tensor("epscol", [P, 1], F32))
        pb.op("gpsimd", lambda e: e.memset(epscol[:], 1e-5), writes=("epscol",))

        def bcast(tag2, rowap, n, bank, reads):
            pb.op("tensor", (lambda e, r=rowap, n=n, bank=bank: e.matmul(
                ps[bank][0:P, 0:n], onesrow[:], r, start=True, stop=True)),
                reads=tuple(reads) + ("onesrow",), writes=(f"ps{bank}",))

        trow = stack.enter_context(nc.sbuf_tensor("trow", [8, P], F32))

        def amax_row(tag2, absbuf, absname, nch, n, rowslice, rowname, psA, psB):
            w = nch // 2
            while w >= 1:
                pb.op("vector", (lambda e, w=w: e.tensor_tensor(
                    out=absbuf[:, 0:w, 0:n], in0=absbuf[:, 0:w, 0:n],
                    in1=absbuf[:, w:2 * w, 0:n], op=Op.max)),
                    reads=(absname,), writes=(absname,))
                w //= 2
            ng = n // P
            def trs(e):
                for g in range(ng):
                    ins = e.transpose(ps[psA][0:P, g * P:(g + 1) * P],
                                      absbuf[:, 0, g * P:(g + 1) * P], ident[:])
                return ins
            pb.op("tensor", trs, reads=(absname, "ident"), writes=(f"ps{psA}",))
            def reds(e):
                for g in range(ng):
                    ins = e.tensor_reduce(out=absbuf[0:P, 0, g:g + 1],
                                          in_=ps[psA][0:P, g * P:(g + 1) * P],
                                          axis=AX.X, op=Op.max)
                return ins
            pb.op("vector", reds, reads=(f"ps{psA}",), writes=(absname,))
            pb.op("tensor", (lambda e: e.transpose(ps[psB][0:ng, 0:P],
                                                   absbuf[:, 0, 0:ng], ident[:])),
                  reads=(absname, "ident"), writes=(f"ps{psB}",))
            pb.op("vector", (lambda e: e.tensor_copy(trow[0:ng, :], ps[psB][0:ng, 0:P])),
                  reads=(f"ps{psB}",), writes=("trow",))
            pb.op("gpsimd", (lambda e: e.dma_start(out=rowslice, in_=trow[0:ng, :])),
                  reads=("trow",), writes=(rowname,), dma=True)

        def ffn_call(tag, src, dst, half, is_out):
            rowsd = f"{tag}rowsd"
            with contextlib.ExitStack() as st1:
                w1q = st1.enter_context(nc.sbuf_tensor(f"{tag}w1q", [P, DC, H_FFN], F8))
                w2q = st1.enter_context(nc.sbuf_tensor(f"{tag}w2q", [P, HC, D_MODEL], F8))
                cw = st1.enter_context(nc.sbuf_tensor(f"{tag}cw", [P, HC, 3], F32))
                cb = st1.enter_context(nc.sbuf_tensor(f"{tag}cb", [P, HC], F32))
                swr = st1.enter_context(nc.sbuf_tensor(f"{tag}swr", [1, 4], F32))

                with contextlib.ExitStack() as st2:
                    wbuf = st2.enter_context(
                        nc.sbuf_tensor(f"{tag}wbuf", [P, 2, 4096], F32))
                    wtmp = st2.enter_context(nc.sbuf_tensor(f"{tag}wtmp", [1 * P, 4096], F16))
                    wsc = st2.enter_context(nc.sbuf_tensor(f"{tag}wsc", [P, 4], F32))
                    for wi, (wsrc, nchunk, wid) in enumerate(
                            ((w1T, DC, H_FFN), (w2T, HC, D_MODEL))):
                        wn = f"{tag}w1q" if wi == 0 else f"{tag}w2q"
                        wdst = w1q if wi == 0 else w2q
                        view = wsrc.rearrange("(c p) w -> p c w", p=P)
                        npc = 4096 // wid
                        npieces = nchunk // npc
                        for pc in range(npieces):
                            pp = pc % 2
                            bw = f"{tag}wbuf{pp}"
                            pb.op("sync", (lambda e, pc=pc, pp=pp, v=view, npc=npc, w=wid:
                                           e.dma_start(out=wbuf[:, pp].rearrange(
                                               "p (a b) -> p a b", a=npc),
                                               in_=v[:, pc * npc:(pc + 1) * npc, :])),
                                  writes=(bw,), dma=True)
                            if pc == 0:
                                pb.op("vector", (lambda e, pp=pp: e.tensor_reduce(
                                    out=wsc[:, 0:1], in_=wbuf[:, pp], axis=AX.X,
                                    op=Op.add, apply_absolute_value=True)),
                                    reads=(bw,), writes=(f"{tag}wsc",))
                            else:
                                pb.op("vector", (lambda e, pp=pp: e.tensor_reduce(
                                    out=wsc[:, 1:2], in_=wbuf[:, pp], axis=AX.X,
                                    op=Op.add, apply_absolute_value=True)),
                                    reads=(bw,), writes=(f"{tag}wsc",))
                                pb.op("vector", (lambda e: e.tensor_tensor(
                                    out=wsc[:, 0:1], in0=wsc[:, 0:1], in1=wsc[:, 1:2],
                                    op=Op.add)), reads=(f"{tag}wsc",), writes=(f"{tag}wsc",))
                        pb.op("tensor", (lambda e: e.matmul(
                            ps[7][0:1, 0:1], wsc[:, 0:1], onescol[:], start=True, stop=True)),
                            reads=(f"{tag}wsc", "onescol"), writes=("ps7",))
                        pb.op("vector", (lambda e, wi=wi, n=nchunk, w=wid: e.tensor_scalar(
                            out=swr[0:1, wi + 2:wi + 3], in0=ps[7][0:1, 0:1],
                            scalar1=1.0 / (n * P * w), scalar2=1e-5,
                            op0=Op.mult, op1=Op.max)),
                            reads=("ps7",), writes=(f"{tag}swr",))
                        pb.op("vector", (lambda e, wi=wi: e.reciprocal(
                            out=swr[0:1, wi:wi + 1], in_=swr[0:1, wi + 2:wi + 3])),
                            reads=(f"{tag}swr",), writes=(f"{tag}swr",))
                        pb.op("tensor", (lambda e, wi=wi: e.matmul(
                            ps[7][0:P, 1:2], onesrow[:], swr[0:1, wi:wi + 1],
                            start=True, stop=True)),
                            reads=(f"{tag}swr", "onesrow"), writes=("ps7",))
                        pb.op("vector", (lambda e: e.tensor_copy(
                            wsc[:, 3:4], ps[7][0:P, 1:2])),
                            reads=("ps7",), writes=(f"{tag}wsc",))
                        for pc in range(npieces):
                            pp = pc % 2
                            bw = f"{tag}wbuf{pp}"
                            pb.op("sync", (lambda e, pc=pc, pp=pp, v=view, npc=npc, w=wid:
                                           e.dma_start(out=wbuf[:, pp].rearrange(
                                               "p (a b) -> p a b", a=npc),
                                               in_=v[:, pc * npc:(pc + 1) * npc, :])),
                                  writes=(bw,), dma=True)
                            pb.op("vector", (lambda e, pp=pp: e.tensor_scalar(
                                out=wbuf[:, pp], in0=wbuf[:, pp], scalar1=wsc[:, 3:4],
                                scalar2=1.4999, op0=Op.mult, op1=Op.min)),
                                reads=(bw, f"{tag}wsc"), writes=(bw,))
                            pb.op("vector", (lambda e, pp=pp: e.tensor_scalar(
                                out=wtmp[:, :], in0=wbuf[:, pp], scalar1=-1.4999,
                                scalar2=MG, op0=Op.max, op1=Op.add)),
                                reads=(bw,), writes=(f"{tag}wtmp",))
                            pb.op("scalar", (lambda e, wd=wdst, pc=pc, npc=npc, w=wid:
                                             e.activation(out=wd[:, pc * npc:(pc + 1) * npc, :]
                                                          .rearrange("p a b -> p (a b)"),
                                                          in_=wtmp[:, :], func=AF.Copy,
                                                          bias=-MG)),
                                  reads=(f"{tag}wtmp",), writes=(wn,))
                    pb.barrier()

                pb.op("sync", (lambda e: e.dma_start(
                    out=cw[:, :, :], in_=cwf.rearrange("(c p) k -> p c k", p=P))),
                    writes=(f"{tag}cw",), dma=True)
                pb.op("sync", (lambda e: e.dma_start(
                    out=cb[:, :], in_=cbf.rearrange("(c p) -> p c", p=P),
                    allow_slow_non_contiguous=True)),
                    writes=(f"{tag}cb",), dma=True)

                with contextlib.ExitStack() as st2:
                    xp = st2.enter_context(nc.sbuf_tensor(f"{tag}xp", [P, 2, DC, 512], F32))
                    sq = st2.enter_context(nc.sbuf_tensor(f"{tag}sq", [P, DC, 512], F32))
                    rt = st2.enter_context(nc.sbuf_tensor(f"{tag}rt", [1, 8, 512], F32))
                    mbc = st2.enter_context(nc.sbuf_tensor(f"{tag}mbc", [P, 512], F32))
                    for i in range(T // 512):
                        pp = i % 2
                        bx = f"{tag}xp{pp}"
                        sl = slice(i * 512, (i + 1) * 512)
                        pb.op("sync", (lambda e, sl=sl, pp=pp: e.dma_start(
                            out=xp[:, pp], in_=fmv(src)[:, :, sl])),
                            writes=(bx,), dma=True)
                        pb.op("scalar", (lambda e, pp=pp: e.activation(
                            out=sq[:], in_=xp[:, pp], func=AF.Square)),
                            reads=(bx,), writes=(f"{tag}sq",))
                        def mmsum(e, pp=pp):
                            for dc in range(DC):
                                ins = e.matmul(ps[6][0:1, 0:512], onescol[:],
                                               xp[:, pp, dc], start=(dc == 0),
                                               stop=(dc == DC - 1))
                            return ins
                        pb.op("tensor", mmsum, reads=(bx, "onescol"), writes=("ps6",))
                        def mmsq(e):
                            for dc in range(DC):
                                ins = e.matmul(ps[7][0:1, 0:512], onescol[:],
                                               sq[:, dc], start=(dc == 0),
                                               stop=(dc == DC - 1))
                            return ins
                        pb.op("tensor", mmsq, reads=(f"{tag}sq", "onescol"), writes=("ps7",))
                        pb.op("vector", (lambda e: e.tensor_scalar(
                            out=rt[0:1, 4], in0=ps[6][0:1, 0:512],
                            scalar1=1.0 / D_MODEL, scalar2=None, op0=Op.mult, op1=Op.bypass)),
                            reads=("ps6",), writes=(f"{tag}rt",))
                        pb.op("vector", (lambda e: e.tensor_tensor(
                            out=rt[0:1, 6], in0=rt[0:1, 4], in1=rt[0:1, 4], op=Op.mult)),
                            reads=(f"{tag}rt",), writes=(f"{tag}rt",))
                        pb.op("vector", (lambda e: e.scalar_tensor_tensor(
                            out=rt[0:1, 5], in0=ps[7][0:1, 0:512], scalar=1.0 / D_MODEL,
                            in1=rt[0:1, 6], op0=Op.mult, op1=Op.subtract)),
                            reads=("ps7", f"{tag}rt"), writes=(f"{tag}rt",))
                        pb.op("scalar", (lambda e: e.activation(
                            out=rt[0:1, 6], in_=rt[0:1, 5], func=AF.Ln,
                            bias=epscol[0:1, 0:1])),
                            reads=(f"{tag}rt", "epscol"), writes=(f"{tag}rt",))
                        pb.op("scalar", (lambda e: e.activation(
                            out=rt[0:1, 0], in_=rt[0:1, 6], func=AF.Exp, scale=-0.5)),
                            reads=(f"{tag}rt",), writes=(f"{tag}rt",))
                        pb.op("vector", (lambda e: e.scalar_tensor_tensor(
                            out=rt[0:1, 1], in0=rt[0:1, 4], scalar=-1.0,
                            in1=rt[0:1, 0], op0=Op.mult, op1=Op.mult)),
                            reads=(f"{tag}rt",), writes=(f"{tag}rt",))
                        bcast(tag, rt[0:1, 4], 512, 6, (f"{tag}rt",))
                        pb.op("vector", (lambda e: e.tensor_copy(mbc[:], ps[6][0:P, 0:512])),
                              reads=("ps6",), writes=(f"{tag}mbc",))
                        def subm(e, pp=pp):
                            rep = mbc[:, :].unsqueeze(1).broadcast_to((P, DC, 512))
                            return e.tensor_tensor(out=sq[:], in0=xp[:, pp], in1=rep,
                                                   op=Op.subtract)
                        pb.op("vector", subm, reads=(bx, f"{tag}mbc"), writes=(f"{tag}sq",))
                        pb.op("scalar", (lambda e: e.activation(out=sq[:], in_=sq[:],
                                                                func=AF.Abs)),
                              reads=(f"{tag}sq",), writes=(f"{tag}sq",))
                        amax_row(tag, sq, f"{tag}sq", DC, 512, rt[0:1, 7], f"{tag}rt", 6, 7)
                        pb.op("vector", (lambda e: e.scalar_tensor_tensor(
                            out=rt[0:1, 6], in0=rt[0:1, 7], scalar=1e-30,
                            in1=rt[0:1, 0], op0=Op.max, op1=Op.mult)),
                            reads=(f"{tag}rt",), writes=(f"{tag}rt",))
                        pb.op("vector", (lambda e: e.tensor_scalar(
                            out=rt[0:1, 6], in0=rt[0:1, 6], scalar1=1e-5,
                            scalar2=None, op0=Op.max, op1=Op.bypass)),
                            reads=(f"{tag}rt",), writes=(f"{tag}rt",))
                        pb.op("vector", (lambda e: e.reciprocal(
                            out=rt[0:1, 7], in_=rt[0:1, 6])),
                            reads=(f"{tag}rt",), writes=(f"{tag}rt",))
                        pb.op("vector", (lambda e: e.tensor_scalar(
                            out=rt[0:1, 2], in0=rt[0:1, 7], scalar1=127.0, scalar2=None,
                            op0=Op.mult, op1=Op.bypass)),
                            reads=(f"{tag}rt",), writes=(f"{tag}rt",))
                        pb.op("vector", (lambda e: e.tensor_scalar(
                            out=rt[0:1, 3], in0=rt[0:1, 6], scalar1=1.0 / 127.0,
                            scalar2=None, op0=Op.mult, op1=Op.bypass)),
                            reads=(f"{tag}rt",), writes=(f"{tag}rt",))
                        pb.op("gpsimd", (lambda e, sl=sl: e.dma_start(
                            out=rowfl[0:4, sl], in_=rt[0:1, 0:4, :])),
                            reads=(f"{tag}rt",), writes=(rowsd,), dma=True)
                    pb.barrier()

                with contextlib.ExitStack() as st2:
                    x0t = st2.enter_context(nc.sbuf_tensor(f"{tag}x0t", [P, 2, DC, 258], F32))
                    xoln = st2.enter_context(nc.sbuf_tensor(f"{tag}xoln", [P, DC, 258], F32))
                    q16 = st2.enter_context(nc.sbuf_tensor(f"{tag}q16", [P, DC, 258], F16))
                    xq = st2.enter_context(nc.sbuf_tensor(f"{tag}xq", [P, DC, 258], BF16))
                    hsb = st2.enter_context(nc.sbuf_tensor(f"{tag}hsb", [P, 2, 258], F32))
                    csb = st2.enter_context(nc.sbuf_tensor(f"{tag}csb", [P, HC, 256], F32))
                    h16 = st2.enter_context(nc.sbuf_tensor(f"{tag}h16", [P, HC, 256], F16))
                    amxb = st2.enter_context(nc.sbuf_tensor(f"{tag}amxb", [P, HC, 256], F32))
                    hq = st2.enter_context(nc.sbuf_tensor(f"{tag}hq", [P, HC, 256], BF16))
                    sbc = st2.enter_context(nc.sbuf_tensor(f"{tag}sbc", [P, 3, 258], F32))
                    rw = st2.enter_context(nc.sbuf_tensor(f"{tag}rw", [1, 4, 258], F32))
                    rr = st2.enter_context(nc.sbuf_tensor(f"{tag}rr", [1, 2, 258], F32))

                    pb.op("gpsimd", (lambda e: e.memset(x0t[:], 0.0)),
                          writes=(f"{tag}x0t0", f"{tag}x0t1"))
                    pb.op("gpsimd", (lambda e: e.memset(rr[:], 0.0)),
                          writes=(f"{tag}rr",))
                    pb.op("gpsimd", (lambda e: e.memset(rw[:], 0.0)),
                          writes=(f"{tag}rw",))

                    for i in range(NFPH):
                        t0 = i * FPH
                        w0 = t0 - 1
                        lo, hi = max(w0, 0), min(t0 + FPH + 1, T)
                        off = lo - w0
                        pp = i % 2
                        bx = f"{tag}x0t{pp}"
                        pb.op("sync", (lambda e, lo=lo, hi=hi, off=off, pp=pp:
                                       e.dma_start(out=x0t[:, pp, :, off:off + (hi - lo)],
                                                   in_=fmv(src)[:, :, lo:hi])),
                              writes=(bx,), dma=True)
                        pb.op("sync", (lambda e, lo=lo, hi=hi, off=off: e.dma_start(
                            out=rw[0:1, :, off:off + (hi - lo)], in_=rowfl[0:4, lo:hi])),
                            reads=(rowsd,), writes=(f"{tag}rw",), dma=True)
                        if off == 1:
                            pb.op("gpsimd", (lambda e: e.memset(rw[0:1, :, 0:1], 0.0)),
                                  writes=(f"{tag}rw",))
                        if hi == T:
                            pb.op("gpsimd", (lambda e: e.memset(rw[0:1, :, 257:258], 0.0)),
                                  writes=(f"{tag}rw",))
                        def mkrr(e):
                            e.tensor_tensor(out=rr[0:1, 0, :], in0=rw[0:1, 0, :],
                                            in1=rw[0:1, 2, :], op=Op.mult)
                            return e.tensor_tensor(out=rr[0:1, 1, :], in0=rw[0:1, 1, :],
                                                   in1=rw[0:1, 2, :], op=Op.mult)
                        pb.op("vector", mkrr, reads=(f"{tag}rw",), writes=(f"{tag}rr",))
                        bcast(tag, rr[0:1, 0, 0:258], 258, 0, (f"{tag}rr",))
                        bcast(tag, rr[0:1, 1, 0:258], 258, 1, (f"{tag}rr",))
                        def cpsbc(e):
                            e.tensor_copy(sbc[:, 0, 0:258], ps[0][0:P, 0:258])
                            return e.tensor_copy(sbc[:, 1, 0:258], ps[1][0:P, 0:258])
                        pb.op("vector", cpsbc, reads=("ps0", "ps1"), writes=(f"{tag}sbc",))
                        def qa(e, pp=pp):
                            rep = sbc[:, 0, 0:258].unsqueeze(1).broadcast_to((P, DC, 258))
                            return e.tensor_tensor(out=xoln[:], in0=x0t[:, pp], in1=rep,
                                                   op=Op.mult)
                        pb.op("vector", qa, reads=(bx, f"{tag}sbc"), writes=(f"{tag}xoln",))
                        def qb(e):
                            rep = sbc[:, 1, 0:258].unsqueeze(1).broadcast_to((P, DC, 258))
                            return e.scalar_tensor_tensor(out=q16[:], in0=xoln[:], scalar=MG,
                                                          in1=rep, op0=Op.add, op1=Op.add)
                        pb.op("vector", qb, reads=(f"{tag}xoln", f"{tag}sbc"),
                              writes=(f"{tag}q16",))
                        pb.op("scalar", (lambda e: e.activation(out=xq[:], in_=q16[:],
                                                                func=AF.Copy, bias=-MG)),
                              reads=(f"{tag}q16",), writes=(f"{tag}xq",))
                        if off == 1:
                            pb.op("gpsimd", (lambda e: e.memset(xq[:, :, 0:1], 0.0)),
                                  writes=(f"{tag}xq",))
                        if hi == T:
                            pb.op("gpsimd", (lambda e: e.memset(xq[:, :, 257:258], 0.0)),
                                  writes=(f"{tag}xq",))
                        pb.op("vector", (lambda e: e.tensor_scalar(
                            out=rr[0:1, 0, :], in0=rw[0:1, 3, :],
                            scalar1=swr[0:1, 2:3], scalar2=None,
                            op0=Op.mult, op1=Op.bypass)),
                            reads=(f"{tag}rw", f"{tag}swr"), writes=(f"{tag}rr",))
                        bcast(tag, rr[0:1, 0, 0:258], 258, 0, (f"{tag}rr",))
                        pb.op("vector", (lambda e: e.tensor_copy(sbc[:, 1, 0:258],
                                                                 ps[0][0:P, 0:258])),
                              reads=("ps0",), writes=(f"{tag}sbc",))

                        for hc in range(HC):
                            bank = 4 + (hc % 2)
                            hr = hc % 2
                            def mm1(e, hc=hc, bank=bank):
                                for dc in range(DC):
                                    ins = e.matmul(ps[bank][0:P, 0:258],
                                                   w1q[:, dc, hc * P:(hc + 1) * P],
                                                   xq[:, dc], start=(dc == 0),
                                                   stop=(dc == DC - 1))
                                return ins
                            bcs = f"{tag}csb{hr}"
                            pb.op("tensor", mm1, reads=(f"{tag}w1q", f"{tag}xq"),
                                  writes=(f"ps{bank}",))
                            pb.op("vector", (lambda e, hr=hr, bank=bank: e.tensor_tensor(
                                out=hsb[:, hr], in0=ps[bank][0:P, 0:258],
                                in1=sbc[:, 1, 0:258], op=Op.mult)),
                                reads=(f"ps{bank}", f"{tag}sbc"), writes=(f"{tag}hsb{hr}",))
                            pb.op("gpsimd", (lambda e, hc=hc, hr=hr: e.tensor_scalar(
                                out=csb[:, hc], in0=hsb[:, hr, 0:256],
                                scalar1=cw[:, hc, 0:1], scalar2=cb[:, hc:hc + 1],
                                op0=Op.mult, op1=Op.add)),
                                reads=(f"{tag}hsb{hr}", f"{tag}cw", f"{tag}cb"),
                                writes=(bcs,))
                            pb.op("vector", (lambda e, hc=hc, hr=hr: e.scalar_tensor_tensor(
                                out=csb[:, hc], in0=hsb[:, hr, 1:257],
                                scalar=cw[:, hc, 1:2], in1=csb[:, hc],
                                op0=Op.mult, op1=Op.add)),
                                reads=(f"{tag}hsb{hr}", f"{tag}cw", bcs),
                                writes=(bcs,))
                            pb.op("vector", (lambda e, hc=hc, hr=hr: e.scalar_tensor_tensor(
                                out=csb[:, hc], in0=hsb[:, hr, 2:258],
                                scalar=cw[:, hc, 2:3], in1=csb[:, hc],
                                op0=Op.mult, op1=Op.add)),
                                reads=(f"{tag}hsb{hr}", f"{tag}cw", bcs),
                                writes=(bcs,))
                        pb.op("scalar", (lambda e: e.activation(
                            out=csb[:], in_=csb[:], func=AF.Silu)),
                            reads=(f"{tag}csb0", f"{tag}csb1"),
                            writes=(f"{tag}csb0", f"{tag}csb1"))
                        pb.op("scalar", (lambda e: e.activation(
                            out=amxb[:], in_=csb[:], func=AF.Abs)),
                            reads=(f"{tag}csb0", f"{tag}csb1"), writes=(f"{tag}amxb",))
                        w = HC // 2
                        while w >= 1:
                            pb.op("vector", (lambda e, w=w: e.tensor_tensor(
                                out=amxb[:, 0:w, :], in0=amxb[:, 0:w, :],
                                in1=amxb[:, w:2 * w, :], op=Op.max)),
                                reads=(f"{tag}amxb",), writes=(f"{tag}amxb",))
                            w //= 2
                        amax_row(tag, amxb, f"{tag}amxb", 1, 256, rr[0:1, 0, 0:256],
                                 f"{tag}rr", 2, 3)
                        pb.op("vector", (lambda e: e.tensor_scalar(
                            out=rr[0:1, 0, 0:256], in0=rr[0:1, 0, 0:256], scalar1=1e-5,
                            scalar2=None, op0=Op.max, op1=Op.bypass)),
                            reads=(f"{tag}rr",), writes=(f"{tag}rr",))
                        pb.op("vector", (lambda e: e.tensor_scalar(
                            out=rr[0:1, 1, 0:256], in0=rr[0:1, 0, 0:256],
                            scalar1=swr[0:1, 3:4], scalar2=half / 127.0,
                            op0=Op.mult, op1=Op.mult)),
                            reads=(f"{tag}rr", f"{tag}swr"), writes=(f"{tag}rr",))
                        pb.op("vector", (lambda e: e.reciprocal(
                            out=rr[0:1, 0, 0:256], in_=rr[0:1, 0, 0:256])),
                            reads=(f"{tag}rr",), writes=(f"{tag}rr",))
                        pb.op("vector", (lambda e: e.tensor_scalar(
                            out=rr[0:1, 0, 0:256], in0=rr[0:1, 0, 0:256], scalar1=127.0,
                            scalar2=None, op0=Op.mult, op1=Op.bypass)),
                            reads=(f"{tag}rr",), writes=(f"{tag}rr",))
                        bcast(tag, rr[0:1, 0, 0:256], 256, 0, (f"{tag}rr",))
                        bcast(tag, rr[0:1, 1, 0:256], 256, 1, (f"{tag}rr",))
                        def cph(e):
                            e.tensor_copy(sbc[:, 1, 0:256], ps[0][0:P, 0:256])
                            return e.tensor_copy(sbc[:, 2, 0:256], ps[1][0:P, 0:256])
                        pb.op("vector", cph, reads=("ps0", "ps1"), writes=(f"{tag}sbc",))
                        pb.op("vector", (lambda e: e.tensor_tensor(
                            out=amxb[:], in0=csb[:],
                            in1=sbc[:, 1, 0:256].unsqueeze(1).broadcast_to((P, HC, 256)),
                            op=Op.mult)),
                            reads=(f"{tag}csb0", f"{tag}csb1", f"{tag}sbc"),
                            writes=(f"{tag}amxb",))
                        pb.op("scalar", (lambda e: e.activation(
                            out=h16[:], in_=amxb[:], func=AF.Copy, bias=MG)),
                            reads=(f"{tag}amxb",), writes=(f"{tag}h16",))
                        pb.op("scalar", (lambda e: e.activation(
                            out=hq[:, :, :], in_=h16[:], func=AF.Copy, bias=-MG)),
                            reads=(f"{tag}h16",), writes=(f"{tag}hq",))
                        for oc in range(DC):
                            bank = 4 + (oc % 2)
                            def mm2(e, oc=oc, bank=bank):
                                for hc2 in range(HC):
                                    ins = e.matmul(ps[bank][0:P, 0:256],
                                                   w2q[:, hc2, oc * P:(oc + 1) * P],
                                                   hq[:, hc2], start=(hc2 == 0),
                                                   stop=(hc2 == HC - 1))
                                return ins
                            pb.op("tensor", mm2, reads=(f"{tag}w2q", f"{tag}hq"),
                                  writes=(f"ps{bank}",))
                            pb.op("vector", (lambda e, oc=oc, bank=bank: e.tensor_tensor(
                                out=xoln[:, oc, 0:256], in0=ps[bank][0:P, 0:256],
                                in1=sbc[:, 2, 0:256], op=Op.mult)),
                                reads=(f"ps{bank}", f"{tag}sbc"), writes=(f"{tag}xoln",))
                        pb.op("vector", (lambda e, pp=pp: e.tensor_tensor(
                            out=xoln[:, :, 0:256], in0=xoln[:, :, 0:256],
                            in1=x0t[:, pp, :, 1:257], op=Op.add)),
                            reads=(f"{tag}xoln", bx), writes=(f"{tag}xoln",))
                        pb.op("gpsimd", (lambda e, t0=t0: e.dma_start(
                            out=fmv(dst)[:, :, t0:t0 + FPH], in_=xoln[:, :, 0:256])),
                            reads=(f"{tag}xoln",), writes=(f"dram{id(dst)}",),
                            dma=True, force_mark=is_out)
                        if DEBUG_FFN and tag == "f1" and i == 0:
                            pb.op("gpsimd", (lambda e: e.dma_start(out=dxq[:, :, :], in_=xq[:])),
                                  reads=(f"{tag}xq",), dma=True, force_mark=True)
                            pb.op("gpsimd", (lambda e: e.dma_start(out=dcs[:, :, :], in_=csb[:])),
                                  reads=(f"{tag}csb0", f"{tag}csb1"), dma=True,
                                  force_mark=True)
                            pb.op("gpsimd", (lambda e: e.dma_start(out=dhq[:, :, :], in_=hq[:])),
                                  reads=(f"{tag}hq",), dma=True, force_mark=True)
                            pb.op("gpsimd", (lambda e: e.dma_start(out=dhh[:, :, :], in_=hsb[:])),
                                  reads=(f"{tag}hsb0", f"{tag}hsb1", f"{tag}hsb2", f"{tag}hsb3"),
                                  dma=True, force_mark=True)
                            pb.op("gpsimd", (lambda e: e.dma_start(out=dsw[:, :], in_=swr[:])),
                                  reads=(f"{tag}swr",), dma=True, force_mark=True)
                    if DEBUG_FFN and tag == "f1":
                        pb.op("gpsimd", (lambda e: e.dma_start(out=drw[:, :], in_=rowfl[:, :])),
                              reads=(rowsd,), dma=True, force_mark=True)
                    pb.barrier()

        def m0():
            with contextlib.ExitStack() as st1:
                xp = st1.enter_context(nc.sbuf_tensor("m0xp", [P, 2, DC, 512], F32))
                sq = st1.enter_context(nc.sbuf_tensor("m0sq", [P, DC, 512], F32))
                rt = st1.enter_context(nc.sbuf_tensor("m0rt", [1, 4, 512], F32))
                bc2 = st1.enter_context(nc.sbuf_tensor("m0bc", [P, 2, 512], F32))
                lnf = st1.enter_context(nc.sbuf_tensor("m0lnf", [P, DC, 512], F32))
                lhi = st1.enter_context(nc.sbuf_tensor("m0lhi", [P, DC, 512], BF16))
                llo = st1.enter_context(nc.sbuf_tensor("m0llo", [P, DC, 512], BF16))
                rwh = st1.enter_context(nc.sbuf_tensor("m0rwh", [P, DC, E], BF16))
                rwl = st1.enter_context(nc.sbuf_tensor("m0rwl", [P, DC, E], BF16))
                rwf = st1.enter_context(nc.sbuf_tensor("m0rwf", [P, DC, E], F32))
                sm = st1.enter_context(nc.sbuf_tensor("m0sm", [P, E], F32))
                smc = st1.enter_context(nc.sbuf_tensor("m0smc", [P, 2], F32))
                pb.op("sync", (lambda e: e.dma_start(
                    out=rwf[:, :, :], in_=rwT.rearrange("(c p) e -> p c e", p=P))),
                    writes=("m0rwf",), dma=True)
                pb.op("vector", (lambda e: e.tensor_copy(rwh[:], rwf[:])),
                      reads=("m0rwf",), writes=("m0rwh",))
                pb.op("vector", (lambda e: e.tensor_tensor(
                    out=rwl[:], in0=rwf[:], in1=rwh[:], op=Op.subtract)),
                    reads=("m0rwf", "m0rwh"), writes=("m0rwl",))
                for i in range(T // 512):
                    pp = i % 2
                    bx = f"m0xp{pp}"
                    sl = slice(i * 512, (i + 1) * 512)
                    pb.op("sync", (lambda e, sl=sl, pp=pp: e.dma_start(
                        out=xp[:, pp], in_=fmv(x1)[:, :, sl])),
                        reads=("dramx1",), writes=(bx,), dma=True)
                    pb.op("scalar", (lambda e, pp=pp: e.activation(
                        out=sq[:], in_=xp[:, pp], func=AF.Square)),
                        reads=(bx,), writes=("m0sq",))
                    def mmsum(e, pp=pp):
                        for dc in range(DC):
                            ins = e.matmul(ps[6][0:1, 0:512], onescol[:],
                                           xp[:, pp, dc], start=(dc == 0),
                                           stop=(dc == DC - 1))
                        return ins
                    pb.op("tensor", mmsum, reads=(bx, "onescol"), writes=("ps6",))
                    def mmsq(e):
                        for dc in range(DC):
                            ins = e.matmul(ps[7][0:1, 0:512], onescol[:],
                                           sq[:, dc], start=(dc == 0),
                                           stop=(dc == DC - 1))
                        return ins
                    pb.op("tensor", mmsq, reads=("m0sq", "onescol"), writes=("ps7",))
                    pb.op("vector", (lambda e: e.tensor_scalar(
                        out=rt[0:1, 0], in0=ps[6][0:1, 0:512],
                        scalar1=1.0 / D_MODEL, scalar2=None, op0=Op.mult, op1=Op.bypass)),
                        reads=("ps6",), writes=("m0rt",))
                    pb.op("vector", (lambda e: e.tensor_tensor(
                        out=rt[0:1, 1], in0=rt[0:1, 0], in1=rt[0:1, 0], op=Op.mult)),
                        reads=("m0rt",), writes=("m0rt",))
                    pb.op("vector", (lambda e: e.scalar_tensor_tensor(
                        out=rt[0:1, 1], in0=ps[7][0:1, 0:512], scalar=1.0 / D_MODEL,
                        in1=rt[0:1, 1], op0=Op.mult, op1=Op.subtract)),
                        reads=("ps7", "m0rt"), writes=("m0rt",))
                    pb.op("scalar", (lambda e: e.activation(
                        out=rt[0:1, 1], in_=rt[0:1, 1], func=AF.Ln,
                        bias=epscol[0:1, 0:1])),
                        reads=("m0rt", "epscol"), writes=("m0rt",))
                    pb.op("scalar", (lambda e: e.activation(
                        out=rt[0:1, 2], in_=rt[0:1, 1], func=AF.Exp, scale=-0.5)),
                        reads=("m0rt",), writes=("m0rt",))
                    pb.op("vector", (lambda e: e.scalar_tensor_tensor(
                        out=rt[0:1, 3], in0=rt[0:1, 0], scalar=-1.0,
                        in1=rt[0:1, 2], op0=Op.mult, op1=Op.mult)),
                        reads=("m0rt",), writes=("m0rt",))
                    bcast("m0", rt[0:1, 2], 512, 6, ("m0rt",))
                    bcast("m0", rt[0:1, 3], 512, 7, ("m0rt",))
                    def cpbc(e):
                        e.tensor_copy(bc2[:, 0], ps[6][0:P, 0:512])
                        return e.tensor_copy(bc2[:, 1], ps[7][0:P, 0:512])
                    pb.op("vector", cpbc, reads=("ps6", "ps7"), writes=("m0bc",))
                    def lna(e, pp=pp):
                        rep = bc2[:, 0].unsqueeze(1).broadcast_to((P, DC, 512))
                        return e.tensor_tensor(out=lnf[:], in0=xp[:, pp], in1=rep,
                                               op=Op.mult)
                    pb.op("vector", lna, reads=(bx, "m0bc"), writes=("m0lnf",))
                    def lnb(e):
                        rep = bc2[:, 1].unsqueeze(1).broadcast_to((P, DC, 512))
                        return e.tensor_tensor(out=lnf[:], in0=lnf[:], in1=rep, op=Op.add)
                    pb.op("vector", lnb, reads=("m0lnf", "m0bc"), writes=("m0lnf",))
                    pb.op("vector", (lambda e: e.tensor_copy(lhi[:], lnf[:])),
                          reads=("m0lnf",), writes=("m0lhi",))
                    pb.op("vector", (lambda e: e.tensor_tensor(
                        out=llo[:], in0=lnf[:], in1=lhi[:], op=Op.subtract)),
                        reads=("m0lnf", "m0lhi"), writes=("m0llo",))
                    pb.op("gpsimd", (lambda e, sl=sl: e.dma_start(
                        out=fmv(ln2hi)[:, :, sl], in_=lhi[:])),
                        reads=("m0lhi",), writes=("dramlnhi",), dma=True)
                    pb.op("gpsimd", (lambda e, sl=sl: e.dma_start(
                        out=fmv(ln2lo)[:, :, sl], in_=llo[:])),
                        reads=("m0llo",), writes=("dramlnlo",), dma=True)
                    for g in range(4):
                        G = i * 4 + g
                        t0g = g * 128
                        def rmm(e, t0g=t0g):
                            for dc in range(DC):
                                e.matmul(ps[5][0:P, 0:E], lhi[:, dc, t0g:t0g + 128],
                                         rwh[:, dc], start=(dc == 0), stop=False)
                                e.matmul(ps[5][0:P, 0:E], lhi[:, dc, t0g:t0g + 128],
                                         rwl[:, dc], start=False, stop=False)
                                ins = e.matmul(ps[5][0:P, 0:E], llo[:, dc, t0g:t0g + 128],
                                               rwh[:, dc], start=False,
                                               stop=(dc == DC - 1))
                            return ins
                        pb.op("tensor", rmm, reads=("m0lhi", "m0llo", "m0rwh", "m0rwl"),
                              writes=("ps5",))
                        pb.op("vector", (lambda e: e.tensor_reduce(
                            out=smc[:, 0:1], in_=ps[5][0:P, 0:E], axis=AX.X, op=Op.max)),
                            reads=("ps5",), writes=("m0smc",))
                        pb.op("vector", (lambda e: e.tensor_scalar(
                            out=sm[:], in0=ps[5][0:P, 0:E], scalar1=smc[:, 0:1],
                            scalar2=None, op0=Op.subtract, op1=Op.bypass)),
                            reads=("ps5", "m0smc"), writes=("m0sm",))
                        pb.op("scalar", (lambda e: e.activation(
                            out=sm[:], in_=sm[:], func=AF.Exp)),
                            reads=("m0sm",), writes=("m0sm",))
                        pb.op("vector", (lambda e: e.tensor_reduce(
                            out=smc[:, 0:1], in_=sm[:], axis=AX.X, op=Op.add)),
                            reads=("m0sm",), writes=("m0smc",))
                        pb.op("vector", (lambda e: e.reciprocal(
                            out=smc[:, 1:2], in_=smc[:, 0:1])),
                            reads=("m0smc",), writes=("m0smc",))
                        pb.op("vector", (lambda e, G=G: e.tensor_scalar(
                            out=r_sb[:, G * E:(G + 1) * E], in0=sm[:],
                            scalar1=smc[:, 1:2], scalar2=None,
                            op0=Op.mult, op1=Op.bypass)),
                            reads=("m0sm", "m0smc"), writes=("r_sb",))
                pb.barrier()

        def m1(ei):
            with contextlib.ExitStack() as st1:
                lhi = st1.enter_context(nc.sbuf_tensor(f"e{ei}lhi", [P, DC, T], BF16))
                llo = st1.enter_context(nc.sbuf_tensor(f"e{ei}llo", [P, DC, T], BF16))
                wf = st1.enter_context(nc.sbuf_tensor(f"e{ei}wf", [P, 2, DC, P], F32))
                whi = st1.enter_context(nc.sbuf_tensor(f"e{ei}whi", [P, 2, DC, P], BF16))
                wlo = st1.enter_context(nc.sbuf_tensor(f"e{ei}wlo", [P, 2, DC, P], BF16))
                pb.op("gpsimd", (lambda e: e.memset(whi[:], 0.0)),
                      writes=(f"e{ei}whi0", f"e{ei}whi1"))
                pb.op("gpsimd", (lambda e: e.memset(wlo[:], 0.0)),
                      writes=(f"e{ei}wlo0", f"e{ei}wlo1"))
                zst = st1.enter_context(nc.sbuf_tensor(f"e{ei}zst", [P, 2, T], F32))
                pb.op("sync", (lambda e: e.dma_start(out=lhi[:], in_=fmv(ln2hi))),
                      reads=("dramlnhi",), writes=(f"e{ei}lhi",), dma=True)
                pb.op("sync", (lambda e: e.dma_start(out=llo[:], in_=fmv(ln2lo))),
                      reads=("dramlnlo",), writes=(f"e{ei}llo",), dma=True)
                inwv = inw[ei].rearrange("(c p) f -> p c f", p=P)
                NFC = 35
                for fc in range(NFC):
                    fw = 128 if fc < 34 else 32
                    f0 = fc * 128
                    pp = fc % 2
                    bw = f"e{ei}wf{pp}"
                    bz = f"e{ei}zst{pp}"
                    pb.op("sync", (lambda e, f0=f0, fw=fw, pp=pp: e.dma_start(
                        out=wf[:, pp, :, 0:fw], in_=inwv[:, :, f0:f0 + fw])),
                        writes=(bw,), dma=True)
                    pb.op("vector", (lambda e, pp=pp, fw=fw: e.tensor_copy(
                        whi[:, pp, :, 0:fw], wf[:, pp, :, 0:fw])),
                        reads=(bw,), writes=(f"e{ei}whi{pp}",))
                    pb.op("vector", (lambda e, pp=pp, fw=fw: e.tensor_tensor(
                        out=wlo[:, pp, :, 0:fw], in0=wf[:, pp, :, 0:fw],
                        in1=whi[:, pp, :, 0:fw], op=Op.subtract)),
                        reads=(bw, f"e{ei}whi{pp}"), writes=(f"e{ei}wlo{pp}",))
                    for tg in range(T // 512):
                        sl = slice(tg * 512, (tg + 1) * 512)
                        bank = 4 + (tg % 2)
                        def pmm(e, pp=pp, sl=sl, bank=bank):
                            for dc in range(DC):
                                e.matmul(ps[bank][0:P, 0:512], whi[:, pp, dc, :],
                                         lhi[:, dc, sl], start=(dc == 0), stop=False)
                                e.matmul(ps[bank][0:P, 0:512], whi[:, pp, dc, :],
                                         llo[:, dc, sl], start=False, stop=False)
                                ins = e.matmul(ps[bank][0:P, 0:512], wlo[:, pp, dc, :],
                                               lhi[:, dc, sl], start=False,
                                               stop=(dc == DC - 1))
                            return ins
                        pb.op("tensor", pmm,
                              reads=(f"e{ei}whi{pp}", f"e{ei}wlo{pp}",
                                     f"e{ei}lhi", f"e{ei}llo"),
                              writes=(f"ps{bank}",))
                        pb.op("vector", (lambda e, fw=fw, sl=sl, pp=pp, bank=bank:
                                         e.tensor_copy(zst[0:fw, pp, sl],
                                                       ps[bank][0:fw, 0:512])),
                              reads=(f"ps{bank}",), writes=(bz,))
                    pb.op("gpsimd", (lambda e, f0=f0, fw=fw, pp=pp: e.dma_start(
                        out=zx[f0:f0 + fw, :], in_=zst[0:fw, pp, :])),
                        reads=(bz,), writes=("dramzx",), dma=True)
                pb.barrier()

        def m2(ei):
            with contextlib.ExitStack() as st1:
                owhi = st1.enter_context(nc.sbuf_tensor(f"o{ei}owhi", [P, DINC, 1024], BF16))
                owlo = st1.enter_context(nc.sbuf_tensor(f"o{ei}owlo", [P, DINC, 1024], BF16))
                cw = st1.enter_context(nc.sbuf_tensor(f"o{ei}cw", [P, CVC, 4], F32))
                cb = st1.enter_context(nc.sbuf_tensor(f"o{ei}cb", [P, CVC], F32))
                nwcol = st1.enter_context(nc.sbuf_tensor(f"o{ei}nw", [P, DINC], F32))
                dskb = st1.enter_context(nc.sbuf_tensor(f"o{ei}dsk", [P, NH], F32))
                acol = st1.enter_context(nc.sbuf_tensor(f"o{ei}acol", [NH, 2], F32))
                onesh = st1.enter_context(nc.sbuf_tensor(f"o{ei}ones", [NH, CH], F32))
                state = st1.enter_context(nc.sbuf_tensor(f"o{ei}st", [P, D_INNER], F32))
                pb.op("sync", (lambda e: e.dma_start(
                    out=cw[:, :, :], in_=cwm[ei].rearrange("(c p) k -> p c k", p=P))),
                    writes=(f"o{ei}cw",), dma=True)
                pb.op("sync", (lambda e: e.dma_start(
                    out=cb[:, :], in_=cbm[ei].rearrange("(c p) -> p c", p=P),
                    allow_slow_non_contiguous=True)),
                    writes=(f"o{ei}cb",), dma=True)
                pb.op("sync", (lambda e: e.dma_start(
                    out=nwcol[:, :], in_=nrmw[ei].rearrange("(c p) -> p c", p=P),
                    allow_slow_non_contiguous=True)),
                    writes=(f"o{ei}nw",), dma=True)
                pb.op("sync", (lambda e: e.dma_start(
                    out=dskb[:, :], in_=dskp[ei:ei + 1, :].partition_broadcast(P))),
                    writes=(f"o{ei}dsk",), dma=True)
                pb.op("sync", (lambda e: e.dma_start(
                    out=acol[:, 0:1], in_=alog[ei].unsqueeze(-1))),
                    writes=(f"o{ei}acol",), dma=True)
                pb.op("sync", (lambda e: e.dma_start(
                    out=acol[:, 1:2], in_=dtb[ei].unsqueeze(-1))),
                    writes=(f"o{ei}acol",), dma=True)
                pb.op("gpsimd", (lambda e: e.memset(onesh[:], 1.0)), writes=(f"o{ei}ones",))
                pb.op("gpsimd", (lambda e: e.memset(state[:], 0.0)), writes=(f"o{ei}st",))
                pb.op("scalar", (lambda e: e.activation(
                    out=acol[:, 0:1], in_=acol[:, 0:1], func=AF.Exp)),
                    reads=(f"o{ei}acol",), writes=(f"o{ei}acol",))
                owv = outw[ei].rearrange("(c p) d -> p c d", p=P)
                with contextlib.ExitStack() as st2:
                    owf = st2.enter_context(nc.sbuf_tensor(f"o{ei}owf", [P, 8, 1024], F32))
                    for pc2 in range(2):
                        pb.op("sync", (lambda e, pc2=pc2: e.dma_start(
                            out=owf[:, :, :], in_=owv[:, pc2 * 8:(pc2 + 1) * 8, :])),
                            writes=(f"o{ei}owf",), dma=True)
                        def nwsc(e, pc2=pc2):
                            for j in range(8):
                                ins = e.tensor_scalar(
                                    out=owf[:, j, :], in0=owf[:, j, :],
                                    scalar1=nwcol[:, pc2 * 8 + j:pc2 * 8 + j + 1],
                                    scalar2=None, op0=Op.mult, op1=Op.bypass)
                            return ins
                        pb.op("vector", nwsc, reads=(f"o{ei}owf", f"o{ei}nw"),
                              writes=(f"o{ei}owf",))
                        pb.op("vector", (lambda e, pc2=pc2: e.tensor_copy(
                            owhi[:, pc2 * 8:(pc2 + 1) * 8, :], owf[:, :, :])),
                            reads=(f"o{ei}owf",), writes=(f"o{ei}owhi",))
                        pb.op("vector", (lambda e, pc2=pc2: e.tensor_tensor(
                            out=owlo[:, pc2 * 8:(pc2 + 1) * 8, :], in0=owf[:, :, :],
                            in1=owhi[:, pc2 * 8:(pc2 + 1) * 8, :], op=Op.subtract)),
                            reads=(f"o{ei}owf", f"o{ei}owhi"), writes=(f"o{ei}owlo",))
                    dts = st2.enter_context(nc.sbuf_tensor(f"o{ei}dts", [NH, 6, 512], F32))
                    for ph in range(T // 512):
                        sl = slice(ph * 512, (ph + 1) * 512)
                        pb.op("sync", (lambda e, sl=sl: e.dma_start(
                            out=dts[:, 0, :], in_=zx[4352:4384, sl])),
                            reads=("dramzx",), writes=(f"o{ei}dts",), dma=True)
                        pb.op("vector", (lambda e: e.tensor_scalar(
                            out=dts[:, 1, :], in0=dts[:, 0, :], scalar1=acol[:, 1:2],
                            scalar2=0.0, op0=Op.add, op1=Op.max)),
                            reads=(f"o{ei}dts", f"o{ei}acol"), writes=(f"o{ei}dts",))
                        pb.op("scalar", (lambda e: e.activation(
                            out=dts[:, 2, :], in_=dts[:, 0, :], func=AF.Abs,
                            bias=acol[:, 1:2])),
                            reads=(f"o{ei}dts", f"o{ei}acol"), writes=(f"o{ei}dts",))
                        pb.op("scalar", (lambda e: e.activation(
                            out=dts[:, 2, :], in_=dts[:, 2, :], func=AF.Exp, scale=-1.0)),
                            reads=(f"o{ei}dts",), writes=(f"o{ei}dts",))
                        pb.op("scalar", (lambda e: e.activation(
                            out=dts[:, 2, :], in_=dts[:, 2, :], func=AF.Ln,
                            bias=onescol[0:NH, 0:1])),
                            reads=(f"o{ei}dts", "onescol"), writes=(f"o{ei}dts",))
                        pb.op("vector", (lambda e: e.tensor_tensor(
                            out=dts[:, 1, :], in0=dts[:, 1, :], in1=dts[:, 2, :],
                            op=Op.add)),
                            reads=(f"o{ei}dts",), writes=(f"o{ei}dts",))
                        pb.op("vector", (lambda e: e.tensor_scalar(
                            out=dts[:, 2, :], in0=dts[:, 1, :], scalar1=acol[:, 0:1],
                            scalar2=-1.0, op0=Op.mult, op1=Op.mult)),
                            reads=(f"o{ei}dts", f"o{ei}acol"), writes=(f"o{ei}dts",))
                        def csscan(e):
                            for cc in range(4):
                                slc = slice(cc * CH, (cc + 1) * CH)
                                ins = e.tensor_tensor_scan(
                                    out=dts[:, 3, slc], data0=onesh[:],
                                    data1=dts[:, 2, slc], initial=0.0,
                                    op0=Op.mult, op1=Op.add)
                            return ins
                        pb.op("vector", csscan, reads=(f"o{ei}dts", f"o{ei}ones"),
                              writes=(f"o{ei}dts",))
                        pb.op("scalar", (lambda e: e.activation(
                            out=dts[:, 5, :], in_=dts[:, 3, :], func=AF.Exp)),
                            reads=(f"o{ei}dts",), writes=(f"o{ei}dts",))
                        def dsx(e):
                            for cc in range(4):
                                slc = slice(cc * CH, (cc + 1) * CH)
                                ins = e.activation(
                                    out=dts[:, 4, slc], in_=dts[:, 3, slc],
                                    func=AF.Exp, scale=-1.0,
                                    bias=dts[:, 3, (cc + 1) * CH - 1:(cc + 1) * CH])
                            return ins
                        pb.op("scalar", dsx, reads=(f"o{ei}dts",), writes=(f"o{ei}dts",))
                        pb.op("vector", (lambda e: e.tensor_tensor(
                            out=dts[:, 2, :], in0=dts[:, 1, :], in1=dts[:, 4, :],
                            op=Op.mult)),
                            reads=(f"o{ei}dts",), writes=(f"o{ei}dts",))
                        for ri, row in ((0, 1), (1, 2), (2, 5), (3, 3)):
                            pb.op("gpsimd", (lambda e, ri=ri, row=row, sl=sl: e.dma_start(
                                out=dtsfl[ri * NH:(ri + 1) * NH, sl],
                                in_=dts[:, row, :])),
                                reads=(f"o{ei}dts",), writes=("dramdts",), dma=True)
                        pb.op("gpsimd", (lambda e, ph=ph: e.dma_start(
                            out=decfl[ph * 4:(ph + 1) * 4, :].transpose([1, 0]),
                            in_=dts[:, 5, CH - 1::CH],
                            allow_slow_non_contiguous=True)),
                            reads=(f"o{ei}dts",), writes=("dramdec",), dma=True)
                    pb.barrier()

                xbc = st1.enter_context(nc.sbuf_tensor(f"o{ei}xbc", [P, 2, CVC, 131], F32))
                zt = st1.enter_context(nc.sbuf_tensor(f"o{ei}zt", [P, DINC, CH], F32))
                csbc = st1.enter_context(nc.sbuf_tensor(f"o{ei}csbc", [P, NH, CH], F32))
                decb = st1.enter_context(nc.sbuf_tensor(f"o{ei}decb", [P, NH], F32))
                cvo = st1.enter_context(nc.sbuf_tensor(f"o{ei}cvo", [P, CVC, CH], F32))
                cvt = st1.enter_context(nc.sbuf_tensor(f"o{ei}cvt", [P, 9, CH], F32))
                dtsl = st1.enter_context(nc.sbuf_tensor(f"o{ei}dtsl", [P, CH], F32))
                tsml = st1.enter_context(nc.sbuf_tensor(f"o{ei}tsml", [P, 4, NH], F32))
                xdtT = st1.enter_context(nc.sbuf_tensor(f"o{ei}xdtT", [P, 1, DINC, P], F32))
                xqT = st1.enter_context(nc.sbuf_tensor(f"o{ei}xqT", [P, DINC, P], F32))
                xskT = st1.enter_context(nc.sbuf_tensor(f"o{ei}xskT", [P, DINC, P], F32))
                bmT = st1.enter_context(nc.sbuf_tensor(f"o{ei}bmT", [P, P], F32))
                szT = st1.enter_context(nc.sbuf_tensor(f"o{ei}szT", [P, D_INNER], F32))
                cbt = st1.enter_context(nc.sbuf_tensor(f"o{ei}cbt", [P, P], F32))
                mh = st1.enter_context(nc.sbuf_tensor(f"o{ei}mh", [P, 2, 8, P], F32))
                ytok = st1.enter_context(nc.sbuf_tensor(f"o{ei}ytok", [P, D_INNER], F32))
                yfh = st1.enter_context(nc.sbuf_tensor(f"o{ei}yfh", [P, DINC, 2, P], BF16))
                yfl = st1.enter_context(nc.sbuf_tensor(f"o{ei}yfl", [P, DINC, 2, P], BF16))
                sscl = st1.enter_context(nc.sbuf_tensor(f"o{ei}sscl", [P, 4], F32))
                xnew = st1.enter_context(nc.sbuf_tensor(f"o{ei}xnew", [P, DC, 2 * CH], F32))

                pb.op("gpsimd", (lambda e: e.memset(xbc[:, :, :, 0:3], 0.0)),
                      writes=(f"o{ei}xbc0", f"o{ei}xbc1"))
                zxv = zx.rearrange("(c p) t -> p c t", p=P)
                for c in range(NCH):
                    pp = c % 2
                    t0 = c * CH
                    sl = slice(t0, t0 + CH)
                    bxb = f"o{ei}xbc{pp}"
                    btm, bct, bbm, bxd = (f"o{ei}tsml", f"o{ei}cbt",
                                          f"o{ei}bmT", f"o{ei}xdtT0")
                    bdec = f"o{ei}decb"
                    if c == 0:
                        pb.op("sync", (lambda e, pp=pp: e.dma_start(
                            out=xbc[:, pp, :, 3:131], in_=zxv[:, 16:34, 0:CH])),
                            reads=("dramzx",), writes=(bxb,), dma=True)
                    else:
                        pb.op("sync", (lambda e, pp=pp, t0=t0: e.dma_start(
                            out=xbc[:, pp, :, 0:131], in_=zxv[:, 16:34, t0 - 3:t0 + CH])),
                            reads=("dramzx",), writes=(bxb,), dma=True)
                    pb.op("sync", (lambda e, sl=sl: e.dma_start(
                        out=zt[:, :, :], in_=zxv[:, 0:DINC, sl])),
                        reads=("dramzx",), writes=(f"o{ei}zt",), dma=True)
                    pb.op("sync", (lambda e, sl=sl: e.dma_start(
                        out=dtsl[:, :], in_=dtsfl[0:P, sl])),
                        reads=("dramdts",), writes=(f"o{ei}dtsl",), dma=True)
                    pb.op("sync", (lambda e, sl=sl: e.dma_start(
                        out=csbc[:, :, :],
                        in_=dtsfl[96:128, sl].partition_broadcast(P))),
                        reads=("dramdts",), writes=(f"o{ei}csbc",), dma=True)
                    pb.op("sync", (lambda e, c=c: e.dma_start(
                        out=decb[:, :],
                        in_=decfl[c, :].partition_broadcast(P))),
                        reads=("dramdec",), writes=(bdec,), dma=True)
                    xsrc = x1 if ei == 0 else x2
                    pb.op("sync", (lambda e, sl=sl, xs=xsrc, pp=pp: e.dma_start(
                        out=xnew[:, :, pp * CH:(pp + 1) * CH], in_=fmv(xs)[:, :, sl])),
                        reads=("dramx1" if ei == 0 else "dramx2",),
                        writes=(f"o{ei}xnew",), dma=True)

                    def tap0(e, pp=pp):
                        rep = cw[:, :, 0:1].broadcast_to((P, CVC, CH))
                        return e.tensor_tensor(out=cvo[:], in0=xbc[:, pp, :, 0:CH],
                                               in1=rep, op=Op.mult)
                    pb.op("vector", tap0, reads=(bxb, f"o{ei}cw"),
                          writes=(f"o{ei}cvo",))
                    for k in (1, 2, 3):
                        for ch0 in (0, 9):
                            def mulk(e, k=k, pp=pp, ch0=ch0):
                                rep = cw[:, ch0:ch0 + 9, k:k + 1] \
                                    .broadcast_to((P, 9, CH))
                                return e.tensor_tensor(
                                    out=cvt[:], in0=xbc[:, pp, ch0:ch0 + 9, k:k + CH],
                                    in1=rep, op=Op.mult)
                            pb.op("vector", mulk,
                                  reads=(bxb, f"o{ei}cw"), writes=(f"o{ei}cvt",))
                            pb.op("vector", (lambda e, ch0=ch0: e.tensor_tensor(
                                out=cvo[:, ch0:ch0 + 9], in0=cvo[:, ch0:ch0 + 9],
                                in1=cvt[:], op=Op.add)),
                                reads=(f"o{ei}cvt", f"o{ei}cvo"), writes=(f"o{ei}cvo",))
                    def addcb(e):
                        rep = cb[:, :].unsqueeze(-1).broadcast_to((P, CVC, CH))
                        return e.tensor_tensor(out=cvo[:], in0=cvo[:], in1=rep, op=Op.add)
                    pb.op("vector", addcb, reads=(f"o{ei}cvo", f"o{ei}cb"),
                          writes=(f"o{ei}cvo",))
                    pb.op("scalar", (lambda e: e.activation(
                        out=cvo[:], in_=cvo[:], func=AF.Silu)),
                        reads=(f"o{ei}cvo",), writes=(f"o{ei}cvo",))
                    pb.op("scalar", (lambda e: e.activation(
                        out=zt[:], in_=zt[:], func=AF.Silu)),
                        reads=(f"o{ei}zt",), writes=(f"o{ei}zt",))

                    pb.op("tensor", (lambda e: e.transpose(
                        ps[2][0:P, 0:128], dtsl[:, :], ident[:])),
                        reads=(f"o{ei}dtsl", "ident"), writes=("ps2",))
                    pb.op("scalar", (lambda e: e.activation(
                        out=tsml[:, :, :], in_=ps[2][0:P, 0:128].rearrange(
                            "p (a b) -> p a b", a=4), func=AF.Copy, bias=0.0)),
                        reads=("ps2",), writes=(btm,))
                    pb.op("tensor", (lambda e: e.transpose(
                        ps[2][0:P, 128:256], cvo[:, 16, :], ident[:])),
                        reads=(f"o{ei}cvo", "ident"), writes=("ps2",))
                    pb.op("scalar", (lambda e: e.activation(
                        out=bmT[:, :], in_=ps[2][0:P, 128:256],
                        func=AF.Copy, bias=0.0)),
                        reads=("ps2",), writes=(bbm,))
                    pb.op("tensor", (lambda e: e.matmul(
                        ps[2][0:P, 256:384], cvo[:, 16, :], cvo[:, 17, :],
                        start=True, stop=True)),
                        reads=(f"o{ei}cvo",), writes=("ps2",))
                    pb.op("scalar", (lambda e: e.activation(
                        out=cbt[:, :], in_=ps[2][0:P, 256:384],
                        func=AF.Copy, bias=0.0)),
                        reads=("ps2",), writes=(bct,))
                    for r4 in range(4):
                        bank = 6 + (r4 % 2)
                        def xtr(e, r4=r4, bank=bank):
                            for j in range(4):
                                cc = r4 * 4 + j
                                ins = e.transpose(ps[bank][0:P, j * 128:(j + 1) * 128],
                                                  cvo[:, cc, :], ident[:])
                            return ins
                        pb.op("tensor", xtr, reads=(f"o{ei}cvo", "ident"),
                              writes=(f"ps{bank}",))
                        def xev(e, r4=r4, pp=pp, bank=bank):
                            pss = ps[bank][0:P, 0:512].rearrange("p (a b) -> p a b", a=8)
                            rep = tsml[:, 0, 8 * r4:8 * r4 + 8].unsqueeze(-1) \
                                .broadcast_to((P, 8, 64))
                            e.tensor_tensor(out=xdtT[:, 0, 4 * r4:4 * r4 + 4].rearrange(
                                "p a (c b) -> p (a c) b", b=64), in0=pss, in1=rep,
                                op=Op.mult)
                            rep2 = tsml[:, 1, 8 * r4:8 * r4 + 8].unsqueeze(-1) \
                                .broadcast_to((P, 8, 64))
                            e.tensor_tensor(out=xqT[:, 4 * r4:4 * r4 + 4].rearrange(
                                "p a (c b) -> p (a c) b", b=64), in0=pss, in1=rep2,
                                op=Op.mult)
                            rep3 = dskb[:, 8 * r4:8 * r4 + 8].unsqueeze(-1) \
                                .broadcast_to((P, 8, 64))
                            return e.tensor_tensor(out=xskT[:, 4 * r4:4 * r4 + 4].rearrange(
                                "p a (c b) -> p (a c) b", b=64), in0=pss, in1=rep3,
                                op=Op.mult)
                        pb.op("vector", xev, reads=(f"ps{bank}", btm, f"o{ei}dsk"),
                              writes=(bxd, f"o{ei}xqT", f"o{ei}xskT"))
                    for r4 in range(4):
                        bank = 6 + (r4 % 2)
                        def ztr(e, r4=r4, bank=bank):
                            for j in range(4):
                                cc = r4 * 4 + j
                                ins = e.transpose(ps[bank][0:P, j * 128:(j + 1) * 128],
                                                  zt[:, cc, :], ident[:])
                            return ins
                        pb.op("tensor", ztr, reads=(f"o{ei}zt", "ident"),
                              writes=(f"ps{bank}",))
                        pb.op("scalar", (lambda e, r4=r4, bank=bank: e.activation(
                            out=szT[:, r4 * 512:(r4 + 1) * 512],
                            in_=ps[bank][0:P, 0:512],
                            func=AF.Copy, bias=0.0)),
                            reads=(f"ps{bank}",), writes=(f"o{ei}szT",))

                    for g in range(4):
                        pb.op("tensor", (lambda e, g=g: e.matmul(
                            ps[g][0:P, 0:512], cvo[:, 17, :],
                            state[:, g * 512:(g + 1) * 512], start=True, stop=True)),
                            reads=(f"o{ei}cvo", f"o{ei}st"), writes=(f"ps{g}",))
                        def yev(e, g=g, pp=pp):
                            rep = tsml[:, 2, 8 * g:8 * g + 8].unsqueeze(-1) \
                                .broadcast_to((P, 8, 64))
                            return e.tensor_tensor(
                                out=ytok[:, g * 512:(g + 1) * 512].rearrange(
                                    "p (a b) -> p a b", a=8),
                                in0=ps[g][0:P, 0:512].rearrange("p (a b) -> p a b", a=8),
                                in1=rep, op=Op.mult)
                        pb.op("vector", yev, reads=(f"ps{g}", btm),
                              writes=(f"o{ei}ytok",))
                    for g in range(4):
                        bank = 4 + (g % 2)
                        pb.op("tensor", (lambda e, g=g, bank=bank: e.matmul(
                            ps[bank][0:P, 0:512], bmT[:, :],
                            xqT[:, 4 * g:4 * g + 4, :].rearrange("p a b -> p (a b)"),
                            start=True, stop=True)),
                            reads=(bbm, f"o{ei}xqT"), writes=(f"ps{bank}",))
                        def srec1(e, g=g, pp=pp):
                            rep = decb[:, 8 * g:8 * g + 8].unsqueeze(-1) \
                                .broadcast_to((P, 8, 64))
                            return e.tensor_tensor(
                                out=state[:, g * 512:(g + 1) * 512]
                                .rearrange("p (a b) -> p a b", a=8),
                                in0=state[:, g * 512:(g + 1) * 512]
                                .rearrange("p (a b) -> p a b", a=8),
                                in1=rep, op=Op.mult)
                        pb.op("vector", srec1, reads=(bdec, f"o{ei}st"),
                              writes=(f"o{ei}st",))
                        pb.op("vector", (lambda e, g=g, bank=bank: e.tensor_tensor(
                            out=state[:, g * 512:(g + 1) * 512],
                            in0=state[:, g * 512:(g + 1) * 512],
                            in1=ps[bank][0:P, 0:512], op=Op.add)),
                            reads=(f"ps{bank}", f"o{ei}st"), writes=(f"o{ei}st",))
                    pb.op("vector", (lambda e: e.tensor_tensor(
                        out=ytok[:], in0=ytok[:],
                        in1=xskT[:, :, :].rearrange("p a b -> p (a b)"), op=Op.add)),
                        reads=(f"o{ei}xskT", f"o{ei}ytok"), writes=(f"o{ei}ytok",))

                    pb.op("vector", (lambda e: e.tensor_tensor(
                        out=csbc[:], in0=csbc[:],
                        in1=masktile[:, :].unsqueeze(1).broadcast_to((P, NH, CH)),
                        op=Op.add)),
                        reads=(f"o{ei}csbc", "masktile"), writes=(f"o{ei}csbc",))
                    for g in range(4):
                        gp = g % 2
                        bmh = f"o{ei}mh{gp}"
                        def msub(e, g=g, gp=gp):
                            rep = tsml[:, 3, 8 * g:8 * g + 8].unsqueeze(-1) \
                                .broadcast_to((P, 8, CH))
                            return e.tensor_tensor(out=mh[:, gp],
                                                   in0=csbc[:, 8 * g:8 * g + 8, :],
                                                   in1=rep, op=Op.subtract)
                        pb.op("vector", msub, reads=(f"o{ei}csbc", btm),
                              writes=(bmh,))
                        pb.op("scalar", (lambda e, gp=gp: e.activation(
                            out=mh[:, gp], in_=mh[:, gp], func=AF.Exp)),
                            reads=(bmh,), writes=(bmh,))
                        def mmul(e, gp=gp):
                            rep = cbt[:, :].unsqueeze(1).broadcast_to((P, 8, CH))
                            return e.tensor_tensor(out=mh[:, gp], in0=mh[:, gp],
                                                   in1=rep, op=Op.mult)
                        pb.op("vector", mmul, reads=(bmh, bct), writes=(bmh,))
                        def ydmm(e, g=g, gp=gp):
                            for hcol in range(8):
                                h = g * 8 + hcol
                                ins = e.matmul(
                                    ps[g][0:P, hcol * 64:(hcol + 1) * 64],
                                    mh[:, gp, hcol, :],
                                    xdtT[:, 0, h // 2, (h % 2) * 64:(h % 2) * 64 + 64],
                                    start=True, stop=True)
                            return ins
                        pb.op("tensor", ydmm, reads=(bmh, bxd), writes=(f"ps{g}",))
                    for g in range(4):
                        pb.op("vector", (lambda e, g=g: e.tensor_tensor(
                            out=ytok[:, g * 512:(g + 1) * 512],
                            in0=ytok[:, g * 512:(g + 1) * 512],
                            in1=ps[g][0:P, 0:512], op=Op.add)),
                            reads=(f"ps{g}", f"o{ei}ytok"), writes=(f"o{ei}ytok",))

                    pb.op("vector", (lambda e: e.tensor_tensor(
                        out=ytok[:], in0=ytok[:], in1=szT[:], op=Op.mult)),
                        reads=(f"o{ei}szT", f"o{ei}ytok"), writes=(f"o{ei}ytok",))
                    pb.op("scalar", (lambda e: e.activation(
                        out=xskT[:, :, :].rearrange("p a b -> p (a b)"), in_=ytok[:],
                        func=AF.Square)),
                        reads=(f"o{ei}ytok",), writes=(f"o{ei}xskT",))
                    pb.op("vector", (lambda e: e.tensor_reduce(
                        out=sscl[:, 0:1], in_=xskT[:, :, :].rearrange("p a b -> p (a b)"),
                        axis=AX.X, op=Op.add)),
                        reads=(f"o{ei}xskT",), writes=(f"o{ei}sscl",))
                    pb.op("scalar", (lambda e: e.activation(
                        out=sscl[:, 1:2], in_=sscl[:, 0:1], func=AF.Ln,
                        scale=1.0 / D_INNER, bias=epscol[0:P, 0:1])),
                        reads=(f"o{ei}sscl", "epscol"), writes=(f"o{ei}sscl",))
                    pb.op("scalar", (lambda e: e.activation(
                        out=sscl[:, 2:3], in_=sscl[:, 1:2], func=AF.Exp, scale=-0.5)),
                        reads=(f"o{ei}sscl",), writes=(f"o{ei}sscl",))
                    pb.op("vector", (lambda e, c=c: e.tensor_tensor(
                        out=sscl[:, 3:4], in0=sscl[:, 2:3],
                        in1=r_sb[:, c * E + ei:c * E + ei + 1], op=Op.mult)),
                        reads=(f"o{ei}sscl", "r_sb"), writes=(f"o{ei}sscl",))
                    pb.op("vector", (lambda e: e.tensor_scalar(
                        out=ytok[:], in0=ytok[:], scalar1=sscl[:, 3:4], scalar2=None,
                        op0=Op.mult, op1=Op.bypass)),
                        reads=(f"o{ei}ytok", f"o{ei}sscl"), writes=(f"o{ei}ytok",))
                    for r4 in range(4):
                        bank = 6 + (r4 % 2)
                        def ytr(e, r4=r4, bank=bank):
                            for j in range(4):
                                cc = r4 * 4 + j
                                ins = e.transpose(ps[bank][0:P, j * 128:(j + 1) * 128],
                                                  ytok[:, cc * 128:(cc + 1) * 128],
                                                  ident[:])
                            return ins
                        pb.op("tensor", ytr, reads=(f"o{ei}ytok", "ident"),
                              writes=(f"ps{bank}",))
                        pb.op("scalar", (lambda e, r4=r4, bank=bank, pp=pp: e.activation(
                            out=yfh[:, 4 * r4:4 * r4 + 4, pp, :],
                            in_=ps[bank][0:P, 0:512].rearrange("p (a b) -> p a b", a=4),
                            func=AF.Copy, bias=0.0)),
                            reads=(f"ps{bank}",), writes=(f"o{ei}yfh{pp}",))
                        pb.op("vector", (lambda e, r4=r4, bank=bank, pp=pp: e.tensor_tensor(
                            out=yfl[:, 4 * r4:4 * r4 + 4, pp, :],
                            in0=ps[bank][0:P, 0:512].rearrange("p (a b) -> p a b", a=4),
                            in1=yfh[:, 4 * r4:4 * r4 + 4, pp, :], op=Op.subtract)),
                            reads=(f"ps{bank}", f"o{ei}yfh{pp}"), writes=(f"o{ei}yfl{pp}",))
                    if pp == 1:
                        for oc in range(DC):
                            bank = 4 + (oc % 4) // 2
                            slot = oc % 2
                            def omm(e, oc=oc, bank=bank, slot=slot):
                                out = ps[bank][0:P, slot * 256:slot * 256 + 256]
                                mvh = yfh.rearrange("p a b c -> p a (b c)")
                                mvl = yfl.rearrange("p a b c -> p a (b c)")
                                for dic in range(DINC):
                                    e.matmul(out, owhi[:, dic, oc * 128:(oc + 1) * 128],
                                             mvh[:, dic, :], start=(dic == 0), stop=False)
                                    e.matmul(out, owhi[:, dic, oc * 128:(oc + 1) * 128],
                                             mvl[:, dic, :], start=False, stop=False)
                                    ins = e.matmul(out,
                                                   owlo[:, dic, oc * 128:(oc + 1) * 128],
                                                   mvh[:, dic, :], start=False,
                                                   stop=(dic == DINC - 1))
                                return ins
                            pb.op("tensor", omm,
                                  reads=(f"o{ei}owhi", f"o{ei}owlo", f"o{ei}yfh0",
                                         f"o{ei}yfh1", f"o{ei}yfl0", f"o{ei}yfl1"),
                                  writes=(f"ps{bank}",))
                            pb.op("vector", (lambda e, oc=oc, bank=bank, slot=slot:
                                             e.tensor_tensor(
                                                 out=xnew[:, oc, :],
                                                 in0=xnew[:, oc, :],
                                                 in1=ps[bank][0:P, slot * 256:
                                                              slot * 256 + 256],
                                                 op=Op.add)),
                                  reads=(f"ps{bank}", f"o{ei}xnew"),
                                  writes=(f"o{ei}xnew",))
                        for hf in range(2):
                            pb.op("gpsimd", (lambda e, t0=t0, hf=hf: e.dma_start(
                                out=fmv(x2)[:, :, t0 - CH + hf * CH:t0 + hf * CH],
                                in_=xnew[:, :, hf * CH:(hf + 1) * CH])),
                                reads=(f"o{ei}xnew",), writes=("dramx2",), dma=True)

                pb.barrier()

        if "F1" in STAGES:
            ffn_call("f1", xT, x1, 0.5, False)
        else:
            pb.op("sync", (lambda e: e.dma_start(out=x1[:, :], in_=xT[:, :])), dma=True)
            pb.barrier()

        src_final = x1
        if "M0" in STAGES:
            m0()
        if "M1" in STAGES:
            for ei in range(E):
                m1(ei)
                if "M2" in STAGES:
                    m2(ei)
            src_final = x2

        if "F2" in STAGES:
            ffn_call("f2", src_final, out, 0.5, True)
        else:
            pb.op("sync", (lambda e, s=src_final: e.dma_start(out=out[:, :], in_=s[:, :])),
                  dma=True, force_mark=True)
            pb.barrier()

        pb.resolve()
        semctx = {s: stack.enter_context(nc.semaphore(s)) for s in SEMS}
        with nc.Block() as block:
            @block.sync
            def _(e):
                pb.emit("sync", e, semctx)

            @block.tensor
            def _(e):
                pb.emit("tensor", e, semctx)

            @block.vector
            def _(e):
                pb.emit("vector", e, semctx)

            @block.scalar
            def _(e):
                pb.emit("scalar", e, semctx)

            @block.gpsimd
            def _(e):
                pb.emit("gpsimd", e, semctx)
                for s in SEMS:
                    if s.startswith("s_l") or s.startswith("s_g"):
                        if pb.final.get(s, 0) > 0:
                            e.wait_ge(semctx[s], pb.final[s])
    return nc


def kernel(L0_attn_x, L0_attn_router_w, L0_attn_in_w, L0_attn_conv_w,
           L0_attn_conv_b, L0_attn_dt_bias, L0_attn_A_log, L0_attn_D_skip,
           L0_attn_norm_w, L0_attn_out_w, L0_ffn_w1, L0_ffn_conv_w,
           L0_ffn_conv_b, L0_ffn_w2):
    x = np.asarray(L0_attn_x)
    shared = {
        "w1T": np.ascontiguousarray(np.asarray(L0_ffn_w1).T),
        "w2T": np.ascontiguousarray(np.asarray(L0_ffn_w2).T),
        "cwf": np.ascontiguousarray(np.asarray(L0_ffn_conv_w)[:, 0, :]),
        "cbf": np.ascontiguousarray(np.asarray(L0_ffn_conv_b)),
        "rwT": np.ascontiguousarray(np.asarray(L0_attn_router_w)),
        "inw": np.ascontiguousarray(np.asarray(L0_attn_in_w)),
        "cwm": np.ascontiguousarray(np.asarray(L0_attn_conv_w)),
        "cbm": np.ascontiguousarray(np.asarray(L0_attn_conv_b)),
        "dtb": np.ascontiguousarray(np.asarray(L0_attn_dt_bias)),
        "alog": np.ascontiguousarray(np.asarray(L0_attn_A_log)),
        "dskp": np.ascontiguousarray(np.asarray(L0_attn_D_skip)),
        "nrmw": np.ascontiguousarray(np.asarray(L0_attn_norm_w)),
        "outw": np.ascontiguousarray(np.asarray(L0_attn_out_w)),
    }
    nc = build_nc()
    in_maps = []
    for b in range(8):
        m = dict(shared)
        m["xT"] = np.ascontiguousarray(x[b].T)
        in_maps.append(m)
    res = run_bass_kernel_spmd(nc, in_maps, core_ids=list(range(8)))
    outs = [np.ascontiguousarray(r["out"].T) for r in res.results]
    return np.stack(outs, axis=0)


# version 18
# speedup vs baseline: 1.2169x; 1.0117x over previous
import contextlib
import numpy as np
import ml_dtypes
import concourse.bass as bass
import concourse.mybir as mybir
from concourse.bass_utils import run_bass_kernel_spmd
from concourse.alu_op_type import AluOpType as Op

F32 = mybir.dt.float32
BF16 = mybir.dt.bfloat16
F16 = mybir.dt.float16
F8 = mybir.dt.float8e4
AF = mybir.ActivationFunctionType
AX = mybir.AxisListType

P = 128
T = 4096
D_MODEL = 1024
DC = 8
H_FFN = 4096
HC = 32
E = 4
D_INNER = 2048
DINC = 16
NH = 32
HD = 64
CONV_DIM = 2304
CVC = 18
D_IN_PROJ = 4384
NST = 128
CH = 128
NCH = T // CH
FPH = 256
NFPH = T // FPH
MG = 1536.0
NEG = -1e30

ENGINES = ("sync", "tensor", "vector", "scalar", "gpsimd")
SEM_OF = {"tensor": "s_te", "vector": "s_ve", "scalar": "s_ac", "gpsimd": "s_gp"}
DMA_POOL = {"sync": ("s_ld0", "s_ld1", "s_ld2", "s_ld3"),
            "gpsimd": ("s_gd0", "s_gd1")}
SEMS = ("s_ld0", "s_ld1", "s_ld2", "s_ld3", "s_gd0", "s_gd1",
        "s_te", "s_ve", "s_ac", "s_gp")


class Mark:
    __slots__ = ("sem", "seq", "val", "used", "step", "vc")

    def __init__(self, sem, seq, step, vc):
        self.sem, self.seq, self.step, self.vc = sem, seq, step, vc
        self.val = None
        self.used = False


class _NoMark:
    used = False
    val = None


_NOMARK = _NoMark()


class PB:

    def __init__(self):
        self.prog = []
        self.seq = {s: 0 for s in SEMS}
        self.vc = {e: {} for e in ENGINES}
        self.bufs = {}
        self._marks = {s: [] for s in SEMS}
        self.dmacnt = {q: 0 for q in DMA_POOL}

    def _buf(self, name):
        b = self.bufs.get(name)
        if b is None:
            b = {"w": None, "r": []}
            self.bufs[name] = b
        return b

    def op(self, eng, fn, reads=(), writes=(), dma=False, force_mark=False):
        need = {}
        if dma:
            qsem = DMA_POOL[eng][self.dmacnt[eng] % len(DMA_POOL[eng])]
            self.dmacnt[eng] += 1
            if self.seq[qsem] > 0:
                need[qsem] = self.seq[qsem] - 1
        for r in reads:
            b = self._buf(r)
            if b["w"] is not None:
                m = b["w"]
                if need.get(m.sem, -1) < m.seq:
                    need[m.sem] = m.seq
        for w in writes:
            b = self._buf(w)
            if b["w"] is not None:
                m = b["w"]
                if need.get(m.sem, -1) < m.seq:
                    need[m.sem] = m.seq
            for m in b["r"]:
                if need.get(m.sem, -1) < m.seq:
                    need[m.sem] = m.seq
        waits = []
        vc = self.vc[eng]
        for sem, seqv in need.items():
            if vc.get(sem, -1) >= seqv:
                continue
            m = self._marks[sem][seqv]
            m.used = True
            waits.append(m)
            for s2, q2 in m.vc.items():
                if vc.get(s2, -1) < q2:
                    vc[s2] = q2
            vc[sem] = max(vc.get(sem, -1), seqv)
        sem = qsem if dma else SEM_OF[eng]
        myvc = dict(vc)
        mark = Mark(sem, self.seq[sem], 16 if dma else 1, myvc)
        myvc[sem] = max(myvc.get(sem, -1), mark.seq)
        if force_mark or dma:
            mark.used = True
        self._marks[sem].append(mark)
        self.seq[sem] += 1
        for w in writes:
            b = self._buf(w)
            b["w"] = mark
            b["r"] = []
        for r in reads:
            self._buf(r)["r"].append(mark)
        self.prog.append((eng, fn, mark, waits))
        return mark

    def barrier(self):
        for eng in ENGINES:
            waits = []
            vc = self.vc[eng]
            for sem in SEMS:
                seqv = self.seq[sem] - 1
                if seqv < 0 or vc.get(sem, -1) >= seqv:
                    continue
                m = self._marks[sem][seqv]
                m.used = True
                waits.append(m)
                for s2, q2 in m.vc.items():
                    if vc.get(s2, -1) < q2:
                        vc[s2] = q2
                vc[sem] = seqv
            if waits:
                self.prog.append((eng, None, _NOMARK, waits))

    def resolve(self):
        self.final = {}
        for sem, marks in self._marks.items():
            c = 0
            for m in marks:
                if m.used:
                    c += m.step
                    m.val = c
            self.final[sem] = c

    def emit(self, eng_name, e, sems):
        for eng, fn, mark, waits in self.prog:
            if eng != eng_name:
                continue
            for m in waits:
                e.wait_ge(sems[m.sem], m.val)
            if fn is None:
                continue
            ins = fn(e)
            if mark.used:
                assert ins is not None, f"op on {eng} needs a sync-bearing instr"
                ins.then_inc(sems[mark.sem], mark.step)


STAGES = ("F1", "M0", "M1", "M2", "F2")
DEBUG_FFN = False


def set_T(t):
    global T, NCH, NFPH
    T = t
    NCH = T // CH
    NFPH = T // FPH


def build_nc():
    nc = bass.Bass("TRN2", target_bir_lowering=False)
    pb = PB()

    xT = nc.dram_tensor("xT", [D_MODEL, T], F32, kind="ExternalInput").ap()
    w1T = nc.dram_tensor("w1T", [D_MODEL, H_FFN], F32, kind="ExternalInput").ap()
    w2T = nc.dram_tensor("w2T", [H_FFN, D_MODEL], F32, kind="ExternalInput").ap()
    cwf = nc.dram_tensor("cwf", [H_FFN, 3], F32, kind="ExternalInput").ap()
    cbf = nc.dram_tensor("cbf", [H_FFN], F32, kind="ExternalInput").ap()
    rwT = nc.dram_tensor("rwT", [D_MODEL, E], F32, kind="ExternalInput").ap()
    inw = nc.dram_tensor("inw", [E, D_MODEL, D_IN_PROJ], F32, kind="ExternalInput").ap()
    cwm = nc.dram_tensor("cwm", [E, CONV_DIM, 4], F32, kind="ExternalInput").ap()
    cbm = nc.dram_tensor("cbm", [E, CONV_DIM], F32, kind="ExternalInput").ap()
    dtb = nc.dram_tensor("dtb", [E, NH], F32, kind="ExternalInput").ap()
    alog = nc.dram_tensor("alog", [E, NH], F32, kind="ExternalInput").ap()
    dskp = nc.dram_tensor("dskp", [E, NH], F32, kind="ExternalInput").ap()
    nrmw = nc.dram_tensor("nrmw", [E, D_INNER], F32, kind="ExternalInput").ap()
    outw = nc.dram_tensor("outw", [E, D_INNER, D_MODEL], F32, kind="ExternalInput").ap()
    out = nc.dram_tensor("out", [D_MODEL, T], F32, kind="ExternalOutput").ap()
    if DEBUG_FFN:
        dxq = nc.dram_tensor("dxq", [P, DC, 258], F32, kind="ExternalOutput").ap()
        dcs = nc.dram_tensor("dcs", [P, HC, 256], F32, kind="ExternalOutput").ap()
        dhq = nc.dram_tensor("dhq", [P, HC, 256], F32, kind="ExternalOutput").ap()
        dhh = nc.dram_tensor("dhh", [P, 4, 258], F32, kind="ExternalOutput").ap()
        drw = nc.dram_tensor("drw", [8, T], F32, kind="ExternalOutput").ap()
        dsw = nc.dram_tensor("dsw", [1, 4], F32, kind="ExternalOutput").ap()

    x1 = nc.dram_tensor("x1", [D_MODEL, T], F32, kind="Internal").ap()
    x2 = nc.dram_tensor("x2", [D_MODEL, T], F32, kind="Internal").ap()
    ln2hi = nc.dram_tensor("ln2hi", [D_MODEL, T], BF16, kind="Internal").ap()
    ln2lo = nc.dram_tensor("ln2lo", [D_MODEL, T], BF16, kind="Internal").ap()
    zx = nc.dram_tensor("zx", [35 * P, T], F32, kind="Internal").ap()
    dtsfl = nc.dram_tensor("dtsfl", [P, T], F32, kind="Internal").ap()
    decfl = nc.dram_tensor("decfl", [NCH, NH], F32, kind="Internal").ap()
    rowfl = nc.dram_tensor("rowfl", [8, T], F32, kind="Internal").ap()

    def fmv(ap):
        return ap.rearrange("(c p) t -> p c t", p=P)

    stack = contextlib.ExitStack()
    with stack:
        ident = stack.enter_context(nc.sbuf_tensor("ident", [P, P], F32))
        masktile = stack.enter_context(nc.sbuf_tensor("masktile", [P, P], F32))
        onescol = stack.enter_context(nc.sbuf_tensor("onescol", [P, 1], F32))
        onesrow = stack.enter_context(nc.sbuf_tensor("onesrow", [1, P], F32))
        r_sb = stack.enter_context(nc.sbuf_tensor("r_sb", [P, NCH * E], F32))
        ps = [stack.enter_context(nc.psum_tensor(f"ps{i}", [P, 512], F32))
              for i in range(8)]

        pb.op("gpsimd", lambda e: e.memset(ident[:], 0.0), writes=("ident",))
        pb.op("gpsimd", (lambda e: e.affine_select(
            out=ident[:], in_=ident[:], compare_op=Op.not_equal, fill=1.0,
            base=0, pattern=[[-1, P]], channel_multiplier=1)),
            reads=("ident",), writes=("ident",))
        pb.op("gpsimd", lambda e: e.memset(masktile[:], 0.0), writes=("masktile",))
        pb.op("gpsimd", (lambda e: e.affine_select(
            out=masktile[:], in_=masktile[:], compare_op=Op.is_ge, fill=NEG,
            base=0, pattern=[[1, P]], channel_multiplier=-1)),
            reads=("masktile",), writes=("masktile",))
        pb.op("gpsimd", lambda e: e.memset(onescol[:], 1.0), writes=("onescol",))
        pb.op("gpsimd", lambda e: e.memset(onesrow[:], 1.0), writes=("onesrow",))
        epscol = stack.enter_context(nc.sbuf_tensor("epscol", [P, 1], F32))
        pb.op("gpsimd", lambda e: e.memset(epscol[:], 1e-5), writes=("epscol",))

        def bcast(tag2, rowap, n, bank, reads):
            pb.op("tensor", (lambda e, r=rowap, n=n, bank=bank: e.matmul(
                ps[bank][0:P, 0:n], onesrow[:], r, start=True, stop=True)),
                reads=tuple(reads) + ("onesrow",), writes=(f"ps{bank}",))

        trow = stack.enter_context(nc.sbuf_tensor("trow", [8, P], F32))

        def amax_row(tag2, absbuf, absname, nch, n, rowslice, rowname, psA, psB):
            w = nch // 2
            while w >= 1:
                pb.op("vector", (lambda e, w=w: e.tensor_tensor(
                    out=absbuf[:, 0:w, 0:n], in0=absbuf[:, 0:w, 0:n],
                    in1=absbuf[:, w:2 * w, 0:n], op=Op.max)),
                    reads=(absname,), writes=(absname,))
                w //= 2
            ng = n // P
            def trs(e):
                for g in range(ng):
                    ins = e.transpose(ps[psA][0:P, g * P:(g + 1) * P],
                                      absbuf[:, 0, g * P:(g + 1) * P], ident[:])
                return ins
            pb.op("tensor", trs, reads=(absname, "ident"), writes=(f"ps{psA}",))
            def reds(e):
                for g in range(ng):
                    ins = e.tensor_reduce(out=absbuf[0:P, 0, g:g + 1],
                                          in_=ps[psA][0:P, g * P:(g + 1) * P],
                                          axis=AX.X, op=Op.max)
                return ins
            pb.op("vector", reds, reads=(f"ps{psA}",), writes=(absname,))
            pb.op("tensor", (lambda e: e.transpose(ps[psB][0:ng, 0:P],
                                                   absbuf[:, 0, 0:ng], ident[:])),
                  reads=(absname, "ident"), writes=(f"ps{psB}",))
            pb.op("vector", (lambda e: e.tensor_copy(trow[0:ng, :], ps[psB][0:ng, 0:P])),
                  reads=(f"ps{psB}",), writes=("trow",))
            pb.op("gpsimd", (lambda e: e.dma_start(out=rowslice, in_=trow[0:ng, :])),
                  reads=("trow",), writes=(rowname,), dma=True)

        def ffn_call(tag, src, dst, half, is_out):
            rowsd = f"{tag}rowsd"
            with contextlib.ExitStack() as st1:
                w1q = st1.enter_context(nc.sbuf_tensor(f"{tag}w1q", [P, DC, H_FFN], F8))
                w2q = st1.enter_context(nc.sbuf_tensor(f"{tag}w2q", [P, HC, D_MODEL], F8))
                cw = st1.enter_context(nc.sbuf_tensor(f"{tag}cw", [P, HC, 3], F32))
                cb = st1.enter_context(nc.sbuf_tensor(f"{tag}cb", [P, HC], F32))
                swr = st1.enter_context(nc.sbuf_tensor(f"{tag}swr", [1, 4], F32))

                with contextlib.ExitStack() as st2:
                    wbuf = st2.enter_context(
                        nc.sbuf_tensor(f"{tag}wbuf", [P, 2, 4096], F32))
                    wtmp = st2.enter_context(nc.sbuf_tensor(f"{tag}wtmp", [1 * P, 4096], F16))
                    wsc = st2.enter_context(nc.sbuf_tensor(f"{tag}wsc", [P, 4], F32))
                    for wi, (wsrc, nchunk, wid) in enumerate(
                            ((w1T, DC, H_FFN), (w2T, HC, D_MODEL))):
                        wn = f"{tag}w1q" if wi == 0 else f"{tag}w2q"
                        wdst = w1q if wi == 0 else w2q
                        view = wsrc.rearrange("(c p) w -> p c w", p=P)
                        npc = 4096 // wid
                        npieces = nchunk // npc
                        for pc in range(npieces):
                            pp = pc % 2
                            bw = f"{tag}wbuf{pp}"
                            pb.op("sync", (lambda e, pc=pc, pp=pp, v=view, npc=npc, w=wid:
                                           e.dma_start(out=wbuf[:, pp].rearrange(
                                               "p (a b) -> p a b", a=npc),
                                               in_=v[:, pc * npc:(pc + 1) * npc, :])),
                                  writes=(bw,), dma=True)
                            if pc == 0:
                                pb.op("vector", (lambda e, pp=pp: e.tensor_reduce(
                                    out=wsc[:, 0:1], in_=wbuf[:, pp], axis=AX.X,
                                    op=Op.add, apply_absolute_value=True)),
                                    reads=(bw,), writes=(f"{tag}wsc",))
                            else:
                                pb.op("vector", (lambda e, pp=pp: e.tensor_reduce(
                                    out=wsc[:, 1:2], in_=wbuf[:, pp], axis=AX.X,
                                    op=Op.add, apply_absolute_value=True)),
                                    reads=(bw,), writes=(f"{tag}wsc",))
                                pb.op("vector", (lambda e: e.tensor_tensor(
                                    out=wsc[:, 0:1], in0=wsc[:, 0:1], in1=wsc[:, 1:2],
                                    op=Op.add)), reads=(f"{tag}wsc",), writes=(f"{tag}wsc",))
                        pb.op("tensor", (lambda e: e.matmul(
                            ps[7][0:1, 0:1], wsc[:, 0:1], onescol[:], start=True, stop=True)),
                            reads=(f"{tag}wsc", "onescol"), writes=("ps7",))
                        pb.op("vector", (lambda e, wi=wi, n=nchunk, w=wid: e.tensor_scalar(
                            out=swr[0:1, wi + 2:wi + 3], in0=ps[7][0:1, 0:1],
                            scalar1=1.0 / (n * P * w), scalar2=1e-5,
                            op0=Op.mult, op1=Op.max)),
                            reads=("ps7",), writes=(f"{tag}swr",))
                        pb.op("vector", (lambda e, wi=wi: e.reciprocal(
                            out=swr[0:1, wi:wi + 1], in_=swr[0:1, wi + 2:wi + 3])),
                            reads=(f"{tag}swr",), writes=(f"{tag}swr",))
                        pb.op("tensor", (lambda e, wi=wi: e.matmul(
                            ps[7][0:P, 1:2], onesrow[:], swr[0:1, wi:wi + 1],
                            start=True, stop=True)),
                            reads=(f"{tag}swr", "onesrow"), writes=("ps7",))
                        pb.op("vector", (lambda e: e.tensor_copy(
                            wsc[:, 3:4], ps[7][0:P, 1:2])),
                            reads=("ps7",), writes=(f"{tag}wsc",))
                        for pc in range(npieces):
                            pp = pc % 2
                            bw = f"{tag}wbuf{pp}"
                            pb.op("sync", (lambda e, pc=pc, pp=pp, v=view, npc=npc, w=wid:
                                           e.dma_start(out=wbuf[:, pp].rearrange(
                                               "p (a b) -> p a b", a=npc),
                                               in_=v[:, pc * npc:(pc + 1) * npc, :])),
                                  writes=(bw,), dma=True)
                            pb.op("vector", (lambda e, pp=pp: e.tensor_scalar(
                                out=wbuf[:, pp], in0=wbuf[:, pp], scalar1=wsc[:, 3:4],
                                scalar2=1.4999, op0=Op.mult, op1=Op.min)),
                                reads=(bw, f"{tag}wsc"), writes=(bw,))
                            pb.op("vector", (lambda e, pp=pp: e.tensor_scalar(
                                out=wtmp[:, :], in0=wbuf[:, pp], scalar1=-1.4999,
                                scalar2=MG, op0=Op.max, op1=Op.add)),
                                reads=(bw,), writes=(f"{tag}wtmp",))
                            pb.op("scalar", (lambda e, wd=wdst, pc=pc, npc=npc, w=wid:
                                             e.activation(out=wd[:, pc * npc:(pc + 1) * npc, :]
                                                          .rearrange("p a b -> p (a b)"),
                                                          in_=wtmp[:, :], func=AF.Copy,
                                                          bias=-MG)),
                                  reads=(f"{tag}wtmp",), writes=(wn,))
                    pb.barrier()

                pb.op("sync", (lambda e: e.dma_start(
                    out=cw[:, :, :], in_=cwf.rearrange("(c p) k -> p c k", p=P))),
                    writes=(f"{tag}cw",), dma=True)
                pb.op("sync", (lambda e: e.dma_start(
                    out=cb[:, :], in_=cbf.rearrange("(c p) -> p c", p=P),
                    allow_slow_non_contiguous=True)),
                    writes=(f"{tag}cb",), dma=True)

                with contextlib.ExitStack() as st2:
                    xp = st2.enter_context(nc.sbuf_tensor(f"{tag}xp", [P, 2, DC, 512], F32))
                    sq = st2.enter_context(nc.sbuf_tensor(f"{tag}sq", [P, DC, 512], F32))
                    rt = st2.enter_context(nc.sbuf_tensor(f"{tag}rt", [1, 8, 512], F32))
                    mbc = st2.enter_context(nc.sbuf_tensor(f"{tag}mbc", [P, 512], F32))
                    for i in range(T // 512):
                        pp = i % 2
                        bx = f"{tag}xp{pp}"
                        sl = slice(i * 512, (i + 1) * 512)
                        pb.op("sync", (lambda e, sl=sl, pp=pp: e.dma_start(
                            out=xp[:, pp], in_=fmv(src)[:, :, sl])),
                            writes=(bx,), dma=True)
                        pb.op("scalar", (lambda e, pp=pp: e.activation(
                            out=sq[:], in_=xp[:, pp], func=AF.Square)),
                            reads=(bx,), writes=(f"{tag}sq",))
                        def mmsum(e, pp=pp):
                            for dc in range(DC):
                                ins = e.matmul(ps[6][0:1, 0:512], onescol[:],
                                               xp[:, pp, dc], start=(dc == 0),
                                               stop=(dc == DC - 1))
                            return ins
                        pb.op("tensor", mmsum, reads=(bx, "onescol"), writes=("ps6",))
                        def mmsq(e):
                            for dc in range(DC):
                                ins = e.matmul(ps[7][0:1, 0:512], onescol[:],
                                               sq[:, dc], start=(dc == 0),
                                               stop=(dc == DC - 1))
                            return ins
                        pb.op("tensor", mmsq, reads=(f"{tag}sq", "onescol"), writes=("ps7",))
                        pb.op("vector", (lambda e: e.tensor_scalar(
                            out=rt[0:1, 4], in0=ps[6][0:1, 0:512],
                            scalar1=1.0 / D_MODEL, scalar2=None, op0=Op.mult, op1=Op.bypass)),
                            reads=("ps6",), writes=(f"{tag}rt",))
                        pb.op("vector", (lambda e: e.tensor_tensor(
                            out=rt[0:1, 6], in0=rt[0:1, 4], in1=rt[0:1, 4], op=Op.mult)),
                            reads=(f"{tag}rt",), writes=(f"{tag}rt",))
                        pb.op("vector", (lambda e: e.scalar_tensor_tensor(
                            out=rt[0:1, 5], in0=ps[7][0:1, 0:512], scalar=1.0 / D_MODEL,
                            in1=rt[0:1, 6], op0=Op.mult, op1=Op.subtract)),
                            reads=("ps7", f"{tag}rt"), writes=(f"{tag}rt",))
                        pb.op("scalar", (lambda e: e.activation(
                            out=rt[0:1, 6], in_=rt[0:1, 5], func=AF.Ln,
                            bias=epscol[0:1, 0:1])),
                            reads=(f"{tag}rt", "epscol"), writes=(f"{tag}rt",))
                        pb.op("scalar", (lambda e: e.activation(
                            out=rt[0:1, 0], in_=rt[0:1, 6], func=AF.Exp, scale=-0.5)),
                            reads=(f"{tag}rt",), writes=(f"{tag}rt",))
                        pb.op("vector", (lambda e: e.scalar_tensor_tensor(
                            out=rt[0:1, 1], in0=rt[0:1, 4], scalar=-1.0,
                            in1=rt[0:1, 0], op0=Op.mult, op1=Op.mult)),
                            reads=(f"{tag}rt",), writes=(f"{tag}rt",))
                        bcast(tag, rt[0:1, 4], 512, 6, (f"{tag}rt",))
                        pb.op("vector", (lambda e: e.tensor_copy(mbc[:], ps[6][0:P, 0:512])),
                              reads=("ps6",), writes=(f"{tag}mbc",))
                        def subm(e, pp=pp):
                            rep = mbc[:, :].unsqueeze(1).broadcast_to((P, DC, 512))
                            return e.tensor_tensor(out=sq[:], in0=xp[:, pp], in1=rep,
                                                   op=Op.subtract)
                        pb.op("vector", subm, reads=(bx, f"{tag}mbc"), writes=(f"{tag}sq",))
                        pb.op("scalar", (lambda e: e.activation(out=sq[:], in_=sq[:],
                                                                func=AF.Abs)),
                              reads=(f"{tag}sq",), writes=(f"{tag}sq",))
                        amax_row(tag, sq, f"{tag}sq", DC, 512, rt[0:1, 7], f"{tag}rt", 6, 7)
                        pb.op("vector", (lambda e: e.scalar_tensor_tensor(
                            out=rt[0:1, 6], in0=rt[0:1, 7], scalar=1e-30,
                            in1=rt[0:1, 0], op0=Op.max, op1=Op.mult)),
                            reads=(f"{tag}rt",), writes=(f"{tag}rt",))
                        pb.op("vector", (lambda e: e.tensor_scalar(
                            out=rt[0:1, 6], in0=rt[0:1, 6], scalar1=1e-5,
                            scalar2=None, op0=Op.max, op1=Op.bypass)),
                            reads=(f"{tag}rt",), writes=(f"{tag}rt",))
                        pb.op("vector", (lambda e: e.reciprocal(
                            out=rt[0:1, 7], in_=rt[0:1, 6])),
                            reads=(f"{tag}rt",), writes=(f"{tag}rt",))
                        pb.op("vector", (lambda e: e.tensor_scalar(
                            out=rt[0:1, 2], in0=rt[0:1, 7], scalar1=127.0, scalar2=None,
                            op0=Op.mult, op1=Op.bypass)),
                            reads=(f"{tag}rt",), writes=(f"{tag}rt",))
                        pb.op("vector", (lambda e: e.tensor_scalar(
                            out=rt[0:1, 3], in0=rt[0:1, 6], scalar1=1.0 / 127.0,
                            scalar2=None, op0=Op.mult, op1=Op.bypass)),
                            reads=(f"{tag}rt",), writes=(f"{tag}rt",))
                        pb.op("gpsimd", (lambda e, sl=sl: e.dma_start(
                            out=rowfl[0:4, sl], in_=rt[0:1, 0:4, :])),
                            reads=(f"{tag}rt",), writes=(rowsd,), dma=True)
                    pb.barrier()

                with contextlib.ExitStack() as st2:
                    x0t = st2.enter_context(nc.sbuf_tensor(f"{tag}x0t", [P, 2, DC, 258], F32))
                    xoln = st2.enter_context(nc.sbuf_tensor(f"{tag}xoln", [P, DC, 258], F32))
                    q16 = st2.enter_context(nc.sbuf_tensor(f"{tag}q16", [P, DC, 258], F16))
                    xq = st2.enter_context(nc.sbuf_tensor(f"{tag}xq", [P, DC, 258], BF16))
                    hsb = st2.enter_context(nc.sbuf_tensor(f"{tag}hsb", [P, 2, 258], F32))
                    csb = st2.enter_context(nc.sbuf_tensor(f"{tag}csb", [P, HC, 256], F32))
                    h16 = st2.enter_context(nc.sbuf_tensor(f"{tag}h16", [P, HC, 256], F16))
                    amxb = st2.enter_context(nc.sbuf_tensor(f"{tag}amxb", [P, HC, 256], F32))
                    hq = st2.enter_context(nc.sbuf_tensor(f"{tag}hq", [P, HC, 256], BF16))
                    sbc = st2.enter_context(nc.sbuf_tensor(f"{tag}sbc", [P, 3, 258], F32))
                    rw = st2.enter_context(nc.sbuf_tensor(f"{tag}rw", [1, 4, 258], F32))
                    rr = st2.enter_context(nc.sbuf_tensor(f"{tag}rr", [1, 2, 258], F32))

                    pb.op("gpsimd", (lambda e: e.memset(x0t[:], 0.0)),
                          writes=(f"{tag}x0t0", f"{tag}x0t1"))
                    pb.op("gpsimd", (lambda e: e.memset(rr[:], 0.0)),
                          writes=(f"{tag}rr",))
                    pb.op("gpsimd", (lambda e: e.memset(rw[:], 0.0)),
                          writes=(f"{tag}rw",))

                    for i in range(NFPH):
                        t0 = i * FPH
                        w0 = t0 - 1
                        lo, hi = max(w0, 0), min(t0 + FPH + 1, T)
                        off = lo - w0
                        pp = i % 2
                        bx = f"{tag}x0t{pp}"
                        pb.op("sync", (lambda e, lo=lo, hi=hi, off=off, pp=pp:
                                       e.dma_start(out=x0t[:, pp, :, off:off + (hi - lo)],
                                                   in_=fmv(src)[:, :, lo:hi])),
                              writes=(bx,), dma=True)
                        pb.op("sync", (lambda e, lo=lo, hi=hi, off=off: e.dma_start(
                            out=rw[0:1, :, off:off + (hi - lo)], in_=rowfl[0:4, lo:hi])),
                            reads=(rowsd,), writes=(f"{tag}rw",), dma=True)
                        if off == 1:
                            pb.op("gpsimd", (lambda e: e.memset(rw[0:1, :, 0:1], 0.0)),
                                  writes=(f"{tag}rw",))
                        if hi == T:
                            pb.op("gpsimd", (lambda e: e.memset(rw[0:1, :, 257:258], 0.0)),
                                  writes=(f"{tag}rw",))
                        def mkrr(e):
                            e.tensor_tensor(out=rr[0:1, 0, :], in0=rw[0:1, 0, :],
                                            in1=rw[0:1, 2, :], op=Op.mult)
                            return e.tensor_tensor(out=rr[0:1, 1, :], in0=rw[0:1, 1, :],
                                                   in1=rw[0:1, 2, :], op=Op.mult)
                        pb.op("vector", mkrr, reads=(f"{tag}rw",), writes=(f"{tag}rr",))
                        bcast(tag, rr[0:1, 0, 0:258], 258, 0, (f"{tag}rr",))
                        bcast(tag, rr[0:1, 1, 0:258], 258, 1, (f"{tag}rr",))
                        def cpsbc(e):
                            e.tensor_copy(sbc[:, 0, 0:258], ps[0][0:P, 0:258])
                            return e.tensor_copy(sbc[:, 1, 0:258], ps[1][0:P, 0:258])
                        pb.op("vector", cpsbc, reads=("ps0", "ps1"), writes=(f"{tag}sbc",))
                        def qa(e, pp=pp):
                            rep = sbc[:, 0, 0:258].unsqueeze(1).broadcast_to((P, DC, 258))
                            return e.tensor_tensor(out=xoln[:], in0=x0t[:, pp], in1=rep,
                                                   op=Op.mult)
                        pb.op("vector", qa, reads=(bx, f"{tag}sbc"), writes=(f"{tag}xoln",))
                        def qb(e):
                            rep = sbc[:, 1, 0:258].unsqueeze(1).broadcast_to((P, DC, 258))
                            return e.scalar_tensor_tensor(out=q16[:], in0=xoln[:], scalar=MG,
                                                          in1=rep, op0=Op.add, op1=Op.add)
                        pb.op("vector", qb, reads=(f"{tag}xoln", f"{tag}sbc"),
                              writes=(f"{tag}q16",))
                        pb.op("scalar", (lambda e: e.activation(out=xq[:], in_=q16[:],
                                                                func=AF.Copy, bias=-MG)),
                              reads=(f"{tag}q16",), writes=(f"{tag}xq",))
                        if off == 1:
                            pb.op("gpsimd", (lambda e: e.memset(xq[:, :, 0:1], 0.0)),
                                  writes=(f"{tag}xq",))
                        if hi == T:
                            pb.op("gpsimd", (lambda e: e.memset(xq[:, :, 257:258], 0.0)),
                                  writes=(f"{tag}xq",))
                        pb.op("vector", (lambda e: e.tensor_scalar(
                            out=rr[0:1, 0, :], in0=rw[0:1, 3, :],
                            scalar1=swr[0:1, 2:3], scalar2=None,
                            op0=Op.mult, op1=Op.bypass)),
                            reads=(f"{tag}rw", f"{tag}swr"), writes=(f"{tag}rr",))
                        bcast(tag, rr[0:1, 0, 0:258], 258, 0, (f"{tag}rr",))
                        pb.op("vector", (lambda e: e.tensor_copy(sbc[:, 1, 0:258],
                                                                 ps[0][0:P, 0:258])),
                              reads=("ps0",), writes=(f"{tag}sbc",))

                        for hc in range(HC):
                            bank = 4 + (hc % 2)
                            hr = hc % 2
                            def mm1(e, hc=hc, bank=bank):
                                for dc in range(DC):
                                    ins = e.matmul(ps[bank][0:P, 0:258],
                                                   w1q[:, dc, hc * P:(hc + 1) * P],
                                                   xq[:, dc], start=(dc == 0),
                                                   stop=(dc == DC - 1))
                                return ins
                            bcs = f"{tag}csb{hr}"
                            pb.op("tensor", mm1, reads=(f"{tag}w1q", f"{tag}xq"),
                                  writes=(f"ps{bank}",))
                            pb.op("vector", (lambda e, hr=hr, bank=bank: e.tensor_tensor(
                                out=hsb[:, hr], in0=ps[bank][0:P, 0:258],
                                in1=sbc[:, 1, 0:258], op=Op.mult)),
                                reads=(f"ps{bank}", f"{tag}sbc"), writes=(f"{tag}hsb{hr}",))
                            pb.op("vector", (lambda e, hc=hc, hr=hr: e.tensor_scalar(
                                out=csb[:, hc], in0=hsb[:, hr, 0:256],
                                scalar1=cw[:, hc, 0:1], scalar2=cb[:, hc:hc + 1],
                                op0=Op.mult, op1=Op.add)),
                                reads=(f"{tag}hsb{hr}", f"{tag}cw", f"{tag}cb"),
                                writes=(bcs,))
                            pb.op("vector", (lambda e, hc=hc, hr=hr: e.scalar_tensor_tensor(
                                out=csb[:, hc], in0=hsb[:, hr, 1:257],
                                scalar=cw[:, hc, 1:2], in1=csb[:, hc],
                                op0=Op.mult, op1=Op.add)),
                                reads=(f"{tag}hsb{hr}", f"{tag}cw", bcs),
                                writes=(bcs,))
                            pb.op("vector", (lambda e, hc=hc, hr=hr: e.scalar_tensor_tensor(
                                out=csb[:, hc], in0=hsb[:, hr, 2:258],
                                scalar=cw[:, hc, 2:3], in1=csb[:, hc],
                                op0=Op.mult, op1=Op.add)),
                                reads=(f"{tag}hsb{hr}", f"{tag}cw", bcs),
                                writes=(bcs,))
                        pb.op("scalar", (lambda e: e.activation(
                            out=csb[:], in_=csb[:], func=AF.Silu)),
                            reads=(f"{tag}csb0", f"{tag}csb1"),
                            writes=(f"{tag}csb0", f"{tag}csb1"))
                        pb.op("scalar", (lambda e: e.activation(
                            out=amxb[:], in_=csb[:], func=AF.Abs)),
                            reads=(f"{tag}csb0", f"{tag}csb1"), writes=(f"{tag}amxb",))
                        w = HC // 2
                        while w >= 1:
                            pb.op("vector", (lambda e, w=w: e.tensor_tensor(
                                out=amxb[:, 0:w, :], in0=amxb[:, 0:w, :],
                                in1=amxb[:, w:2 * w, :], op=Op.max)),
                                reads=(f"{tag}amxb",), writes=(f"{tag}amxb",))
                            w //= 2
                        amax_row(tag, amxb, f"{tag}amxb", 1, 256, rr[0:1, 0, 0:256],
                                 f"{tag}rr", 2, 3)
                        pb.op("vector", (lambda e: e.tensor_scalar(
                            out=rr[0:1, 0, 0:256], in0=rr[0:1, 0, 0:256], scalar1=1e-5,
                            scalar2=None, op0=Op.max, op1=Op.bypass)),
                            reads=(f"{tag}rr",), writes=(f"{tag}rr",))
                        pb.op("vector", (lambda e: e.tensor_scalar(
                            out=rr[0:1, 1, 0:256], in0=rr[0:1, 0, 0:256],
                            scalar1=swr[0:1, 3:4], scalar2=half / 127.0,
                            op0=Op.mult, op1=Op.mult)),
                            reads=(f"{tag}rr", f"{tag}swr"), writes=(f"{tag}rr",))
                        pb.op("vector", (lambda e: e.reciprocal(
                            out=rr[0:1, 0, 0:256], in_=rr[0:1, 0, 0:256])),
                            reads=(f"{tag}rr",), writes=(f"{tag}rr",))
                        pb.op("vector", (lambda e: e.tensor_scalar(
                            out=rr[0:1, 0, 0:256], in0=rr[0:1, 0, 0:256], scalar1=127.0,
                            scalar2=None, op0=Op.mult, op1=Op.bypass)),
                            reads=(f"{tag}rr",), writes=(f"{tag}rr",))
                        bcast(tag, rr[0:1, 0, 0:256], 256, 0, (f"{tag}rr",))
                        bcast(tag, rr[0:1, 1, 0:256], 256, 1, (f"{tag}rr",))
                        def cph(e):
                            e.tensor_copy(sbc[:, 1, 0:256], ps[0][0:P, 0:256])
                            return e.tensor_copy(sbc[:, 2, 0:256], ps[1][0:P, 0:256])
                        pb.op("vector", cph, reads=("ps0", "ps1"), writes=(f"{tag}sbc",))
                        pb.op("vector", (lambda e: e.tensor_tensor(
                            out=amxb[:], in0=csb[:],
                            in1=sbc[:, 1, 0:256].unsqueeze(1).broadcast_to((P, HC, 256)),
                            op=Op.mult)),
                            reads=(f"{tag}csb0", f"{tag}csb1", f"{tag}sbc"),
                            writes=(f"{tag}amxb",))
                        pb.op("scalar", (lambda e: e.activation(
                            out=h16[:], in_=amxb[:], func=AF.Copy, bias=MG)),
                            reads=(f"{tag}amxb",), writes=(f"{tag}h16",))
                        pb.op("scalar", (lambda e: e.activation(
                            out=hq[:, :, :], in_=h16[:], func=AF.Copy, bias=-MG)),
                            reads=(f"{tag}h16",), writes=(f"{tag}hq",))
                        for oc in range(DC):
                            bank = 4 + (oc % 2)
                            def mm2(e, oc=oc, bank=bank):
                                for hc2 in range(HC):
                                    ins = e.matmul(ps[bank][0:P, 0:256],
                                                   w2q[:, hc2, oc * P:(oc + 1) * P],
                                                   hq[:, hc2], start=(hc2 == 0),
                                                   stop=(hc2 == HC - 1))
                                return ins
                            pb.op("tensor", mm2, reads=(f"{tag}w2q", f"{tag}hq"),
                                  writes=(f"ps{bank}",))
                            pb.op("vector", (lambda e, oc=oc, bank=bank: e.tensor_tensor(
                                out=xoln[:, oc, 0:256], in0=ps[bank][0:P, 0:256],
                                in1=sbc[:, 2, 0:256], op=Op.mult)),
                                reads=(f"ps{bank}", f"{tag}sbc"), writes=(f"{tag}xoln",))
                        pb.op("vector", (lambda e, pp=pp: e.tensor_tensor(
                            out=xoln[:, :, 0:256], in0=xoln[:, :, 0:256],
                            in1=x0t[:, pp, :, 1:257], op=Op.add)),
                            reads=(f"{tag}xoln", bx), writes=(f"{tag}xoln",))
                        pb.op("gpsimd", (lambda e, t0=t0: e.dma_start(
                            out=fmv(dst)[:, :, t0:t0 + FPH], in_=xoln[:, :, 0:256])),
                            reads=(f"{tag}xoln",), writes=(f"dram{id(dst)}",),
                            dma=True, force_mark=is_out)
                        if DEBUG_FFN and tag == "f1" and i == 0:
                            pb.op("gpsimd", (lambda e: e.dma_start(out=dxq[:, :, :], in_=xq[:])),
                                  reads=(f"{tag}xq",), dma=True, force_mark=True)
                            pb.op("gpsimd", (lambda e: e.dma_start(out=dcs[:, :, :], in_=csb[:])),
                                  reads=(f"{tag}csb0", f"{tag}csb1"), dma=True,
                                  force_mark=True)
                            pb.op("gpsimd", (lambda e: e.dma_start(out=dhq[:, :, :], in_=hq[:])),
                                  reads=(f"{tag}hq",), dma=True, force_mark=True)
                            pb.op("gpsimd", (lambda e: e.dma_start(out=dhh[:, :, :], in_=hsb[:])),
                                  reads=(f"{tag}hsb0", f"{tag}hsb1", f"{tag}hsb2", f"{tag}hsb3"),
                                  dma=True, force_mark=True)
                            pb.op("gpsimd", (lambda e: e.dma_start(out=dsw[:, :], in_=swr[:])),
                                  reads=(f"{tag}swr",), dma=True, force_mark=True)
                    if DEBUG_FFN and tag == "f1":
                        pb.op("gpsimd", (lambda e: e.dma_start(out=drw[:, :], in_=rowfl[:, :])),
                              reads=(rowsd,), dma=True, force_mark=True)
                    pb.barrier()

        def m0():
            with contextlib.ExitStack() as st1:
                xp = st1.enter_context(nc.sbuf_tensor("m0xp", [P, 2, DC, 512], F32))
                sq = st1.enter_context(nc.sbuf_tensor("m0sq", [P, DC, 512], F32))
                rt = st1.enter_context(nc.sbuf_tensor("m0rt", [1, 4, 512], F32))
                bc2 = st1.enter_context(nc.sbuf_tensor("m0bc", [P, 2, 512], F32))
                lnf = st1.enter_context(nc.sbuf_tensor("m0lnf", [P, DC, 512], F32))
                lhi = st1.enter_context(nc.sbuf_tensor("m0lhi", [P, DC, 512], BF16))
                llo = st1.enter_context(nc.sbuf_tensor("m0llo", [P, DC, 512], BF16))
                rwh = st1.enter_context(nc.sbuf_tensor("m0rwh", [P, DC, E], BF16))
                rwl = st1.enter_context(nc.sbuf_tensor("m0rwl", [P, DC, E], BF16))
                rwf = st1.enter_context(nc.sbuf_tensor("m0rwf", [P, DC, E], F32))
                sm = st1.enter_context(nc.sbuf_tensor("m0sm", [P, E], F32))
                smc = st1.enter_context(nc.sbuf_tensor("m0smc", [P, 2], F32))
                pb.op("sync", (lambda e: e.dma_start(
                    out=rwf[:, :, :], in_=rwT.rearrange("(c p) e -> p c e", p=P))),
                    writes=("m0rwf",), dma=True)
                pb.op("vector", (lambda e: e.tensor_copy(rwh[:], rwf[:])),
                      reads=("m0rwf",), writes=("m0rwh",))
                pb.op("vector", (lambda e: e.tensor_tensor(
                    out=rwl[:], in0=rwf[:], in1=rwh[:], op=Op.subtract)),
                    reads=("m0rwf", "m0rwh"), writes=("m0rwl",))
                for i in range(T // 512):
                    pp = i % 2
                    bx = f"m0xp{pp}"
                    sl = slice(i * 512, (i + 1) * 512)
                    pb.op("sync", (lambda e, sl=sl, pp=pp: e.dma_start(
                        out=xp[:, pp], in_=fmv(x1)[:, :, sl])),
                        reads=("dramx1",), writes=(bx,), dma=True)
                    pb.op("scalar", (lambda e, pp=pp: e.activation(
                        out=sq[:], in_=xp[:, pp], func=AF.Square)),
                        reads=(bx,), writes=("m0sq",))
                    def mmsum(e, pp=pp):
                        for dc in range(DC):
                            ins = e.matmul(ps[6][0:1, 0:512], onescol[:],
                                           xp[:, pp, dc], start=(dc == 0),
                                           stop=(dc == DC - 1))
                        return ins
                    pb.op("tensor", mmsum, reads=(bx, "onescol"), writes=("ps6",))
                    def mmsq(e):
                        for dc in range(DC):
                            ins = e.matmul(ps[7][0:1, 0:512], onescol[:],
                                           sq[:, dc], start=(dc == 0),
                                           stop=(dc == DC - 1))
                        return ins
                    pb.op("tensor", mmsq, reads=("m0sq", "onescol"), writes=("ps7",))
                    pb.op("vector", (lambda e: e.tensor_scalar(
                        out=rt[0:1, 0], in0=ps[6][0:1, 0:512],
                        scalar1=1.0 / D_MODEL, scalar2=None, op0=Op.mult, op1=Op.bypass)),
                        reads=("ps6",), writes=("m0rt",))
                    pb.op("vector", (lambda e: e.tensor_tensor(
                        out=rt[0:1, 1], in0=rt[0:1, 0], in1=rt[0:1, 0], op=Op.mult)),
                        reads=("m0rt",), writes=("m0rt",))
                    pb.op("vector", (lambda e: e.scalar_tensor_tensor(
                        out=rt[0:1, 1], in0=ps[7][0:1, 0:512], scalar=1.0 / D_MODEL,
                        in1=rt[0:1, 1], op0=Op.mult, op1=Op.subtract)),
                        reads=("ps7", "m0rt"), writes=("m0rt",))
                    pb.op("scalar", (lambda e: e.activation(
                        out=rt[0:1, 1], in_=rt[0:1, 1], func=AF.Ln,
                        bias=epscol[0:1, 0:1])),
                        reads=("m0rt", "epscol"), writes=("m0rt",))
                    pb.op("scalar", (lambda e: e.activation(
                        out=rt[0:1, 2], in_=rt[0:1, 1], func=AF.Exp, scale=-0.5)),
                        reads=("m0rt",), writes=("m0rt",))
                    pb.op("vector", (lambda e: e.scalar_tensor_tensor(
                        out=rt[0:1, 3], in0=rt[0:1, 0], scalar=-1.0,
                        in1=rt[0:1, 2], op0=Op.mult, op1=Op.mult)),
                        reads=("m0rt",), writes=("m0rt",))
                    bcast("m0", rt[0:1, 2], 512, 6, ("m0rt",))
                    bcast("m0", rt[0:1, 3], 512, 7, ("m0rt",))
                    def cpbc(e):
                        e.tensor_copy(bc2[:, 0], ps[6][0:P, 0:512])
                        return e.tensor_copy(bc2[:, 1], ps[7][0:P, 0:512])
                    pb.op("vector", cpbc, reads=("ps6", "ps7"), writes=("m0bc",))
                    def lna(e, pp=pp):
                        rep = bc2[:, 0].unsqueeze(1).broadcast_to((P, DC, 512))
                        return e.tensor_tensor(out=lnf[:], in0=xp[:, pp], in1=rep,
                                               op=Op.mult)
                    pb.op("vector", lna, reads=(bx, "m0bc"), writes=("m0lnf",))
                    def lnb(e):
                        rep = bc2[:, 1].unsqueeze(1).broadcast_to((P, DC, 512))
                        return e.tensor_tensor(out=lnf[:], in0=lnf[:], in1=rep, op=Op.add)
                    pb.op("vector", lnb, reads=("m0lnf", "m0bc"), writes=("m0lnf",))
                    pb.op("vector", (lambda e: e.tensor_copy(lhi[:], lnf[:])),
                          reads=("m0lnf",), writes=("m0lhi",))
                    pb.op("vector", (lambda e: e.tensor_tensor(
                        out=llo[:], in0=lnf[:], in1=lhi[:], op=Op.subtract)),
                        reads=("m0lnf", "m0lhi"), writes=("m0llo",))
                    pb.op("gpsimd", (lambda e, sl=sl: e.dma_start(
                        out=fmv(ln2hi)[:, :, sl], in_=lhi[:])),
                        reads=("m0lhi",), writes=("dramlnhi",), dma=True)
                    pb.op("gpsimd", (lambda e, sl=sl: e.dma_start(
                        out=fmv(ln2lo)[:, :, sl], in_=llo[:])),
                        reads=("m0llo",), writes=("dramlnlo",), dma=True)
                    for g in range(4):
                        G = i * 4 + g
                        t0g = g * 128
                        def rmm(e, t0g=t0g):
                            for dc in range(DC):
                                e.matmul(ps[5][0:P, 0:E], lhi[:, dc, t0g:t0g + 128],
                                         rwh[:, dc], start=(dc == 0), stop=False)
                                e.matmul(ps[5][0:P, 0:E], lhi[:, dc, t0g:t0g + 128],
                                         rwl[:, dc], start=False, stop=False)
                                ins = e.matmul(ps[5][0:P, 0:E], llo[:, dc, t0g:t0g + 128],
                                               rwh[:, dc], start=False,
                                               stop=(dc == DC - 1))
                            return ins
                        pb.op("tensor", rmm, reads=("m0lhi", "m0llo", "m0rwh", "m0rwl"),
                              writes=("ps5",))
                        pb.op("vector", (lambda e: e.tensor_reduce(
                            out=smc[:, 0:1], in_=ps[5][0:P, 0:E], axis=AX.X, op=Op.max)),
                            reads=("ps5",), writes=("m0smc",))
                        pb.op("vector", (lambda e: e.tensor_scalar(
                            out=sm[:], in0=ps[5][0:P, 0:E], scalar1=smc[:, 0:1],
                            scalar2=None, op0=Op.subtract, op1=Op.bypass)),
                            reads=("ps5", "m0smc"), writes=("m0sm",))
                        pb.op("scalar", (lambda e: e.activation(
                            out=sm[:], in_=sm[:], func=AF.Exp)),
                            reads=("m0sm",), writes=("m0sm",))
                        pb.op("vector", (lambda e: e.tensor_reduce(
                            out=smc[:, 0:1], in_=sm[:], axis=AX.X, op=Op.add)),
                            reads=("m0sm",), writes=("m0smc",))
                        pb.op("vector", (lambda e: e.reciprocal(
                            out=smc[:, 1:2], in_=smc[:, 0:1])),
                            reads=("m0smc",), writes=("m0smc",))
                        pb.op("vector", (lambda e, G=G: e.tensor_scalar(
                            out=r_sb[:, G * E:(G + 1) * E], in0=sm[:],
                            scalar1=smc[:, 1:2], scalar2=None,
                            op0=Op.mult, op1=Op.bypass)),
                            reads=("m0sm", "m0smc"), writes=("r_sb",))
                pb.barrier()

        def m1(ei):
            with contextlib.ExitStack() as st1:
                lhi = st1.enter_context(nc.sbuf_tensor(f"e{ei}lhi", [P, DC, T], BF16))
                llo = st1.enter_context(nc.sbuf_tensor(f"e{ei}llo", [P, DC, T], BF16))
                wf = st1.enter_context(nc.sbuf_tensor(f"e{ei}wf", [P, 2, DC, P], F32))
                whi = st1.enter_context(nc.sbuf_tensor(f"e{ei}whi", [P, 2, DC, P], BF16))
                wlo = st1.enter_context(nc.sbuf_tensor(f"e{ei}wlo", [P, 2, DC, P], BF16))
                pb.op("gpsimd", (lambda e: e.memset(whi[:], 0.0)),
                      writes=(f"e{ei}whi0", f"e{ei}whi1"))
                pb.op("gpsimd", (lambda e: e.memset(wlo[:], 0.0)),
                      writes=(f"e{ei}wlo0", f"e{ei}wlo1"))
                zst = st1.enter_context(nc.sbuf_tensor(f"e{ei}zst", [P, 2, T], F32))
                pb.op("sync", (lambda e: e.dma_start(out=lhi[:], in_=fmv(ln2hi))),
                      reads=("dramlnhi",), writes=(f"e{ei}lhi",), dma=True)
                pb.op("sync", (lambda e: e.dma_start(out=llo[:], in_=fmv(ln2lo))),
                      reads=("dramlnlo",), writes=(f"e{ei}llo",), dma=True)
                inwv = inw[ei].rearrange("(c p) f -> p c f", p=P)
                NFC = 35
                for fc in range(NFC):
                    fw = 128 if fc < 34 else 32
                    f0 = fc * 128
                    pp = fc % 2
                    bw = f"e{ei}wf{pp}"
                    bz = f"e{ei}zst{pp}"
                    pb.op("sync", (lambda e, f0=f0, fw=fw, pp=pp: e.dma_start(
                        out=wf[:, pp, :, 0:fw], in_=inwv[:, :, f0:f0 + fw])),
                        writes=(bw,), dma=True)
                    pb.op("vector", (lambda e, pp=pp, fw=fw: e.tensor_copy(
                        whi[:, pp, :, 0:fw], wf[:, pp, :, 0:fw])),
                        reads=(bw,), writes=(f"e{ei}whi{pp}",))
                    pb.op("vector", (lambda e, pp=pp, fw=fw: e.tensor_tensor(
                        out=wlo[:, pp, :, 0:fw], in0=wf[:, pp, :, 0:fw],
                        in1=whi[:, pp, :, 0:fw], op=Op.subtract)),
                        reads=(bw, f"e{ei}whi{pp}"), writes=(f"e{ei}wlo{pp}",))
                    for tg in range(T // 512):
                        sl = slice(tg * 512, (tg + 1) * 512)
                        bank = 4 + (tg % 2)
                        def pmm(e, pp=pp, sl=sl, bank=bank):
                            for dc in range(DC):
                                e.matmul(ps[bank][0:P, 0:512], whi[:, pp, dc, :],
                                         lhi[:, dc, sl], start=(dc == 0), stop=False)
                                e.matmul(ps[bank][0:P, 0:512], whi[:, pp, dc, :],
                                         llo[:, dc, sl], start=False, stop=False)
                                ins = e.matmul(ps[bank][0:P, 0:512], wlo[:, pp, dc, :],
                                               lhi[:, dc, sl], start=False,
                                               stop=(dc == DC - 1))
                            return ins
                        pb.op("tensor", pmm,
                              reads=(f"e{ei}whi{pp}", f"e{ei}wlo{pp}",
                                     f"e{ei}lhi", f"e{ei}llo"),
                              writes=(f"ps{bank}",))
                        pb.op("vector", (lambda e, fw=fw, sl=sl, pp=pp, bank=bank:
                                         e.tensor_copy(zst[0:fw, pp, sl],
                                                       ps[bank][0:fw, 0:512])),
                              reads=(f"ps{bank}",), writes=(bz,))
                    pb.op("gpsimd", (lambda e, f0=f0, fw=fw, pp=pp: e.dma_start(
                        out=zx[f0:f0 + fw, :], in_=zst[0:fw, pp, :])),
                        reads=(bz,), writes=("dramzx",), dma=True)
                pb.barrier()

        def m2(ei):
            with contextlib.ExitStack() as st1:
                owhi = st1.enter_context(nc.sbuf_tensor(f"o{ei}owhi", [P, DINC, 1024], BF16))
                owlo = st1.enter_context(nc.sbuf_tensor(f"o{ei}owlo", [P, DINC, 1024], BF16))
                cw = st1.enter_context(nc.sbuf_tensor(f"o{ei}cw", [P, CVC, 4], F32))
                cb = st1.enter_context(nc.sbuf_tensor(f"o{ei}cb", [P, CVC], F32))
                nwcol = st1.enter_context(nc.sbuf_tensor(f"o{ei}nw", [P, DINC], F32))
                dskb = st1.enter_context(nc.sbuf_tensor(f"o{ei}dsk", [P, NH], F32))
                acol = st1.enter_context(nc.sbuf_tensor(f"o{ei}acol", [NH, 2], F32))
                onesh = st1.enter_context(nc.sbuf_tensor(f"o{ei}ones", [NH, CH], F32))
                state = st1.enter_context(nc.sbuf_tensor(f"o{ei}st", [P, D_INNER], F32))
                pb.op("sync", (lambda e: e.dma_start(
                    out=cw[:, :, :], in_=cwm[ei].rearrange("(c p) k -> p c k", p=P))),
                    writes=(f"o{ei}cw",), dma=True)
                pb.op("sync", (lambda e: e.dma_start(
                    out=cb[:, :], in_=cbm[ei].rearrange("(c p) -> p c", p=P),
                    allow_slow_non_contiguous=True)),
                    writes=(f"o{ei}cb",), dma=True)
                pb.op("sync", (lambda e: e.dma_start(
                    out=nwcol[:, :], in_=nrmw[ei].rearrange("(c p) -> p c", p=P),
                    allow_slow_non_contiguous=True)),
                    writes=(f"o{ei}nw",), dma=True)
                pb.op("sync", (lambda e: e.dma_start(
                    out=dskb[:, :], in_=dskp[ei:ei + 1, :].partition_broadcast(P))),
                    writes=(f"o{ei}dsk",), dma=True)
                pb.op("sync", (lambda e: e.dma_start(
                    out=acol[:, 0:1], in_=alog[ei].unsqueeze(-1))),
                    writes=(f"o{ei}acol",), dma=True)
                pb.op("sync", (lambda e: e.dma_start(
                    out=acol[:, 1:2], in_=dtb[ei].unsqueeze(-1))),
                    writes=(f"o{ei}acol",), dma=True)
                pb.op("gpsimd", (lambda e: e.memset(onesh[:], 1.0)), writes=(f"o{ei}ones",))
                pb.op("gpsimd", (lambda e: e.memset(state[:], 0.0)), writes=(f"o{ei}st",))
                pb.op("scalar", (lambda e: e.activation(
                    out=acol[:, 0:1], in_=acol[:, 0:1], func=AF.Exp)),
                    reads=(f"o{ei}acol",), writes=(f"o{ei}acol",))
                owv = outw[ei].rearrange("(c p) d -> p c d", p=P)
                with contextlib.ExitStack() as st2:
                    owf = st2.enter_context(nc.sbuf_tensor(f"o{ei}owf", [P, 8, 1024], F32))
                    for pc2 in range(2):
                        pb.op("sync", (lambda e, pc2=pc2: e.dma_start(
                            out=owf[:, :, :], in_=owv[:, pc2 * 8:(pc2 + 1) * 8, :])),
                            writes=(f"o{ei}owf",), dma=True)
                        def nwsc(e, pc2=pc2):
                            for j in range(8):
                                ins = e.tensor_scalar(
                                    out=owf[:, j, :], in0=owf[:, j, :],
                                    scalar1=nwcol[:, pc2 * 8 + j:pc2 * 8 + j + 1],
                                    scalar2=None, op0=Op.mult, op1=Op.bypass)
                            return ins
                        pb.op("vector", nwsc, reads=(f"o{ei}owf", f"o{ei}nw"),
                              writes=(f"o{ei}owf",))
                        pb.op("vector", (lambda e, pc2=pc2: e.tensor_copy(
                            owhi[:, pc2 * 8:(pc2 + 1) * 8, :], owf[:, :, :])),
                            reads=(f"o{ei}owf",), writes=(f"o{ei}owhi",))
                        pb.op("vector", (lambda e, pc2=pc2: e.tensor_tensor(
                            out=owlo[:, pc2 * 8:(pc2 + 1) * 8, :], in0=owf[:, :, :],
                            in1=owhi[:, pc2 * 8:(pc2 + 1) * 8, :], op=Op.subtract)),
                            reads=(f"o{ei}owf", f"o{ei}owhi"), writes=(f"o{ei}owlo",))
                    dts = st2.enter_context(nc.sbuf_tensor(f"o{ei}dts", [NH, 6, 512], F32))
                    for ph in range(T // 512):
                        sl = slice(ph * 512, (ph + 1) * 512)
                        pb.op("sync", (lambda e, sl=sl: e.dma_start(
                            out=dts[:, 0, :], in_=zx[4352:4384, sl])),
                            reads=("dramzx",), writes=(f"o{ei}dts",), dma=True)
                        pb.op("vector", (lambda e: e.tensor_scalar(
                            out=dts[:, 1, :], in0=dts[:, 0, :], scalar1=acol[:, 1:2],
                            scalar2=0.0, op0=Op.add, op1=Op.max)),
                            reads=(f"o{ei}dts", f"o{ei}acol"), writes=(f"o{ei}dts",))
                        pb.op("scalar", (lambda e: e.activation(
                            out=dts[:, 2, :], in_=dts[:, 0, :], func=AF.Abs,
                            bias=acol[:, 1:2])),
                            reads=(f"o{ei}dts", f"o{ei}acol"), writes=(f"o{ei}dts",))
                        pb.op("scalar", (lambda e: e.activation(
                            out=dts[:, 2, :], in_=dts[:, 2, :], func=AF.Exp, scale=-1.0)),
                            reads=(f"o{ei}dts",), writes=(f"o{ei}dts",))
                        pb.op("scalar", (lambda e: e.activation(
                            out=dts[:, 2, :], in_=dts[:, 2, :], func=AF.Ln,
                            bias=onescol[0:NH, 0:1])),
                            reads=(f"o{ei}dts", "onescol"), writes=(f"o{ei}dts",))
                        pb.op("vector", (lambda e: e.tensor_tensor(
                            out=dts[:, 1, :], in0=dts[:, 1, :], in1=dts[:, 2, :],
                            op=Op.add)),
                            reads=(f"o{ei}dts",), writes=(f"o{ei}dts",))
                        pb.op("vector", (lambda e: e.tensor_scalar(
                            out=dts[:, 2, :], in0=dts[:, 1, :], scalar1=acol[:, 0:1],
                            scalar2=-1.0, op0=Op.mult, op1=Op.mult)),
                            reads=(f"o{ei}dts", f"o{ei}acol"), writes=(f"o{ei}dts",))
                        def csscan(e):
                            for cc in range(4):
                                slc = slice(cc * CH, (cc + 1) * CH)
                                ins = e.tensor_tensor_scan(
                                    out=dts[:, 3, slc], data0=onesh[:],
                                    data1=dts[:, 2, slc], initial=0.0,
                                    op0=Op.mult, op1=Op.add)
                            return ins
                        pb.op("vector", csscan, reads=(f"o{ei}dts", f"o{ei}ones"),
                              writes=(f"o{ei}dts",))
                        pb.op("scalar", (lambda e: e.activation(
                            out=dts[:, 5, :], in_=dts[:, 3, :], func=AF.Exp)),
                            reads=(f"o{ei}dts",), writes=(f"o{ei}dts",))
                        def dsx(e):
                            for cc in range(4):
                                slc = slice(cc * CH, (cc + 1) * CH)
                                ins = e.activation(
                                    out=dts[:, 4, slc], in_=dts[:, 3, slc],
                                    func=AF.Exp, scale=-1.0,
                                    bias=dts[:, 3, (cc + 1) * CH - 1:(cc + 1) * CH])
                            return ins
                        pb.op("scalar", dsx, reads=(f"o{ei}dts",), writes=(f"o{ei}dts",))
                        pb.op("vector", (lambda e: e.tensor_tensor(
                            out=dts[:, 2, :], in0=dts[:, 1, :], in1=dts[:, 4, :],
                            op=Op.mult)),
                            reads=(f"o{ei}dts",), writes=(f"o{ei}dts",))
                        for ri, row in ((0, 1), (1, 2), (2, 5), (3, 3)):
                            pb.op("gpsimd", (lambda e, ri=ri, row=row, sl=sl: e.dma_start(
                                out=dtsfl[ri * NH:(ri + 1) * NH, sl],
                                in_=dts[:, row, :])),
                                reads=(f"o{ei}dts",), writes=("dramdts",), dma=True)
                        pb.op("gpsimd", (lambda e, ph=ph: e.dma_start(
                            out=decfl[ph * 4:(ph + 1) * 4, :].transpose([1, 0]),
                            in_=dts[:, 5, CH - 1::CH],
                            allow_slow_non_contiguous=True)),
                            reads=(f"o{ei}dts",), writes=("dramdec",), dma=True)
                    pb.barrier()

                xbc = st1.enter_context(nc.sbuf_tensor(f"o{ei}xbc", [P, 2, CVC, 131], F32))
                zt = st1.enter_context(nc.sbuf_tensor(f"o{ei}zt", [P, DINC, CH], F32))
                csbc = st1.enter_context(nc.sbuf_tensor(f"o{ei}csbc", [P, NH, CH], F32))
                decb = st1.enter_context(nc.sbuf_tensor(f"o{ei}decb", [P, NH], F32))
                cvo = st1.enter_context(nc.sbuf_tensor(f"o{ei}cvo", [P, CVC, CH], F32))
                dtsl = st1.enter_context(nc.sbuf_tensor(f"o{ei}dtsl", [P, CH], F32))
                tsml = st1.enter_context(nc.sbuf_tensor(f"o{ei}tsml", [P, 4, NH], F32))
                xdtT = st1.enter_context(nc.sbuf_tensor(f"o{ei}xdtT", [P, 1, DINC, P], F32))
                xqT = st1.enter_context(nc.sbuf_tensor(f"o{ei}xqT", [P, DINC, P], F32))
                xskT = st1.enter_context(nc.sbuf_tensor(f"o{ei}xskT", [P, DINC, P], F32))
                bmT = st1.enter_context(nc.sbuf_tensor(f"o{ei}bmT", [P, P], F32))
                szT = st1.enter_context(nc.sbuf_tensor(f"o{ei}szT", [P, D_INNER], F32))
                cbt = st1.enter_context(nc.sbuf_tensor(f"o{ei}cbt", [P, P], F32))
                mh = st1.enter_context(nc.sbuf_tensor(f"o{ei}mh", [P, 2, 8, P], F32))
                ytok = st1.enter_context(nc.sbuf_tensor(f"o{ei}ytok", [P, D_INNER], F32))
                yfh = st1.enter_context(nc.sbuf_tensor(f"o{ei}yfh", [P, DINC, 2, P], BF16))
                yfl = st1.enter_context(nc.sbuf_tensor(f"o{ei}yfl", [P, DINC, 2, P], BF16))
                sscl = st1.enter_context(nc.sbuf_tensor(f"o{ei}sscl", [P, 4], F32))
                xnew = st1.enter_context(nc.sbuf_tensor(f"o{ei}xnew", [P, DC, 2 * CH], F32))

                pb.op("gpsimd", (lambda e: e.memset(xbc[:, :, :, 0:3], 0.0)),
                      writes=(f"o{ei}xbc0", f"o{ei}xbc1"))
                zxv = zx.rearrange("(c p) t -> p c t", p=P)
                for c in range(NCH):
                    pp = c % 2
                    t0 = c * CH
                    sl = slice(t0, t0 + CH)
                    bxb = f"o{ei}xbc{pp}"
                    btm, bct, bbm, bxd = (f"o{ei}tsml", f"o{ei}cbt",
                                          f"o{ei}bmT", f"o{ei}xdtT0")
                    bdec = f"o{ei}decb"
                    if c == 0:
                        pb.op("sync", (lambda e, pp=pp: e.dma_start(
                            out=xbc[:, pp, :, 3:131], in_=zxv[:, 16:34, 0:CH])),
                            reads=("dramzx",), writes=(bxb,), dma=True)
                    else:
                        pb.op("sync", (lambda e, pp=pp, t0=t0: e.dma_start(
                            out=xbc[:, pp, :, 0:131], in_=zxv[:, 16:34, t0 - 3:t0 + CH])),
                            reads=("dramzx",), writes=(bxb,), dma=True)
                    pb.op("sync", (lambda e, sl=sl: e.dma_start(
                        out=zt[:, :, :], in_=zxv[:, 0:DINC, sl])),
                        reads=("dramzx",), writes=(f"o{ei}zt",), dma=True)
                    pb.op("sync", (lambda e, sl=sl: e.dma_start(
                        out=dtsl[:, :], in_=dtsfl[0:P, sl])),
                        reads=("dramdts",), writes=(f"o{ei}dtsl",), dma=True)
                    pb.op("sync", (lambda e, sl=sl: e.dma_start(
                        out=csbc[:, :, :],
                        in_=dtsfl[96:128, sl].partition_broadcast(P))),
                        reads=("dramdts",), writes=(f"o{ei}csbc",), dma=True)
                    pb.op("sync", (lambda e, c=c: e.dma_start(
                        out=decb[:, :],
                        in_=decfl[c, :].partition_broadcast(P))),
                        reads=("dramdec",), writes=(bdec,), dma=True)
                    xsrc = x1 if ei == 0 else x2
                    pb.op("sync", (lambda e, sl=sl, xs=xsrc, pp=pp: e.dma_start(
                        out=xnew[:, :, pp * CH:(pp + 1) * CH], in_=fmv(xs)[:, :, sl])),
                        reads=("dramx1" if ei == 0 else "dramx2",),
                        writes=(f"o{ei}xnew",), dma=True)

                    def tap0(e, pp=pp):
                        for cc in range(CVC):
                            ins = e.tensor_scalar(
                                out=cvo[:, cc], in0=xbc[:, pp, cc, 0:CH],
                                scalar1=cw[:, cc, 0:1], scalar2=cb[:, cc:cc + 1],
                                op0=Op.mult, op1=Op.add)
                        return ins
                    pb.op("gpsimd", tap0, reads=(bxb, f"o{ei}cw", f"o{ei}cb"),
                          writes=(f"o{ei}cvo",))
                    for k in (1, 2, 3):
                        def tapk(e, k=k, pp=pp):
                            for cc in range(CVC):
                                ins = e.scalar_tensor_tensor(
                                    out=cvo[:, cc], in0=xbc[:, pp, cc, k:k + CH],
                                    scalar=cw[:, cc, k:k + 1], in1=cvo[:, cc],
                                    op0=Op.mult, op1=Op.add)
                            return ins
                        pb.op("vector", tapk, reads=(bxb, f"o{ei}cw", f"o{ei}cvo"),
                              writes=(f"o{ei}cvo",))
                    pb.op("scalar", (lambda e: e.activation(
                        out=cvo[:], in_=cvo[:], func=AF.Silu)),
                        reads=(f"o{ei}cvo",), writes=(f"o{ei}cvo",))
                    pb.op("scalar", (lambda e: e.activation(
                        out=zt[:], in_=zt[:], func=AF.Silu)),
                        reads=(f"o{ei}zt",), writes=(f"o{ei}zt",))

                    pb.op("tensor", (lambda e: e.transpose(
                        ps[2][0:P, 0:128], dtsl[:, :], ident[:])),
                        reads=(f"o{ei}dtsl", "ident"), writes=("ps2",))
                    pb.op("scalar", (lambda e: e.activation(
                        out=tsml[:, :, :], in_=ps[2][0:P, 0:128].rearrange(
                            "p (a b) -> p a b", a=4), func=AF.Copy, bias=0.0)),
                        reads=("ps2",), writes=(btm,))
                    pb.op("tensor", (lambda e: e.transpose(
                        ps[2][0:P, 128:256], cvo[:, 16, :], ident[:])),
                        reads=(f"o{ei}cvo", "ident"), writes=("ps2",))
                    pb.op("scalar", (lambda e: e.activation(
                        out=bmT[:, :], in_=ps[2][0:P, 128:256],
                        func=AF.Copy, bias=0.0)),
                        reads=("ps2",), writes=(bbm,))
                    pb.op("tensor", (lambda e: e.matmul(
                        ps[2][0:P, 256:384], cvo[:, 16, :], cvo[:, 17, :],
                        start=True, stop=True)),
                        reads=(f"o{ei}cvo",), writes=("ps2",))
                    pb.op("scalar", (lambda e: e.activation(
                        out=cbt[:, :], in_=ps[2][0:P, 256:384],
                        func=AF.Copy, bias=0.0)),
                        reads=("ps2",), writes=(bct,))
                    for r4 in range(4):
                        bank = 6 + (r4 % 2)
                        def xtr(e, r4=r4, bank=bank):
                            for j in range(4):
                                cc = r4 * 4 + j
                                ins = e.transpose(ps[bank][0:P, j * 128:(j + 1) * 128],
                                                  cvo[:, cc, :], ident[:])
                            return ins
                        pb.op("tensor", xtr, reads=(f"o{ei}cvo", "ident"),
                              writes=(f"ps{bank}",))
                        def xev(e, r4=r4, pp=pp, bank=bank):
                            pss = ps[bank][0:P, 0:512].rearrange("p (a b) -> p a b", a=8)
                            rep = tsml[:, 0, 8 * r4:8 * r4 + 8].unsqueeze(-1) \
                                .broadcast_to((P, 8, 64))
                            e.tensor_tensor(out=xdtT[:, 0, 4 * r4:4 * r4 + 4].rearrange(
                                "p a (c b) -> p (a c) b", b=64), in0=pss, in1=rep,
                                op=Op.mult)
                            rep2 = tsml[:, 1, 8 * r4:8 * r4 + 8].unsqueeze(-1) \
                                .broadcast_to((P, 8, 64))
                            e.tensor_tensor(out=xqT[:, 4 * r4:4 * r4 + 4].rearrange(
                                "p a (c b) -> p (a c) b", b=64), in0=pss, in1=rep2,
                                op=Op.mult)
                            rep3 = dskb[:, 8 * r4:8 * r4 + 8].unsqueeze(-1) \
                                .broadcast_to((P, 8, 64))
                            return e.tensor_tensor(out=xskT[:, 4 * r4:4 * r4 + 4].rearrange(
                                "p a (c b) -> p (a c) b", b=64), in0=pss, in1=rep3,
                                op=Op.mult)
                        pb.op("vector", xev, reads=(f"ps{bank}", btm, f"o{ei}dsk"),
                              writes=(bxd, f"o{ei}xqT", f"o{ei}xskT"))
                    for r4 in range(4):
                        bank = 6 + (r4 % 2)
                        def ztr(e, r4=r4, bank=bank):
                            for j in range(4):
                                cc = r4 * 4 + j
                                ins = e.transpose(ps[bank][0:P, j * 128:(j + 1) * 128],
                                                  zt[:, cc, :], ident[:])
                            return ins
                        pb.op("tensor", ztr, reads=(f"o{ei}zt", "ident"),
                              writes=(f"ps{bank}",))
                        pb.op("scalar", (lambda e, r4=r4, bank=bank: e.activation(
                            out=szT[:, r4 * 512:(r4 + 1) * 512],
                            in_=ps[bank][0:P, 0:512],
                            func=AF.Copy, bias=0.0)),
                            reads=(f"ps{bank}",), writes=(f"o{ei}szT",))

                    for g in range(4):
                        pb.op("tensor", (lambda e, g=g: e.matmul(
                            ps[g][0:P, 0:512], cvo[:, 17, :],
                            state[:, g * 512:(g + 1) * 512], start=True, stop=True)),
                            reads=(f"o{ei}cvo", f"o{ei}st"), writes=(f"ps{g}",))
                        def yev(e, g=g, pp=pp):
                            rep = tsml[:, 2, 8 * g:8 * g + 8].unsqueeze(-1) \
                                .broadcast_to((P, 8, 64))
                            return e.tensor_tensor(
                                out=ytok[:, g * 512:(g + 1) * 512].rearrange(
                                    "p (a b) -> p a b", a=8),
                                in0=ps[g][0:P, 0:512].rearrange("p (a b) -> p a b", a=8),
                                in1=rep, op=Op.mult)
                        pb.op("vector", yev, reads=(f"ps{g}", btm),
                              writes=(f"o{ei}ytok",))
                    for g in range(4):
                        bank = 4 + (g % 2)
                        pb.op("tensor", (lambda e, g=g, bank=bank: e.matmul(
                            ps[bank][0:P, 0:512], bmT[:, :],
                            xqT[:, 4 * g:4 * g + 4, :].rearrange("p a b -> p (a b)"),
                            start=True, stop=True)),
                            reads=(bbm, f"o{ei}xqT"), writes=(f"ps{bank}",))
                        def srec1(e, g=g, pp=pp):
                            rep = decb[:, 8 * g:8 * g + 8].unsqueeze(-1) \
                                .broadcast_to((P, 8, 64))
                            return e.tensor_tensor(
                                out=state[:, g * 512:(g + 1) * 512]
                                .rearrange("p (a b) -> p a b", a=8),
                                in0=state[:, g * 512:(g + 1) * 512]
                                .rearrange("p (a b) -> p a b", a=8),
                                in1=rep, op=Op.mult)
                        pb.op("vector", srec1, reads=(bdec, f"o{ei}st"),
                              writes=(f"o{ei}st",))
                        pb.op("vector", (lambda e, g=g, bank=bank: e.tensor_tensor(
                            out=state[:, g * 512:(g + 1) * 512],
                            in0=state[:, g * 512:(g + 1) * 512],
                            in1=ps[bank][0:P, 0:512], op=Op.add)),
                            reads=(f"ps{bank}", f"o{ei}st"), writes=(f"o{ei}st",))
                    pb.op("vector", (lambda e: e.tensor_tensor(
                        out=ytok[:], in0=ytok[:],
                        in1=xskT[:, :, :].rearrange("p a b -> p (a b)"), op=Op.add)),
                        reads=(f"o{ei}xskT", f"o{ei}ytok"), writes=(f"o{ei}ytok",))

                    pb.op("vector", (lambda e: e.tensor_tensor(
                        out=csbc[:], in0=csbc[:],
                        in1=masktile[:, :].unsqueeze(1).broadcast_to((P, NH, CH)),
                        op=Op.add)),
                        reads=(f"o{ei}csbc", "masktile"), writes=(f"o{ei}csbc",))
                    for g in range(4):
                        gp = g % 2
                        bmh = f"o{ei}mh{gp}"
                        def msub(e, g=g, gp=gp):
                            rep = tsml[:, 3, 8 * g:8 * g + 8].unsqueeze(-1) \
                                .broadcast_to((P, 8, CH))
                            return e.tensor_tensor(out=mh[:, gp],
                                                   in0=csbc[:, 8 * g:8 * g + 8, :],
                                                   in1=rep, op=Op.subtract)
                        pb.op("vector", msub, reads=(f"o{ei}csbc", btm),
                              writes=(bmh,))
                        pb.op("scalar", (lambda e, gp=gp: e.activation(
                            out=mh[:, gp], in_=mh[:, gp], func=AF.Exp)),
                            reads=(bmh,), writes=(bmh,))
                        def mmul(e, gp=gp):
                            rep = cbt[:, :].unsqueeze(1).broadcast_to((P, 8, CH))
                            return e.tensor_tensor(out=mh[:, gp], in0=mh[:, gp],
                                                   in1=rep, op=Op.mult)
                        pb.op("vector", mmul, reads=(bmh, bct), writes=(bmh,))
                        def ydmm(e, g=g, gp=gp):
                            for hcol in range(8):
                                h = g * 8 + hcol
                                ins = e.matmul(
                                    ps[g][0:P, hcol * 64:(hcol + 1) * 64],
                                    mh[:, gp, hcol, :],
                                    xdtT[:, 0, h // 2, (h % 2) * 64:(h % 2) * 64 + 64],
                                    start=True, stop=True)
                            return ins
                        pb.op("tensor", ydmm, reads=(bmh, bxd), writes=(f"ps{g}",))
                    for g in range(4):
                        pb.op("vector", (lambda e, g=g: e.tensor_tensor(
                            out=ytok[:, g * 512:(g + 1) * 512],
                            in0=ytok[:, g * 512:(g + 1) * 512],
                            in1=ps[g][0:P, 0:512], op=Op.add)),
                            reads=(f"ps{g}", f"o{ei}ytok"), writes=(f"o{ei}ytok",))

                    pb.op("vector", (lambda e: e.tensor_tensor(
                        out=ytok[:], in0=ytok[:], in1=szT[:], op=Op.mult)),
                        reads=(f"o{ei}szT", f"o{ei}ytok"), writes=(f"o{ei}ytok",))
                    pb.op("scalar", (lambda e: e.activation(
                        out=xskT[:, :, :].rearrange("p a b -> p (a b)"), in_=ytok[:],
                        func=AF.Square)),
                        reads=(f"o{ei}ytok",), writes=(f"o{ei}xskT",))
                    pb.op("vector", (lambda e: e.tensor_reduce(
                        out=sscl[:, 0:1], in_=xskT[:, :, :].rearrange("p a b -> p (a b)"),
                        axis=AX.X, op=Op.add)),
                        reads=(f"o{ei}xskT",), writes=(f"o{ei}sscl",))
                    pb.op("scalar", (lambda e: e.activation(
                        out=sscl[:, 1:2], in_=sscl[:, 0:1], func=AF.Ln,
                        scale=1.0 / D_INNER, bias=epscol[0:P, 0:1])),
                        reads=(f"o{ei}sscl", "epscol"), writes=(f"o{ei}sscl",))
                    pb.op("scalar", (lambda e: e.activation(
                        out=sscl[:, 2:3], in_=sscl[:, 1:2], func=AF.Exp, scale=-0.5)),
                        reads=(f"o{ei}sscl",), writes=(f"o{ei}sscl",))
                    pb.op("vector", (lambda e, c=c: e.tensor_tensor(
                        out=sscl[:, 3:4], in0=sscl[:, 2:3],
                        in1=r_sb[:, c * E + ei:c * E + ei + 1], op=Op.mult)),
                        reads=(f"o{ei}sscl", "r_sb"), writes=(f"o{ei}sscl",))
                    pb.op("vector", (lambda e: e.tensor_scalar(
                        out=ytok[:], in0=ytok[:], scalar1=sscl[:, 3:4], scalar2=None,
                        op0=Op.mult, op1=Op.bypass)),
                        reads=(f"o{ei}ytok", f"o{ei}sscl"), writes=(f"o{ei}ytok",))
                    for r4 in range(4):
                        bank = 6 + (r4 % 2)
                        def ytr(e, r4=r4, bank=bank):
                            for j in range(4):
                                cc = r4 * 4 + j
                                ins = e.transpose(ps[bank][0:P, j * 128:(j + 1) * 128],
                                                  ytok[:, cc * 128:(cc + 1) * 128],
                                                  ident[:])
                            return ins
                        pb.op("tensor", ytr, reads=(f"o{ei}ytok", "ident"),
                              writes=(f"ps{bank}",))
                        pb.op("scalar", (lambda e, r4=r4, bank=bank, pp=pp: e.activation(
                            out=yfh[:, 4 * r4:4 * r4 + 4, pp, :],
                            in_=ps[bank][0:P, 0:512].rearrange("p (a b) -> p a b", a=4),
                            func=AF.Copy, bias=0.0)),
                            reads=(f"ps{bank}",), writes=(f"o{ei}yfh{pp}",))
                        pb.op("vector", (lambda e, r4=r4, bank=bank, pp=pp: e.tensor_tensor(
                            out=yfl[:, 4 * r4:4 * r4 + 4, pp, :],
                            in0=ps[bank][0:P, 0:512].rearrange("p (a b) -> p a b", a=4),
                            in1=yfh[:, 4 * r4:4 * r4 + 4, pp, :], op=Op.subtract)),
                            reads=(f"ps{bank}", f"o{ei}yfh{pp}"), writes=(f"o{ei}yfl{pp}",))
                    if pp == 1:
                        for oc in range(DC):
                            bank = 4 + (oc % 4) // 2
                            slot = oc % 2
                            def omm(e, oc=oc, bank=bank, slot=slot):
                                out = ps[bank][0:P, slot * 256:slot * 256 + 256]
                                mvh = yfh.rearrange("p a b c -> p a (b c)")
                                mvl = yfl.rearrange("p a b c -> p a (b c)")
                                for dic in range(DINC):
                                    e.matmul(out, owhi[:, dic, oc * 128:(oc + 1) * 128],
                                             mvh[:, dic, :], start=(dic == 0), stop=False)
                                    e.matmul(out, owhi[:, dic, oc * 128:(oc + 1) * 128],
                                             mvl[:, dic, :], start=False, stop=False)
                                    ins = e.matmul(out,
                                                   owlo[:, dic, oc * 128:(oc + 1) * 128],
                                                   mvh[:, dic, :], start=False,
                                                   stop=(dic == DINC - 1))
                                return ins
                            pb.op("tensor", omm,
                                  reads=(f"o{ei}owhi", f"o{ei}owlo", f"o{ei}yfh0",
                                         f"o{ei}yfh1", f"o{ei}yfl0", f"o{ei}yfl1"),
                                  writes=(f"ps{bank}",))
                            pb.op("vector", (lambda e, oc=oc, bank=bank, slot=slot:
                                             e.tensor_tensor(
                                                 out=xnew[:, oc, :],
                                                 in0=xnew[:, oc, :],
                                                 in1=ps[bank][0:P, slot * 256:
                                                              slot * 256 + 256],
                                                 op=Op.add)),
                                  reads=(f"ps{bank}", f"o{ei}xnew"),
                                  writes=(f"o{ei}xnew",))
                        for hf in range(2):
                            pb.op("gpsimd", (lambda e, t0=t0, hf=hf: e.dma_start(
                                out=fmv(x2)[:, :, t0 - CH + hf * CH:t0 + hf * CH],
                                in_=xnew[:, :, hf * CH:(hf + 1) * CH])),
                                reads=(f"o{ei}xnew",), writes=("dramx2",), dma=True)

                pb.barrier()

        if "F1" in STAGES:
            ffn_call("f1", xT, x1, 0.5, False)
        else:
            pb.op("sync", (lambda e: e.dma_start(out=x1[:, :], in_=xT[:, :])), dma=True)
            pb.barrier()

        src_final = x1
        if "M0" in STAGES:
            m0()
        if "M1" in STAGES:
            for ei in range(E):
                m1(ei)
                if "M2" in STAGES:
                    m2(ei)
            src_final = x2

        if "F2" in STAGES:
            ffn_call("f2", src_final, out, 0.5, True)
        else:
            pb.op("sync", (lambda e, s=src_final: e.dma_start(out=out[:, :], in_=s[:, :])),
                  dma=True, force_mark=True)
            pb.barrier()

        pb.resolve()
        semctx = {s: stack.enter_context(nc.semaphore(s)) for s in SEMS}
        with nc.Block() as block:
            @block.sync
            def _(e):
                pb.emit("sync", e, semctx)

            @block.tensor
            def _(e):
                pb.emit("tensor", e, semctx)

            @block.vector
            def _(e):
                pb.emit("vector", e, semctx)

            @block.scalar
            def _(e):
                pb.emit("scalar", e, semctx)

            @block.gpsimd
            def _(e):
                pb.emit("gpsimd", e, semctx)
                for s in SEMS:
                    if s.startswith("s_l") or s.startswith("s_g"):
                        if pb.final.get(s, 0) > 0:
                            e.wait_ge(semctx[s], pb.final[s])
    return nc


def kernel(L0_attn_x, L0_attn_router_w, L0_attn_in_w, L0_attn_conv_w,
           L0_attn_conv_b, L0_attn_dt_bias, L0_attn_A_log, L0_attn_D_skip,
           L0_attn_norm_w, L0_attn_out_w, L0_ffn_w1, L0_ffn_conv_w,
           L0_ffn_conv_b, L0_ffn_w2):
    x = np.asarray(L0_attn_x)
    shared = {
        "w1T": np.ascontiguousarray(np.asarray(L0_ffn_w1).T),
        "w2T": np.ascontiguousarray(np.asarray(L0_ffn_w2).T),
        "cwf": np.ascontiguousarray(np.asarray(L0_ffn_conv_w)[:, 0, :]),
        "cbf": np.ascontiguousarray(np.asarray(L0_ffn_conv_b)),
        "rwT": np.ascontiguousarray(np.asarray(L0_attn_router_w)),
        "inw": np.ascontiguousarray(np.asarray(L0_attn_in_w)),
        "cwm": np.ascontiguousarray(np.asarray(L0_attn_conv_w)),
        "cbm": np.ascontiguousarray(np.asarray(L0_attn_conv_b)),
        "dtb": np.ascontiguousarray(np.asarray(L0_attn_dt_bias)),
        "alog": np.ascontiguousarray(np.asarray(L0_attn_A_log)),
        "dskp": np.ascontiguousarray(np.asarray(L0_attn_D_skip)),
        "nrmw": np.ascontiguousarray(np.asarray(L0_attn_norm_w)),
        "outw": np.ascontiguousarray(np.asarray(L0_attn_out_w)),
    }
    nc = build_nc()
    in_maps = []
    for b in range(8):
        m = dict(shared)
        m["xT"] = np.ascontiguousarray(x[b].T)
        in_maps.append(m)
    res = run_bass_kernel_spmd(nc, in_maps, core_ids=list(range(8)))
    outs = [np.ascontiguousarray(r["out"].T) for r in res.results]
    return np.stack(outs, axis=0)


# version 19
# speedup vs baseline: 1.2514x; 1.0283x over previous
import contextlib
import numpy as np
import ml_dtypes
import concourse.bass as bass
import concourse.mybir as mybir
from concourse.bass_utils import run_bass_kernel_spmd
from concourse.alu_op_type import AluOpType as Op

F32 = mybir.dt.float32
BF16 = mybir.dt.bfloat16
F16 = mybir.dt.float16
F8 = mybir.dt.float8e4
AF = mybir.ActivationFunctionType
AX = mybir.AxisListType

P = 128
T = 4096
D_MODEL = 1024
DC = 8
H_FFN = 4096
HC = 32
E = 4
D_INNER = 2048
DINC = 16
NH = 32
HD = 64
CONV_DIM = 2304
CVC = 18
D_IN_PROJ = 4384
NST = 128
CH = 128
NCH = T // CH
FPH = 256
NFPH = T // FPH
MG = 1536.0
NEG = -1e30

ENGINES = ("sync", "tensor", "vector", "scalar", "gpsimd")
SEM_OF = {"tensor": "s_te", "vector": "s_ve", "scalar": "s_ac", "gpsimd": "s_gp"}
DMA_POOL = {"sync": ("s_ld0", "s_ld1", "s_ld2", "s_ld3"),
            "gpsimd": ("s_gd0", "s_gd1")}
SEMS = ("s_ld0", "s_ld1", "s_ld2", "s_ld3", "s_gd0", "s_gd1",
        "s_te", "s_ve", "s_ac", "s_gp")


class Mark:
    __slots__ = ("sem", "seq", "val", "used", "step", "vc")

    def __init__(self, sem, seq, step, vc):
        self.sem, self.seq, self.step, self.vc = sem, seq, step, vc
        self.val = None
        self.used = False


class _NoMark:
    used = False
    val = None


_NOMARK = _NoMark()


class PB:

    def __init__(self):
        self.prog = []
        self.seq = {s: 0 for s in SEMS}
        self.vc = {e: {} for e in ENGINES}
        self.bufs = {}
        self._marks = {s: [] for s in SEMS}
        self.dmacnt = {q: 0 for q in DMA_POOL}

    def _buf(self, name):
        b = self.bufs.get(name)
        if b is None:
            b = {"w": None, "r": []}
            self.bufs[name] = b
        return b

    def op(self, eng, fn, reads=(), writes=(), dma=False, force_mark=False):
        need = {}
        if dma:
            qsem = DMA_POOL[eng][self.dmacnt[eng] % len(DMA_POOL[eng])]
            self.dmacnt[eng] += 1
            if self.seq[qsem] > 0:
                need[qsem] = self.seq[qsem] - 1
        for r in reads:
            b = self._buf(r)
            if b["w"] is not None:
                m = b["w"]
                if need.get(m.sem, -1) < m.seq:
                    need[m.sem] = m.seq
        for w in writes:
            b = self._buf(w)
            if b["w"] is not None:
                m = b["w"]
                if need.get(m.sem, -1) < m.seq:
                    need[m.sem] = m.seq
            for m in b["r"]:
                if need.get(m.sem, -1) < m.seq:
                    need[m.sem] = m.seq
        waits = []
        vc = self.vc[eng]
        for sem, seqv in need.items():
            if vc.get(sem, -1) >= seqv:
                continue
            m = self._marks[sem][seqv]
            m.used = True
            waits.append(m)
            for s2, q2 in m.vc.items():
                if vc.get(s2, -1) < q2:
                    vc[s2] = q2
            vc[sem] = max(vc.get(sem, -1), seqv)
        sem = qsem if dma else SEM_OF[eng]
        myvc = dict(vc)
        mark = Mark(sem, self.seq[sem], 16 if dma else 1, myvc)
        myvc[sem] = max(myvc.get(sem, -1), mark.seq)
        if force_mark or dma:
            mark.used = True
        self._marks[sem].append(mark)
        self.seq[sem] += 1
        for w in writes:
            b = self._buf(w)
            b["w"] = mark
            b["r"] = []
        for r in reads:
            self._buf(r)["r"].append(mark)
        self.prog.append((eng, fn, mark, waits))
        return mark

    def barrier(self):
        for eng in ENGINES:
            waits = []
            vc = self.vc[eng]
            for sem in SEMS:
                seqv = self.seq[sem] - 1
                if seqv < 0 or vc.get(sem, -1) >= seqv:
                    continue
                m = self._marks[sem][seqv]
                m.used = True
                waits.append(m)
                for s2, q2 in m.vc.items():
                    if vc.get(s2, -1) < q2:
                        vc[s2] = q2
                vc[sem] = seqv
            if waits:
                self.prog.append((eng, None, _NOMARK, waits))

    def resolve(self):
        self.final = {}
        for sem, marks in self._marks.items():
            c = 0
            for m in marks:
                if m.used:
                    c += m.step
                    m.val = c
            self.final[sem] = c

    def emit(self, eng_name, e, sems):
        for eng, fn, mark, waits in self.prog:
            if eng != eng_name:
                continue
            for m in waits:
                e.wait_ge(sems[m.sem], m.val)
            if fn is None:
                continue
            ins = fn(e)
            if mark.used:
                assert ins is not None, f"op on {eng} needs a sync-bearing instr"
                ins.then_inc(sems[mark.sem], mark.step)


STAGES = ("F1", "M0", "M1", "M2", "F2")
DEBUG_FFN = False


def set_T(t):
    global T, NCH, NFPH
    T = t
    NCH = T // CH
    NFPH = T // FPH


def build_nc():
    nc = bass.Bass("TRN2", target_bir_lowering=False)
    pb = PB()

    xT = nc.dram_tensor("xT", [D_MODEL, T], F32, kind="ExternalInput").ap()
    w1T = nc.dram_tensor("w1T", [D_MODEL, H_FFN], F32, kind="ExternalInput").ap()
    w2T = nc.dram_tensor("w2T", [H_FFN, D_MODEL], F32, kind="ExternalInput").ap()
    cwf = nc.dram_tensor("cwf", [H_FFN, 3], F32, kind="ExternalInput").ap()
    cbf = nc.dram_tensor("cbf", [H_FFN], F32, kind="ExternalInput").ap()
    rwT = nc.dram_tensor("rwT", [D_MODEL, E], F32, kind="ExternalInput").ap()
    inw = nc.dram_tensor("inw", [E, D_MODEL, D_IN_PROJ], F32, kind="ExternalInput").ap()
    cwm = nc.dram_tensor("cwm", [E, CONV_DIM, 4], F32, kind="ExternalInput").ap()
    cbm = nc.dram_tensor("cbm", [E, CONV_DIM], F32, kind="ExternalInput").ap()
    dtb = nc.dram_tensor("dtb", [E, NH], F32, kind="ExternalInput").ap()
    alog = nc.dram_tensor("alog", [E, NH], F32, kind="ExternalInput").ap()
    dskp = nc.dram_tensor("dskp", [E, NH], F32, kind="ExternalInput").ap()
    nrmw = nc.dram_tensor("nrmw", [E, D_INNER], F32, kind="ExternalInput").ap()
    outw = nc.dram_tensor("outw", [E, D_INNER, D_MODEL], F32, kind="ExternalInput").ap()
    out = nc.dram_tensor("out", [D_MODEL, T], F32, kind="ExternalOutput").ap()
    if DEBUG_FFN:
        dxq = nc.dram_tensor("dxq", [P, DC, 258], F32, kind="ExternalOutput").ap()
        dcs = nc.dram_tensor("dcs", [P, HC, 256], F32, kind="ExternalOutput").ap()
        dhq = nc.dram_tensor("dhq", [P, HC, 256], F32, kind="ExternalOutput").ap()
        dhh = nc.dram_tensor("dhh", [P, 4, 258], F32, kind="ExternalOutput").ap()
        drw = nc.dram_tensor("drw", [8, T], F32, kind="ExternalOutput").ap()
        dsw = nc.dram_tensor("dsw", [1, 4], F32, kind="ExternalOutput").ap()

    x1 = nc.dram_tensor("x1", [D_MODEL, T], F32, kind="Internal").ap()
    x2 = nc.dram_tensor("x2", [D_MODEL, T], F32, kind="Internal").ap()
    ln2hi = nc.dram_tensor("ln2hi", [D_MODEL, T], BF16, kind="Internal").ap()
    ln2lo = nc.dram_tensor("ln2lo", [D_MODEL, T], BF16, kind="Internal").ap()
    zx = nc.dram_tensor("zx", [35 * P, T], F32, kind="Internal").ap()
    dtsfl = nc.dram_tensor("dtsfl", [P, T], F32, kind="Internal").ap()
    decfl = nc.dram_tensor("decfl", [NCH, NH], F32, kind="Internal").ap()
    rowfl = nc.dram_tensor("rowfl", [8, T], F32, kind="Internal").ap()

    def fmv(ap):
        return ap.rearrange("(c p) t -> p c t", p=P)

    stack = contextlib.ExitStack()
    with stack:
        ident = stack.enter_context(nc.sbuf_tensor("ident", [P, P], F32))
        masktile = stack.enter_context(nc.sbuf_tensor("masktile", [P, P], F32))
        onescol = stack.enter_context(nc.sbuf_tensor("onescol", [P, 1], F32))
        onesrow = stack.enter_context(nc.sbuf_tensor("onesrow", [1, P], F32))
        r_sb = stack.enter_context(nc.sbuf_tensor("r_sb", [P, NCH * E], F32))
        ps = [stack.enter_context(nc.psum_tensor(f"ps{i}", [P, 512], F32))
              for i in range(8)]

        pb.op("gpsimd", lambda e: e.memset(ident[:], 0.0), writes=("ident",))
        pb.op("gpsimd", (lambda e: e.affine_select(
            out=ident[:], in_=ident[:], compare_op=Op.not_equal, fill=1.0,
            base=0, pattern=[[-1, P]], channel_multiplier=1)),
            reads=("ident",), writes=("ident",))
        pb.op("gpsimd", lambda e: e.memset(masktile[:], 0.0), writes=("masktile",))
        pb.op("gpsimd", (lambda e: e.affine_select(
            out=masktile[:], in_=masktile[:], compare_op=Op.is_ge, fill=NEG,
            base=0, pattern=[[1, P]], channel_multiplier=-1)),
            reads=("masktile",), writes=("masktile",))
        pb.op("gpsimd", lambda e: e.memset(onescol[:], 1.0), writes=("onescol",))
        pb.op("gpsimd", lambda e: e.memset(onesrow[:], 1.0), writes=("onesrow",))
        epscol = stack.enter_context(nc.sbuf_tensor("epscol", [P, 1], F32))
        pb.op("gpsimd", lambda e: e.memset(epscol[:], 1e-5), writes=("epscol",))

        def bcast(tag2, rowap, n, bank, reads):
            pb.op("tensor", (lambda e, r=rowap, n=n, bank=bank: e.matmul(
                ps[bank][0:P, 0:n], onesrow[:], r, start=True, stop=True)),
                reads=tuple(reads) + ("onesrow",), writes=(f"ps{bank}",))

        trow = stack.enter_context(nc.sbuf_tensor("trow", [8, P], F32))

        def amax_row(tag2, absbuf, absname, nch, n, rowslice, rowname, psA, psB):
            w = nch // 2
            while w >= 1:
                pb.op("vector", (lambda e, w=w: e.tensor_tensor(
                    out=absbuf[:, 0:w, 0:n], in0=absbuf[:, 0:w, 0:n],
                    in1=absbuf[:, w:2 * w, 0:n], op=Op.max)),
                    reads=(absname,), writes=(absname,))
                w //= 2
            ng = n // P
            def trs(e):
                for g in range(ng):
                    ins = e.transpose(ps[psA][0:P, g * P:(g + 1) * P],
                                      absbuf[:, 0, g * P:(g + 1) * P], ident[:])
                return ins
            pb.op("tensor", trs, reads=(absname, "ident"), writes=(f"ps{psA}",))
            def reds(e):
                for g in range(ng):
                    ins = e.tensor_reduce(out=absbuf[0:P, 0, g:g + 1],
                                          in_=ps[psA][0:P, g * P:(g + 1) * P],
                                          axis=AX.X, op=Op.max)
                return ins
            pb.op("vector", reds, reads=(f"ps{psA}",), writes=(absname,))
            pb.op("tensor", (lambda e: e.transpose(ps[psB][0:ng, 0:P],
                                                   absbuf[:, 0, 0:ng], ident[:])),
                  reads=(absname, "ident"), writes=(f"ps{psB}",))
            pb.op("vector", (lambda e: e.tensor_copy(trow[0:ng, :], ps[psB][0:ng, 0:P])),
                  reads=(f"ps{psB}",), writes=("trow",))
            pb.op("gpsimd", (lambda e: e.dma_start(out=rowslice, in_=trow[0:ng, :])),
                  reads=("trow",), writes=(rowname,), dma=True)

        def ffn_call(tag, src, dst, half, is_out):
            rowsd = f"{tag}rowsd"
            with contextlib.ExitStack() as st1:
                w1q = st1.enter_context(nc.sbuf_tensor(f"{tag}w1q", [P, DC, H_FFN], F8))
                w2q = st1.enter_context(nc.sbuf_tensor(f"{tag}w2q", [P, HC, D_MODEL], F8))
                cw = st1.enter_context(nc.sbuf_tensor(f"{tag}cw", [P, HC, 3], F32))
                cb = st1.enter_context(nc.sbuf_tensor(f"{tag}cb", [P, HC], F32))
                swr = st1.enter_context(nc.sbuf_tensor(f"{tag}swr", [1, 4], F32))

                with contextlib.ExitStack() as st2:
                    wbuf = st2.enter_context(
                        nc.sbuf_tensor(f"{tag}wbuf", [P, 2, 4096], F32))
                    wtmp = st2.enter_context(nc.sbuf_tensor(f"{tag}wtmp", [1 * P, 4096], F16))
                    wsc = st2.enter_context(nc.sbuf_tensor(f"{tag}wsc", [P, 4], F32))
                    for wi, (wsrc, nchunk, wid) in enumerate(
                            ((w1T, DC, H_FFN), (w2T, HC, D_MODEL))):
                        wn = f"{tag}w1q" if wi == 0 else f"{tag}w2q"
                        wdst = w1q if wi == 0 else w2q
                        view = wsrc.rearrange("(c p) w -> p c w", p=P)
                        npc = 4096 // wid
                        npieces = nchunk // npc
                        for pc in range(npieces):
                            pp = pc % 2
                            bw = f"{tag}wbuf{pp}"
                            pb.op("sync", (lambda e, pc=pc, pp=pp, v=view, npc=npc, w=wid:
                                           e.dma_start(out=wbuf[:, pp].rearrange(
                                               "p (a b) -> p a b", a=npc),
                                               in_=v[:, pc * npc:(pc + 1) * npc, :])),
                                  writes=(bw,), dma=True)
                            if pc == 0:
                                pb.op("vector", (lambda e, pp=pp: e.tensor_reduce(
                                    out=wsc[:, 0:1], in_=wbuf[:, pp], axis=AX.X,
                                    op=Op.add, apply_absolute_value=True)),
                                    reads=(bw,), writes=(f"{tag}wsc",))
                            else:
                                pb.op("vector", (lambda e, pp=pp: e.tensor_reduce(
                                    out=wsc[:, 1:2], in_=wbuf[:, pp], axis=AX.X,
                                    op=Op.add, apply_absolute_value=True)),
                                    reads=(bw,), writes=(f"{tag}wsc",))
                                pb.op("vector", (lambda e: e.tensor_tensor(
                                    out=wsc[:, 0:1], in0=wsc[:, 0:1], in1=wsc[:, 1:2],
                                    op=Op.add)), reads=(f"{tag}wsc",), writes=(f"{tag}wsc",))
                        pb.op("tensor", (lambda e: e.matmul(
                            ps[7][0:1, 0:1], wsc[:, 0:1], onescol[:], start=True, stop=True)),
                            reads=(f"{tag}wsc", "onescol"), writes=("ps7",))
                        pb.op("vector", (lambda e, wi=wi, n=nchunk, w=wid: e.tensor_scalar(
                            out=swr[0:1, wi + 2:wi + 3], in0=ps[7][0:1, 0:1],
                            scalar1=1.0 / (n * P * w), scalar2=1e-5,
                            op0=Op.mult, op1=Op.max)),
                            reads=("ps7",), writes=(f"{tag}swr",))
                        pb.op("vector", (lambda e, wi=wi: e.reciprocal(
                            out=swr[0:1, wi:wi + 1], in_=swr[0:1, wi + 2:wi + 3])),
                            reads=(f"{tag}swr",), writes=(f"{tag}swr",))
                        pb.op("tensor", (lambda e, wi=wi: e.matmul(
                            ps[7][0:P, 1:2], onesrow[:], swr[0:1, wi:wi + 1],
                            start=True, stop=True)),
                            reads=(f"{tag}swr", "onesrow"), writes=("ps7",))
                        pb.op("vector", (lambda e: e.tensor_copy(
                            wsc[:, 3:4], ps[7][0:P, 1:2])),
                            reads=("ps7",), writes=(f"{tag}wsc",))
                        for pc in range(npieces):
                            pp = pc % 2
                            bw = f"{tag}wbuf{pp}"
                            pb.op("sync", (lambda e, pc=pc, pp=pp, v=view, npc=npc, w=wid:
                                           e.dma_start(out=wbuf[:, pp].rearrange(
                                               "p (a b) -> p a b", a=npc),
                                               in_=v[:, pc * npc:(pc + 1) * npc, :])),
                                  writes=(bw,), dma=True)
                            pb.op("vector", (lambda e, pp=pp: e.tensor_scalar(
                                out=wbuf[:, pp], in0=wbuf[:, pp], scalar1=wsc[:, 3:4],
                                scalar2=1.4999, op0=Op.mult, op1=Op.min)),
                                reads=(bw, f"{tag}wsc"), writes=(bw,))
                            pb.op("vector", (lambda e, pp=pp: e.tensor_scalar(
                                out=wtmp[:, :], in0=wbuf[:, pp], scalar1=-1.4999,
                                scalar2=MG, op0=Op.max, op1=Op.add)),
                                reads=(bw,), writes=(f"{tag}wtmp",))
                            pb.op("scalar", (lambda e, wd=wdst, pc=pc, npc=npc, w=wid:
                                             e.activation(out=wd[:, pc * npc:(pc + 1) * npc, :]
                                                          .rearrange("p a b -> p (a b)"),
                                                          in_=wtmp[:, :], func=AF.Copy,
                                                          bias=-MG)),
                                  reads=(f"{tag}wtmp",), writes=(wn,))
                    pb.barrier()

                pb.op("sync", (lambda e: e.dma_start(
                    out=cw[:, :, :], in_=cwf.rearrange("(c p) k -> p c k", p=P))),
                    writes=(f"{tag}cw",), dma=True)
                pb.op("sync", (lambda e: e.dma_start(
                    out=cb[:, :], in_=cbf.rearrange("(c p) -> p c", p=P),
                    allow_slow_non_contiguous=True)),
                    writes=(f"{tag}cb",), dma=True)

                with contextlib.ExitStack() as st2:
                    xp = st2.enter_context(nc.sbuf_tensor(f"{tag}xp", [P, 2, DC, 512], F32))
                    sq = st2.enter_context(nc.sbuf_tensor(f"{tag}sq", [P, DC, 512], F32))
                    rt = st2.enter_context(nc.sbuf_tensor(f"{tag}rt", [1, 8, 512], F32))
                    mbc = st2.enter_context(nc.sbuf_tensor(f"{tag}mbc", [P, 512], F32))
                    for i in range(T // 512):
                        pp = i % 2
                        bx = f"{tag}xp{pp}"
                        sl = slice(i * 512, (i + 1) * 512)
                        pb.op("sync", (lambda e, sl=sl, pp=pp: e.dma_start(
                            out=xp[:, pp], in_=fmv(src)[:, :, sl])),
                            writes=(bx,), dma=True)
                        pb.op("scalar", (lambda e, pp=pp: e.activation(
                            out=sq[:], in_=xp[:, pp], func=AF.Square)),
                            reads=(bx,), writes=(f"{tag}sq",))
                        def mmsum(e, pp=pp):
                            for dc in range(DC):
                                ins = e.matmul(ps[6][0:1, 0:512], onescol[:],
                                               xp[:, pp, dc], start=(dc == 0),
                                               stop=(dc == DC - 1))
                            return ins
                        pb.op("tensor", mmsum, reads=(bx, "onescol"), writes=("ps6",))
                        def mmsq(e):
                            for dc in range(DC):
                                ins = e.matmul(ps[7][0:1, 0:512], onescol[:],
                                               sq[:, dc], start=(dc == 0),
                                               stop=(dc == DC - 1))
                            return ins
                        pb.op("tensor", mmsq, reads=(f"{tag}sq", "onescol"), writes=("ps7",))
                        pb.op("vector", (lambda e: e.tensor_scalar(
                            out=rt[0:1, 4], in0=ps[6][0:1, 0:512],
                            scalar1=1.0 / D_MODEL, scalar2=None, op0=Op.mult, op1=Op.bypass)),
                            reads=("ps6",), writes=(f"{tag}rt",))
                        pb.op("vector", (lambda e: e.tensor_tensor(
                            out=rt[0:1, 6], in0=rt[0:1, 4], in1=rt[0:1, 4], op=Op.mult)),
                            reads=(f"{tag}rt",), writes=(f"{tag}rt",))
                        pb.op("vector", (lambda e: e.scalar_tensor_tensor(
                            out=rt[0:1, 5], in0=ps[7][0:1, 0:512], scalar=1.0 / D_MODEL,
                            in1=rt[0:1, 6], op0=Op.mult, op1=Op.subtract)),
                            reads=("ps7", f"{tag}rt"), writes=(f"{tag}rt",))
                        pb.op("scalar", (lambda e: e.activation(
                            out=rt[0:1, 6], in_=rt[0:1, 5], func=AF.Ln,
                            bias=epscol[0:1, 0:1])),
                            reads=(f"{tag}rt", "epscol"), writes=(f"{tag}rt",))
                        pb.op("scalar", (lambda e: e.activation(
                            out=rt[0:1, 0], in_=rt[0:1, 6], func=AF.Exp, scale=-0.5)),
                            reads=(f"{tag}rt",), writes=(f"{tag}rt",))
                        pb.op("vector", (lambda e: e.scalar_tensor_tensor(
                            out=rt[0:1, 1], in0=rt[0:1, 4], scalar=-1.0,
                            in1=rt[0:1, 0], op0=Op.mult, op1=Op.mult)),
                            reads=(f"{tag}rt",), writes=(f"{tag}rt",))
                        bcast(tag, rt[0:1, 4], 512, 6, (f"{tag}rt",))
                        pb.op("vector", (lambda e: e.tensor_copy(mbc[:], ps[6][0:P, 0:512])),
                              reads=("ps6",), writes=(f"{tag}mbc",))
                        def subm(e, pp=pp):
                            rep = mbc[:, :].unsqueeze(1).broadcast_to((P, DC, 512))
                            return e.tensor_tensor(out=sq[:], in0=xp[:, pp], in1=rep,
                                                   op=Op.subtract)
                        pb.op("vector", subm, reads=(bx, f"{tag}mbc"), writes=(f"{tag}sq",))
                        pb.op("scalar", (lambda e: e.activation(out=sq[:], in_=sq[:],
                                                                func=AF.Abs)),
                              reads=(f"{tag}sq",), writes=(f"{tag}sq",))
                        amax_row(tag, sq, f"{tag}sq", DC, 512, rt[0:1, 7], f"{tag}rt", 6, 7)
                        pb.op("vector", (lambda e: e.scalar_tensor_tensor(
                            out=rt[0:1, 6], in0=rt[0:1, 7], scalar=1e-30,
                            in1=rt[0:1, 0], op0=Op.max, op1=Op.mult)),
                            reads=(f"{tag}rt",), writes=(f"{tag}rt",))
                        pb.op("vector", (lambda e: e.tensor_scalar(
                            out=rt[0:1, 6], in0=rt[0:1, 6], scalar1=1e-5,
                            scalar2=None, op0=Op.max, op1=Op.bypass)),
                            reads=(f"{tag}rt",), writes=(f"{tag}rt",))
                        pb.op("vector", (lambda e: e.reciprocal(
                            out=rt[0:1, 7], in_=rt[0:1, 6])),
                            reads=(f"{tag}rt",), writes=(f"{tag}rt",))
                        pb.op("vector", (lambda e: e.tensor_scalar(
                            out=rt[0:1, 2], in0=rt[0:1, 7], scalar1=127.0, scalar2=None,
                            op0=Op.mult, op1=Op.bypass)),
                            reads=(f"{tag}rt",), writes=(f"{tag}rt",))
                        pb.op("vector", (lambda e: e.tensor_scalar(
                            out=rt[0:1, 3], in0=rt[0:1, 6], scalar1=1.0 / 127.0,
                            scalar2=None, op0=Op.mult, op1=Op.bypass)),
                            reads=(f"{tag}rt",), writes=(f"{tag}rt",))
                        pb.op("gpsimd", (lambda e, sl=sl: e.dma_start(
                            out=rowfl[0:4, sl], in_=rt[0:1, 0:4, :])),
                            reads=(f"{tag}rt",), writes=(rowsd,), dma=True)
                    pb.barrier()

                with contextlib.ExitStack() as st2:
                    x0t = st2.enter_context(nc.sbuf_tensor(f"{tag}x0t", [P, 2, DC, 258], F32))
                    xoln = st2.enter_context(nc.sbuf_tensor(f"{tag}xoln", [P, DC, 258], F32))
                    q16 = st2.enter_context(nc.sbuf_tensor(f"{tag}q16", [P, DC, 258], F16))
                    xq = st2.enter_context(nc.sbuf_tensor(f"{tag}xq", [P, DC, 258], BF16))
                    hsb = st2.enter_context(nc.sbuf_tensor(f"{tag}hsb", [P, 2, 258], F32))
                    csb = st2.enter_context(nc.sbuf_tensor(f"{tag}csb", [P, HC, 256], F32))
                    h16 = st2.enter_context(nc.sbuf_tensor(f"{tag}h16", [P, HC, 256], F16))
                    amxb = st2.enter_context(nc.sbuf_tensor(f"{tag}amxb", [P, HC, 256], F32))
                    hq = st2.enter_context(nc.sbuf_tensor(f"{tag}hq", [P, HC, 256], BF16))
                    sbc = st2.enter_context(nc.sbuf_tensor(f"{tag}sbc", [P, 3, 258], F32))
                    rw = st2.enter_context(nc.sbuf_tensor(f"{tag}rw", [1, 4, 258], F32))
                    rr = st2.enter_context(nc.sbuf_tensor(f"{tag}rr", [1, 2, 258], F32))

                    pb.op("gpsimd", (lambda e: e.memset(x0t[:], 0.0)),
                          writes=(f"{tag}x0t0", f"{tag}x0t1"))
                    pb.op("gpsimd", (lambda e: e.memset(rr[:], 0.0)),
                          writes=(f"{tag}rr",))
                    pb.op("gpsimd", (lambda e: e.memset(rw[:], 0.0)),
                          writes=(f"{tag}rw",))

                    for i in range(NFPH):
                        t0 = i * FPH
                        w0 = t0 - 1
                        lo, hi = max(w0, 0), min(t0 + FPH + 1, T)
                        off = lo - w0
                        pp = i % 2
                        bx = f"{tag}x0t{pp}"
                        pb.op("sync", (lambda e, lo=lo, hi=hi, off=off, pp=pp:
                                       e.dma_start(out=x0t[:, pp, :, off:off + (hi - lo)],
                                                   in_=fmv(src)[:, :, lo:hi])),
                              writes=(bx,), dma=True)
                        pb.op("sync", (lambda e, lo=lo, hi=hi, off=off: e.dma_start(
                            out=rw[0:1, :, off:off + (hi - lo)], in_=rowfl[0:4, lo:hi])),
                            reads=(rowsd,), writes=(f"{tag}rw",), dma=True)
                        if off == 1:
                            pb.op("gpsimd", (lambda e: e.memset(rw[0:1, :, 0:1], 0.0)),
                                  writes=(f"{tag}rw",))
                        if hi == T:
                            pb.op("gpsimd", (lambda e: e.memset(rw[0:1, :, 257:258], 0.0)),
                                  writes=(f"{tag}rw",))
                        def mkrr(e):
                            e.tensor_tensor(out=rr[0:1, 0, :], in0=rw[0:1, 0, :],
                                            in1=rw[0:1, 2, :], op=Op.mult)
                            return e.tensor_tensor(out=rr[0:1, 1, :], in0=rw[0:1, 1, :],
                                                   in1=rw[0:1, 2, :], op=Op.mult)
                        pb.op("vector", mkrr, reads=(f"{tag}rw",), writes=(f"{tag}rr",))
                        bcast(tag, rr[0:1, 0, 0:258], 258, 0, (f"{tag}rr",))
                        bcast(tag, rr[0:1, 1, 0:258], 258, 1, (f"{tag}rr",))
                        def cpsbc(e):
                            e.tensor_copy(sbc[:, 0, 0:258], ps[0][0:P, 0:258])
                            return e.tensor_copy(sbc[:, 1, 0:258], ps[1][0:P, 0:258])
                        pb.op("vector", cpsbc, reads=("ps0", "ps1"), writes=(f"{tag}sbc",))
                        def qa(e, pp=pp):
                            rep = sbc[:, 0, 0:258].unsqueeze(1).broadcast_to((P, DC, 258))
                            return e.tensor_tensor(out=xoln[:], in0=x0t[:, pp], in1=rep,
                                                   op=Op.mult)
                        pb.op("vector", qa, reads=(bx, f"{tag}sbc"), writes=(f"{tag}xoln",))
                        def qb(e):
                            rep = sbc[:, 1, 0:258].unsqueeze(1).broadcast_to((P, DC, 258))
                            return e.scalar_tensor_tensor(out=q16[:], in0=xoln[:], scalar=MG,
                                                          in1=rep, op0=Op.add, op1=Op.add)
                        pb.op("vector", qb, reads=(f"{tag}xoln", f"{tag}sbc"),
                              writes=(f"{tag}q16",))
                        pb.op("scalar", (lambda e: e.activation(out=xq[:], in_=q16[:],
                                                                func=AF.Copy, bias=-MG)),
                              reads=(f"{tag}q16",), writes=(f"{tag}xq",))
                        if off == 1:
                            pb.op("gpsimd", (lambda e: e.memset(xq[:, :, 0:1], 0.0)),
                                  writes=(f"{tag}xq",))
                        if hi == T:
                            pb.op("gpsimd", (lambda e: e.memset(xq[:, :, 257:258], 0.0)),
                                  writes=(f"{tag}xq",))
                        pb.op("vector", (lambda e: e.tensor_scalar(
                            out=rr[0:1, 0, :], in0=rw[0:1, 3, :],
                            scalar1=swr[0:1, 2:3], scalar2=None,
                            op0=Op.mult, op1=Op.bypass)),
                            reads=(f"{tag}rw", f"{tag}swr"), writes=(f"{tag}rr",))
                        bcast(tag, rr[0:1, 0, 0:258], 258, 0, (f"{tag}rr",))
                        pb.op("vector", (lambda e: e.tensor_copy(sbc[:, 1, 0:258],
                                                                 ps[0][0:P, 0:258])),
                              reads=("ps0",), writes=(f"{tag}sbc",))

                        for hc in range(HC):
                            bank = 4 + (hc % 2)
                            hr = hc % 2
                            def mm1(e, hc=hc, bank=bank):
                                for dc in range(DC):
                                    ins = e.matmul(ps[bank][0:P, 0:258],
                                                   w1q[:, dc, hc * P:(hc + 1) * P],
                                                   xq[:, dc], start=(dc == 0),
                                                   stop=(dc == DC - 1))
                                return ins
                            bcs = f"{tag}csb{hr}"
                            pb.op("tensor", mm1, reads=(f"{tag}w1q", f"{tag}xq"),
                                  writes=(f"ps{bank}",))
                            pb.op("vector", (lambda e, hr=hr, bank=bank: e.tensor_tensor(
                                out=hsb[:, hr], in0=ps[bank][0:P, 0:258],
                                in1=sbc[:, 1, 0:258], op=Op.mult)),
                                reads=(f"ps{bank}", f"{tag}sbc"), writes=(f"{tag}hsb{hr}",))
                            pb.op("scalar", (lambda e, hc=hc, hr=hr: e.activation(
                                out=csb[:, hc], in_=hsb[:, hr, 0:256],
                                func=AF.Identity, scale=cw[:, hc, 0:1],
                                bias=cb[:, hc:hc + 1])),
                                reads=(f"{tag}hsb{hr}", f"{tag}cw", f"{tag}cb"),
                                writes=(bcs,))
                            pb.op("vector", (lambda e, hc=hc, hr=hr: e.scalar_tensor_tensor(
                                out=csb[:, hc], in0=hsb[:, hr, 1:257],
                                scalar=cw[:, hc, 1:2], in1=csb[:, hc],
                                op0=Op.mult, op1=Op.add)),
                                reads=(f"{tag}hsb{hr}", f"{tag}cw", bcs),
                                writes=(bcs,))
                            pb.op("vector", (lambda e, hc=hc, hr=hr: e.scalar_tensor_tensor(
                                out=csb[:, hc], in0=hsb[:, hr, 2:258],
                                scalar=cw[:, hc, 2:3], in1=csb[:, hc],
                                op0=Op.mult, op1=Op.add)),
                                reads=(f"{tag}hsb{hr}", f"{tag}cw", bcs),
                                writes=(bcs,))
                        pb.op("scalar", (lambda e: e.activation(
                            out=csb[:], in_=csb[:], func=AF.Silu)),
                            reads=(f"{tag}csb0", f"{tag}csb1"),
                            writes=(f"{tag}csb0", f"{tag}csb1"))
                        pb.op("scalar", (lambda e: e.activation(
                            out=amxb[:], in_=csb[:], func=AF.Abs)),
                            reads=(f"{tag}csb0", f"{tag}csb1"), writes=(f"{tag}amxb",))
                        w = HC // 2
                        while w >= 1:
                            pb.op("vector", (lambda e, w=w: e.tensor_tensor(
                                out=amxb[:, 0:w, :], in0=amxb[:, 0:w, :],
                                in1=amxb[:, w:2 * w, :], op=Op.max)),
                                reads=(f"{tag}amxb",), writes=(f"{tag}amxb",))
                            w //= 2
                        amax_row(tag, amxb, f"{tag}amxb", 1, 256, rr[0:1, 0, 0:256],
                                 f"{tag}rr", 2, 3)
                        pb.op("vector", (lambda e: e.tensor_scalar(
                            out=rr[0:1, 0, 0:256], in0=rr[0:1, 0, 0:256], scalar1=1e-5,
                            scalar2=None, op0=Op.max, op1=Op.bypass)),
                            reads=(f"{tag}rr",), writes=(f"{tag}rr",))
                        pb.op("vector", (lambda e: e.tensor_scalar(
                            out=rr[0:1, 1, 0:256], in0=rr[0:1, 0, 0:256],
                            scalar1=swr[0:1, 3:4], scalar2=half / 127.0,
                            op0=Op.mult, op1=Op.mult)),
                            reads=(f"{tag}rr", f"{tag}swr"), writes=(f"{tag}rr",))
                        pb.op("vector", (lambda e: e.reciprocal(
                            out=rr[0:1, 0, 0:256], in_=rr[0:1, 0, 0:256])),
                            reads=(f"{tag}rr",), writes=(f"{tag}rr",))
                        pb.op("vector", (lambda e: e.tensor_scalar(
                            out=rr[0:1, 0, 0:256], in0=rr[0:1, 0, 0:256], scalar1=127.0,
                            scalar2=None, op0=Op.mult, op1=Op.bypass)),
                            reads=(f"{tag}rr",), writes=(f"{tag}rr",))
                        bcast(tag, rr[0:1, 0, 0:256], 256, 0, (f"{tag}rr",))
                        bcast(tag, rr[0:1, 1, 0:256], 256, 1, (f"{tag}rr",))
                        def cph(e):
                            e.tensor_copy(sbc[:, 1, 0:256], ps[0][0:P, 0:256])
                            return e.tensor_copy(sbc[:, 2, 0:256], ps[1][0:P, 0:256])
                        pb.op("vector", cph, reads=("ps0", "ps1"), writes=(f"{tag}sbc",))
                        pb.op("vector", (lambda e: e.tensor_tensor(
                            out=amxb[:], in0=csb[:],
                            in1=sbc[:, 1, 0:256].unsqueeze(1).broadcast_to((P, HC, 256)),
                            op=Op.mult)),
                            reads=(f"{tag}csb0", f"{tag}csb1", f"{tag}sbc"),
                            writes=(f"{tag}amxb",))
                        pb.op("scalar", (lambda e: e.activation(
                            out=h16[:], in_=amxb[:], func=AF.Copy, bias=MG)),
                            reads=(f"{tag}amxb",), writes=(f"{tag}h16",))
                        pb.op("scalar", (lambda e: e.activation(
                            out=hq[:, :, :], in_=h16[:], func=AF.Copy, bias=-MG)),
                            reads=(f"{tag}h16",), writes=(f"{tag}hq",))
                        for oc in range(DC):
                            bank = 4 + (oc % 2)
                            def mm2(e, oc=oc, bank=bank):
                                for hc2 in range(HC):
                                    ins = e.matmul(ps[bank][0:P, 0:256],
                                                   w2q[:, hc2, oc * P:(oc + 1) * P],
                                                   hq[:, hc2], start=(hc2 == 0),
                                                   stop=(hc2 == HC - 1))
                                return ins
                            pb.op("tensor", mm2, reads=(f"{tag}w2q", f"{tag}hq"),
                                  writes=(f"ps{bank}",))
                            pb.op("vector", (lambda e, oc=oc, bank=bank: e.tensor_tensor(
                                out=xoln[:, oc, 0:256], in0=ps[bank][0:P, 0:256],
                                in1=sbc[:, 2, 0:256], op=Op.mult)),
                                reads=(f"ps{bank}", f"{tag}sbc"), writes=(f"{tag}xoln",))
                        pb.op("vector", (lambda e, pp=pp: e.tensor_tensor(
                            out=xoln[:, :, 0:256], in0=xoln[:, :, 0:256],
                            in1=x0t[:, pp, :, 1:257], op=Op.add)),
                            reads=(f"{tag}xoln", bx), writes=(f"{tag}xoln",))
                        pb.op("gpsimd", (lambda e, t0=t0: e.dma_start(
                            out=fmv(dst)[:, :, t0:t0 + FPH], in_=xoln[:, :, 0:256])),
                            reads=(f"{tag}xoln",), writes=(f"dram{id(dst)}",),
                            dma=True, force_mark=is_out)
                        if DEBUG_FFN and tag == "f1" and i == 0:
                            pb.op("gpsimd", (lambda e: e.dma_start(out=dxq[:, :, :], in_=xq[:])),
                                  reads=(f"{tag}xq",), dma=True, force_mark=True)
                            pb.op("gpsimd", (lambda e: e.dma_start(out=dcs[:, :, :], in_=csb[:])),
                                  reads=(f"{tag}csb0", f"{tag}csb1"), dma=True,
                                  force_mark=True)
                            pb.op("gpsimd", (lambda e: e.dma_start(out=dhq[:, :, :], in_=hq[:])),
                                  reads=(f"{tag}hq",), dma=True, force_mark=True)
                            pb.op("gpsimd", (lambda e: e.dma_start(out=dhh[:, :, :], in_=hsb[:])),
                                  reads=(f"{tag}hsb0", f"{tag}hsb1", f"{tag}hsb2", f"{tag}hsb3"),
                                  dma=True, force_mark=True)
                            pb.op("gpsimd", (lambda e: e.dma_start(out=dsw[:, :], in_=swr[:])),
                                  reads=(f"{tag}swr",), dma=True, force_mark=True)
                    if DEBUG_FFN and tag == "f1":
                        pb.op("gpsimd", (lambda e: e.dma_start(out=drw[:, :], in_=rowfl[:, :])),
                              reads=(rowsd,), dma=True, force_mark=True)
                    pb.barrier()

        def m0():
            with contextlib.ExitStack() as st1:
                xp = st1.enter_context(nc.sbuf_tensor("m0xp", [P, 2, DC, 512], F32))
                sq = st1.enter_context(nc.sbuf_tensor("m0sq", [P, DC, 512], F32))
                rt = st1.enter_context(nc.sbuf_tensor("m0rt", [1, 4, 512], F32))
                bc2 = st1.enter_context(nc.sbuf_tensor("m0bc", [P, 2, 512], F32))
                lnf = st1.enter_context(nc.sbuf_tensor("m0lnf", [P, DC, 512], F32))
                lhi = st1.enter_context(nc.sbuf_tensor("m0lhi", [P, DC, 512], BF16))
                llo = st1.enter_context(nc.sbuf_tensor("m0llo", [P, DC, 512], BF16))
                rwh = st1.enter_context(nc.sbuf_tensor("m0rwh", [P, DC, E], BF16))
                rwl = st1.enter_context(nc.sbuf_tensor("m0rwl", [P, DC, E], BF16))
                rwf = st1.enter_context(nc.sbuf_tensor("m0rwf", [P, DC, E], F32))
                sm = st1.enter_context(nc.sbuf_tensor("m0sm", [P, E], F32))
                smc = st1.enter_context(nc.sbuf_tensor("m0smc", [P, 2], F32))
                pb.op("sync", (lambda e: e.dma_start(
                    out=rwf[:, :, :], in_=rwT.rearrange("(c p) e -> p c e", p=P))),
                    writes=("m0rwf",), dma=True)
                pb.op("vector", (lambda e: e.tensor_copy(rwh[:], rwf[:])),
                      reads=("m0rwf",), writes=("m0rwh",))
                pb.op("vector", (lambda e: e.tensor_tensor(
                    out=rwl[:], in0=rwf[:], in1=rwh[:], op=Op.subtract)),
                    reads=("m0rwf", "m0rwh"), writes=("m0rwl",))
                for i in range(T // 512):
                    pp = i % 2
                    bx = f"m0xp{pp}"
                    sl = slice(i * 512, (i + 1) * 512)
                    pb.op("sync", (lambda e, sl=sl, pp=pp: e.dma_start(
                        out=xp[:, pp], in_=fmv(x1)[:, :, sl])),
                        reads=("dramx1",), writes=(bx,), dma=True)
                    pb.op("scalar", (lambda e, pp=pp: e.activation(
                        out=sq[:], in_=xp[:, pp], func=AF.Square)),
                        reads=(bx,), writes=("m0sq",))
                    def mmsum(e, pp=pp):
                        for dc in range(DC):
                            ins = e.matmul(ps[6][0:1, 0:512], onescol[:],
                                           xp[:, pp, dc], start=(dc == 0),
                                           stop=(dc == DC - 1))
                        return ins
                    pb.op("tensor", mmsum, reads=(bx, "onescol"), writes=("ps6",))
                    def mmsq(e):
                        for dc in range(DC):
                            ins = e.matmul(ps[7][0:1, 0:512], onescol[:],
                                           sq[:, dc], start=(dc == 0),
                                           stop=(dc == DC - 1))
                        return ins
                    pb.op("tensor", mmsq, reads=("m0sq", "onescol"), writes=("ps7",))
                    pb.op("vector", (lambda e: e.tensor_scalar(
                        out=rt[0:1, 0], in0=ps[6][0:1, 0:512],
                        scalar1=1.0 / D_MODEL, scalar2=None, op0=Op.mult, op1=Op.bypass)),
                        reads=("ps6",), writes=("m0rt",))
                    pb.op("vector", (lambda e: e.tensor_tensor(
                        out=rt[0:1, 1], in0=rt[0:1, 0], in1=rt[0:1, 0], op=Op.mult)),
                        reads=("m0rt",), writes=("m0rt",))
                    pb.op("vector", (lambda e: e.scalar_tensor_tensor(
                        out=rt[0:1, 1], in0=ps[7][0:1, 0:512], scalar=1.0 / D_MODEL,
                        in1=rt[0:1, 1], op0=Op.mult, op1=Op.subtract)),
                        reads=("ps7", "m0rt"), writes=("m0rt",))
                    pb.op("scalar", (lambda e: e.activation(
                        out=rt[0:1, 1], in_=rt[0:1, 1], func=AF.Ln,
                        bias=epscol[0:1, 0:1])),
                        reads=("m0rt", "epscol"), writes=("m0rt",))
                    pb.op("scalar", (lambda e: e.activation(
                        out=rt[0:1, 2], in_=rt[0:1, 1], func=AF.Exp, scale=-0.5)),
                        reads=("m0rt",), writes=("m0rt",))
                    pb.op("vector", (lambda e: e.scalar_tensor_tensor(
                        out=rt[0:1, 3], in0=rt[0:1, 0], scalar=-1.0,
                        in1=rt[0:1, 2], op0=Op.mult, op1=Op.mult)),
                        reads=("m0rt",), writes=("m0rt",))
                    bcast("m0", rt[0:1, 2], 512, 6, ("m0rt",))
                    bcast("m0", rt[0:1, 3], 512, 7, ("m0rt",))
                    def cpbc(e):
                        e.tensor_copy(bc2[:, 0], ps[6][0:P, 0:512])
                        return e.tensor_copy(bc2[:, 1], ps[7][0:P, 0:512])
                    pb.op("vector", cpbc, reads=("ps6", "ps7"), writes=("m0bc",))
                    def lna(e, pp=pp):
                        rep = bc2[:, 0].unsqueeze(1).broadcast_to((P, DC, 512))
                        return e.tensor_tensor(out=lnf[:], in0=xp[:, pp], in1=rep,
                                               op=Op.mult)
                    pb.op("vector", lna, reads=(bx, "m0bc"), writes=("m0lnf",))
                    def lnb(e):
                        rep = bc2[:, 1].unsqueeze(1).broadcast_to((P, DC, 512))
                        return e.tensor_tensor(out=lnf[:], in0=lnf[:], in1=rep, op=Op.add)
                    pb.op("vector", lnb, reads=("m0lnf", "m0bc"), writes=("m0lnf",))
                    pb.op("vector", (lambda e: e.tensor_copy(lhi[:], lnf[:])),
                          reads=("m0lnf",), writes=("m0lhi",))
                    pb.op("vector", (lambda e: e.tensor_tensor(
                        out=llo[:], in0=lnf[:], in1=lhi[:], op=Op.subtract)),
                        reads=("m0lnf", "m0lhi"), writes=("m0llo",))
                    pb.op("gpsimd", (lambda e, sl=sl: e.dma_start(
                        out=fmv(ln2hi)[:, :, sl], in_=lhi[:])),
                        reads=("m0lhi",), writes=("dramlnhi",), dma=True)
                    pb.op("gpsimd", (lambda e, sl=sl: e.dma_start(
                        out=fmv(ln2lo)[:, :, sl], in_=llo[:])),
                        reads=("m0llo",), writes=("dramlnlo",), dma=True)
                    for g in range(4):
                        G = i * 4 + g
                        t0g = g * 128
                        def rmm(e, t0g=t0g):
                            for dc in range(DC):
                                e.matmul(ps[5][0:P, 0:E], lhi[:, dc, t0g:t0g + 128],
                                         rwh[:, dc], start=(dc == 0), stop=False)
                                e.matmul(ps[5][0:P, 0:E], lhi[:, dc, t0g:t0g + 128],
                                         rwl[:, dc], start=False, stop=False)
                                ins = e.matmul(ps[5][0:P, 0:E], llo[:, dc, t0g:t0g + 128],
                                               rwh[:, dc], start=False,
                                               stop=(dc == DC - 1))
                            return ins
                        pb.op("tensor", rmm, reads=("m0lhi", "m0llo", "m0rwh", "m0rwl"),
                              writes=("ps5",))
                        pb.op("vector", (lambda e: e.tensor_reduce(
                            out=smc[:, 0:1], in_=ps[5][0:P, 0:E], axis=AX.X, op=Op.max)),
                            reads=("ps5",), writes=("m0smc",))
                        pb.op("vector", (lambda e: e.tensor_scalar(
                            out=sm[:], in0=ps[5][0:P, 0:E], scalar1=smc[:, 0:1],
                            scalar2=None, op0=Op.subtract, op1=Op.bypass)),
                            reads=("ps5", "m0smc"), writes=("m0sm",))
                        pb.op("scalar", (lambda e: e.activation(
                            out=sm[:], in_=sm[:], func=AF.Exp)),
                            reads=("m0sm",), writes=("m0sm",))
                        pb.op("vector", (lambda e: e.tensor_reduce(
                            out=smc[:, 0:1], in_=sm[:], axis=AX.X, op=Op.add)),
                            reads=("m0sm",), writes=("m0smc",))
                        pb.op("vector", (lambda e: e.reciprocal(
                            out=smc[:, 1:2], in_=smc[:, 0:1])),
                            reads=("m0smc",), writes=("m0smc",))
                        pb.op("vector", (lambda e, G=G: e.tensor_scalar(
                            out=r_sb[:, G * E:(G + 1) * E], in0=sm[:],
                            scalar1=smc[:, 1:2], scalar2=None,
                            op0=Op.mult, op1=Op.bypass)),
                            reads=("m0sm", "m0smc"), writes=("r_sb",))
                pb.barrier()

        def m1(ei):
            with contextlib.ExitStack() as st1:
                lhi = st1.enter_context(nc.sbuf_tensor(f"e{ei}lhi", [P, DC, T], BF16))
                llo = st1.enter_context(nc.sbuf_tensor(f"e{ei}llo", [P, DC, T], BF16))
                wf = st1.enter_context(nc.sbuf_tensor(f"e{ei}wf", [P, 2, DC, P], F32))
                whi = st1.enter_context(nc.sbuf_tensor(f"e{ei}whi", [P, 2, DC, P], BF16))
                wlo = st1.enter_context(nc.sbuf_tensor(f"e{ei}wlo", [P, 2, DC, P], BF16))
                pb.op("gpsimd", (lambda e: e.memset(whi[:], 0.0)),
                      writes=(f"e{ei}whi0", f"e{ei}whi1"))
                pb.op("gpsimd", (lambda e: e.memset(wlo[:], 0.0)),
                      writes=(f"e{ei}wlo0", f"e{ei}wlo1"))
                zst = st1.enter_context(nc.sbuf_tensor(f"e{ei}zst", [P, 2, T], F32))
                pb.op("sync", (lambda e: e.dma_start(out=lhi[:], in_=fmv(ln2hi))),
                      reads=("dramlnhi",), writes=(f"e{ei}lhi",), dma=True)
                pb.op("sync", (lambda e: e.dma_start(out=llo[:], in_=fmv(ln2lo))),
                      reads=("dramlnlo",), writes=(f"e{ei}llo",), dma=True)
                inwv = inw[ei].rearrange("(c p) f -> p c f", p=P)
                NFC = 35
                for fc in range(NFC):
                    fw = 128 if fc < 34 else 32
                    f0 = fc * 128
                    pp = fc % 2
                    bw = f"e{ei}wf{pp}"
                    bz = f"e{ei}zst{pp}"
                    pb.op("sync", (lambda e, f0=f0, fw=fw, pp=pp: e.dma_start(
                        out=wf[:, pp, :, 0:fw], in_=inwv[:, :, f0:f0 + fw])),
                        writes=(bw,), dma=True)
                    pb.op("vector", (lambda e, pp=pp, fw=fw: e.tensor_copy(
                        whi[:, pp, :, 0:fw], wf[:, pp, :, 0:fw])),
                        reads=(bw,), writes=(f"e{ei}whi{pp}",))
                    pb.op("vector", (lambda e, pp=pp, fw=fw: e.tensor_tensor(
                        out=wlo[:, pp, :, 0:fw], in0=wf[:, pp, :, 0:fw],
                        in1=whi[:, pp, :, 0:fw], op=Op.subtract)),
                        reads=(bw, f"e{ei}whi{pp}"), writes=(f"e{ei}wlo{pp}",))
                    for tg in range(T // 512):
                        sl = slice(tg * 512, (tg + 1) * 512)
                        bank = 4 + (tg % 2)
                        def pmm(e, pp=pp, sl=sl, bank=bank):
                            for dc in range(DC):
                                e.matmul(ps[bank][0:P, 0:512], whi[:, pp, dc, :],
                                         lhi[:, dc, sl], start=(dc == 0), stop=False)
                                e.matmul(ps[bank][0:P, 0:512], whi[:, pp, dc, :],
                                         llo[:, dc, sl], start=False, stop=False)
                                ins = e.matmul(ps[bank][0:P, 0:512], wlo[:, pp, dc, :],
                                               lhi[:, dc, sl], start=False,
                                               stop=(dc == DC - 1))
                            return ins
                        pb.op("tensor", pmm,
                              reads=(f"e{ei}whi{pp}", f"e{ei}wlo{pp}",
                                     f"e{ei}lhi", f"e{ei}llo"),
                              writes=(f"ps{bank}",))
                        pb.op("vector", (lambda e, fw=fw, sl=sl, pp=pp, bank=bank:
                                         e.tensor_copy(zst[0:fw, pp, sl],
                                                       ps[bank][0:fw, 0:512])),
                              reads=(f"ps{bank}",), writes=(bz,))
                    pb.op("gpsimd", (lambda e, f0=f0, fw=fw, pp=pp: e.dma_start(
                        out=zx[f0:f0 + fw, :], in_=zst[0:fw, pp, :])),
                        reads=(bz,), writes=("dramzx",), dma=True)
                pb.barrier()

        def m2(ei):
            with contextlib.ExitStack() as st1:
                owhi = st1.enter_context(nc.sbuf_tensor(f"o{ei}owhi", [P, DINC, 1024], BF16))
                owlo = st1.enter_context(nc.sbuf_tensor(f"o{ei}owlo", [P, DINC, 1024], BF16))
                cw = st1.enter_context(nc.sbuf_tensor(f"o{ei}cw", [P, CVC, 4], F32))
                cb = st1.enter_context(nc.sbuf_tensor(f"o{ei}cb", [P, CVC], F32))
                nwcol = st1.enter_context(nc.sbuf_tensor(f"o{ei}nw", [P, DINC], F32))
                dskb = st1.enter_context(nc.sbuf_tensor(f"o{ei}dsk", [P, NH], F32))
                acol = st1.enter_context(nc.sbuf_tensor(f"o{ei}acol", [NH, 2], F32))
                onesh = st1.enter_context(nc.sbuf_tensor(f"o{ei}ones", [NH, CH], F32))
                state = st1.enter_context(nc.sbuf_tensor(f"o{ei}st", [P, D_INNER], F32))
                pb.op("sync", (lambda e: e.dma_start(
                    out=cw[:, :, :], in_=cwm[ei].rearrange("(c p) k -> p c k", p=P))),
                    writes=(f"o{ei}cw",), dma=True)
                pb.op("sync", (lambda e: e.dma_start(
                    out=cb[:, :], in_=cbm[ei].rearrange("(c p) -> p c", p=P),
                    allow_slow_non_contiguous=True)),
                    writes=(f"o{ei}cb",), dma=True)
                pb.op("sync", (lambda e: e.dma_start(
                    out=nwcol[:, :], in_=nrmw[ei].rearrange("(c p) -> p c", p=P),
                    allow_slow_non_contiguous=True)),
                    writes=(f"o{ei}nw",), dma=True)
                pb.op("sync", (lambda e: e.dma_start(
                    out=dskb[:, :], in_=dskp[ei:ei + 1, :].partition_broadcast(P))),
                    writes=(f"o{ei}dsk",), dma=True)
                pb.op("sync", (lambda e: e.dma_start(
                    out=acol[:, 0:1], in_=alog[ei].unsqueeze(-1))),
                    writes=(f"o{ei}acol",), dma=True)
                pb.op("sync", (lambda e: e.dma_start(
                    out=acol[:, 1:2], in_=dtb[ei].unsqueeze(-1))),
                    writes=(f"o{ei}acol",), dma=True)
                pb.op("gpsimd", (lambda e: e.memset(onesh[:], 1.0)), writes=(f"o{ei}ones",))
                pb.op("gpsimd", (lambda e: e.memset(state[:], 0.0)), writes=(f"o{ei}st",))
                pb.op("scalar", (lambda e: e.activation(
                    out=acol[:, 0:1], in_=acol[:, 0:1], func=AF.Exp)),
                    reads=(f"o{ei}acol",), writes=(f"o{ei}acol",))
                owv = outw[ei].rearrange("(c p) d -> p c d", p=P)
                with contextlib.ExitStack() as st2:
                    owf = st2.enter_context(nc.sbuf_tensor(f"o{ei}owf", [P, 8, 1024], F32))
                    for pc2 in range(2):
                        pb.op("sync", (lambda e, pc2=pc2: e.dma_start(
                            out=owf[:, :, :], in_=owv[:, pc2 * 8:(pc2 + 1) * 8, :])),
                            writes=(f"o{ei}owf",), dma=True)
                        def nwsc(e, pc2=pc2):
                            for j in range(8):
                                ins = e.tensor_scalar(
                                    out=owf[:, j, :], in0=owf[:, j, :],
                                    scalar1=nwcol[:, pc2 * 8 + j:pc2 * 8 + j + 1],
                                    scalar2=None, op0=Op.mult, op1=Op.bypass)
                            return ins
                        pb.op("vector", nwsc, reads=(f"o{ei}owf", f"o{ei}nw"),
                              writes=(f"o{ei}owf",))
                        pb.op("vector", (lambda e, pc2=pc2: e.tensor_copy(
                            owhi[:, pc2 * 8:(pc2 + 1) * 8, :], owf[:, :, :])),
                            reads=(f"o{ei}owf",), writes=(f"o{ei}owhi",))
                        pb.op("vector", (lambda e, pc2=pc2: e.tensor_tensor(
                            out=owlo[:, pc2 * 8:(pc2 + 1) * 8, :], in0=owf[:, :, :],
                            in1=owhi[:, pc2 * 8:(pc2 + 1) * 8, :], op=Op.subtract)),
                            reads=(f"o{ei}owf", f"o{ei}owhi"), writes=(f"o{ei}owlo",))
                    dts = st2.enter_context(nc.sbuf_tensor(f"o{ei}dts", [NH, 6, 512], F32))
                    for ph in range(T // 512):
                        sl = slice(ph * 512, (ph + 1) * 512)
                        pb.op("sync", (lambda e, sl=sl: e.dma_start(
                            out=dts[:, 0, :], in_=zx[4352:4384, sl])),
                            reads=("dramzx",), writes=(f"o{ei}dts",), dma=True)
                        pb.op("vector", (lambda e: e.tensor_scalar(
                            out=dts[:, 1, :], in0=dts[:, 0, :], scalar1=acol[:, 1:2],
                            scalar2=0.0, op0=Op.add, op1=Op.max)),
                            reads=(f"o{ei}dts", f"o{ei}acol"), writes=(f"o{ei}dts",))
                        pb.op("scalar", (lambda e: e.activation(
                            out=dts[:, 2, :], in_=dts[:, 0, :], func=AF.Abs,
                            bias=acol[:, 1:2])),
                            reads=(f"o{ei}dts", f"o{ei}acol"), writes=(f"o{ei}dts",))
                        pb.op("scalar", (lambda e: e.activation(
                            out=dts[:, 2, :], in_=dts[:, 2, :], func=AF.Exp, scale=-1.0)),
                            reads=(f"o{ei}dts",), writes=(f"o{ei}dts",))
                        pb.op("scalar", (lambda e: e.activation(
                            out=dts[:, 2, :], in_=dts[:, 2, :], func=AF.Ln,
                            bias=onescol[0:NH, 0:1])),
                            reads=(f"o{ei}dts", "onescol"), writes=(f"o{ei}dts",))
                        pb.op("vector", (lambda e: e.tensor_tensor(
                            out=dts[:, 1, :], in0=dts[:, 1, :], in1=dts[:, 2, :],
                            op=Op.add)),
                            reads=(f"o{ei}dts",), writes=(f"o{ei}dts",))
                        pb.op("vector", (lambda e: e.tensor_scalar(
                            out=dts[:, 2, :], in0=dts[:, 1, :], scalar1=acol[:, 0:1],
                            scalar2=-1.0, op0=Op.mult, op1=Op.mult)),
                            reads=(f"o{ei}dts", f"o{ei}acol"), writes=(f"o{ei}dts",))
                        def csscan(e):
                            for cc in range(4):
                                slc = slice(cc * CH, (cc + 1) * CH)
                                ins = e.tensor_tensor_scan(
                                    out=dts[:, 3, slc], data0=onesh[:],
                                    data1=dts[:, 2, slc], initial=0.0,
                                    op0=Op.mult, op1=Op.add)
                            return ins
                        pb.op("vector", csscan, reads=(f"o{ei}dts", f"o{ei}ones"),
                              writes=(f"o{ei}dts",))
                        pb.op("scalar", (lambda e: e.activation(
                            out=dts[:, 5, :], in_=dts[:, 3, :], func=AF.Exp)),
                            reads=(f"o{ei}dts",), writes=(f"o{ei}dts",))
                        def dsx(e):
                            for cc in range(4):
                                slc = slice(cc * CH, (cc + 1) * CH)
                                ins = e.activation(
                                    out=dts[:, 4, slc], in_=dts[:, 3, slc],
                                    func=AF.Exp, scale=-1.0,
                                    bias=dts[:, 3, (cc + 1) * CH - 1:(cc + 1) * CH])
                            return ins
                        pb.op("scalar", dsx, reads=(f"o{ei}dts",), writes=(f"o{ei}dts",))
                        pb.op("vector", (lambda e: e.tensor_tensor(
                            out=dts[:, 2, :], in0=dts[:, 1, :], in1=dts[:, 4, :],
                            op=Op.mult)),
                            reads=(f"o{ei}dts",), writes=(f"o{ei}dts",))
                        for ri, row in ((0, 1), (1, 2), (2, 5), (3, 3)):
                            pb.op("gpsimd", (lambda e, ri=ri, row=row, sl=sl: e.dma_start(
                                out=dtsfl[ri * NH:(ri + 1) * NH, sl],
                                in_=dts[:, row, :])),
                                reads=(f"o{ei}dts",), writes=("dramdts",), dma=True)
                        pb.op("gpsimd", (lambda e, ph=ph: e.dma_start(
                            out=decfl[ph * 4:(ph + 1) * 4, :].transpose([1, 0]),
                            in_=dts[:, 5, CH - 1::CH],
                            allow_slow_non_contiguous=True)),
                            reads=(f"o{ei}dts",), writes=("dramdec",), dma=True)
                    pb.barrier()

                xbc = st1.enter_context(nc.sbuf_tensor(f"o{ei}xbc", [P, 2, CVC, 131], F32))
                zt = st1.enter_context(nc.sbuf_tensor(f"o{ei}zt", [P, DINC, CH], F32))
                csbc = st1.enter_context(nc.sbuf_tensor(f"o{ei}csbc", [P, NH, CH], F32))
                decb = st1.enter_context(nc.sbuf_tensor(f"o{ei}decb", [P, NH], F32))
                cvo = st1.enter_context(nc.sbuf_tensor(f"o{ei}cvo", [P, CVC, CH], F32))
                dtsl = st1.enter_context(nc.sbuf_tensor(f"o{ei}dtsl", [P, CH], F32))
                tsml = st1.enter_context(nc.sbuf_tensor(f"o{ei}tsml", [P, 4, NH], F32))
                xdtT = st1.enter_context(nc.sbuf_tensor(f"o{ei}xdtT", [P, 1, DINC, P], F32))
                xqT = st1.enter_context(nc.sbuf_tensor(f"o{ei}xqT", [P, DINC, P], F32))
                xskT = st1.enter_context(nc.sbuf_tensor(f"o{ei}xskT", [P, DINC, P], F32))
                bmT = st1.enter_context(nc.sbuf_tensor(f"o{ei}bmT", [P, P], F32))
                szT = st1.enter_context(nc.sbuf_tensor(f"o{ei}szT", [P, D_INNER], F32))
                cbt = st1.enter_context(nc.sbuf_tensor(f"o{ei}cbt", [P, P], F32))
                mh = st1.enter_context(nc.sbuf_tensor(f"o{ei}mh", [P, 2, 8, P], F32))
                ytok = st1.enter_context(nc.sbuf_tensor(f"o{ei}ytok", [P, D_INNER], F32))
                yfh = st1.enter_context(nc.sbuf_tensor(f"o{ei}yfh", [P, DINC, 2, P], BF16))
                yfl = st1.enter_context(nc.sbuf_tensor(f"o{ei}yfl", [P, DINC, 2, P], BF16))
                sscl = st1.enter_context(nc.sbuf_tensor(f"o{ei}sscl", [P, 4], F32))
                xnew = st1.enter_context(nc.sbuf_tensor(f"o{ei}xnew", [P, DC, 2 * CH], F32))

                pb.op("gpsimd", (lambda e: e.memset(xbc[:, :, :, 0:3], 0.0)),
                      writes=(f"o{ei}xbc0", f"o{ei}xbc1"))
                zxv = zx.rearrange("(c p) t -> p c t", p=P)
                for c in range(NCH):
                    pp = c % 2
                    t0 = c * CH
                    sl = slice(t0, t0 + CH)
                    bxb = f"o{ei}xbc{pp}"
                    btm, bct, bbm, bxd = (f"o{ei}tsml", f"o{ei}cbt",
                                          f"o{ei}bmT", f"o{ei}xdtT0")
                    bdec = f"o{ei}decb"
                    if c == 0:
                        pb.op("sync", (lambda e, pp=pp: e.dma_start(
                            out=xbc[:, pp, :, 3:131], in_=zxv[:, 16:34, 0:CH])),
                            reads=("dramzx",), writes=(bxb,), dma=True)
                    else:
                        pb.op("sync", (lambda e, pp=pp, t0=t0: e.dma_start(
                            out=xbc[:, pp, :, 0:131], in_=zxv[:, 16:34, t0 - 3:t0 + CH])),
                            reads=("dramzx",), writes=(bxb,), dma=True)
                    pb.op("sync", (lambda e, sl=sl: e.dma_start(
                        out=zt[:, :, :], in_=zxv[:, 0:DINC, sl])),
                        reads=("dramzx",), writes=(f"o{ei}zt",), dma=True)
                    pb.op("sync", (lambda e, sl=sl: e.dma_start(
                        out=dtsl[:, :], in_=dtsfl[0:P, sl])),
                        reads=("dramdts",), writes=(f"o{ei}dtsl",), dma=True)
                    pb.op("sync", (lambda e, sl=sl: e.dma_start(
                        out=csbc[:, :, :],
                        in_=dtsfl[96:128, sl].partition_broadcast(P))),
                        reads=("dramdts",), writes=(f"o{ei}csbc",), dma=True)
                    pb.op("sync", (lambda e, c=c: e.dma_start(
                        out=decb[:, :],
                        in_=decfl[c, :].partition_broadcast(P))),
                        reads=("dramdec",), writes=(bdec,), dma=True)
                    xsrc = x1 if ei == 0 else x2
                    pb.op("sync", (lambda e, sl=sl, xs=xsrc, pp=pp: e.dma_start(
                        out=xnew[:, :, pp * CH:(pp + 1) * CH], in_=fmv(xs)[:, :, sl])),
                        reads=("dramx1" if ei == 0 else "dramx2",),
                        writes=(f"o{ei}xnew",), dma=True)

                    def tap0(e, pp=pp):
                        for cc in range(CVC):
                            ins = e.activation(out=cvo[:, cc], in_=xbc[:, pp, cc, 0:CH],
                                               func=AF.Identity, scale=cw[:, cc, 0:1],
                                               bias=cb[:, cc:cc + 1])
                        return ins
                    pb.op("scalar", tap0, reads=(bxb, f"o{ei}cw", f"o{ei}cb"),
                          writes=(f"o{ei}cvo",))
                    for k in (1, 2, 3):
                        def tapk(e, k=k, pp=pp):
                            for cc in range(CVC):
                                ins = e.scalar_tensor_tensor(
                                    out=cvo[:, cc], in0=xbc[:, pp, cc, k:k + CH],
                                    scalar=cw[:, cc, k:k + 1], in1=cvo[:, cc],
                                    op0=Op.mult, op1=Op.add)
                            return ins
                        pb.op("vector", tapk, reads=(bxb, f"o{ei}cw", f"o{ei}cvo"),
                              writes=(f"o{ei}cvo",))
                    pb.op("scalar", (lambda e: e.activation(
                        out=cvo[:], in_=cvo[:], func=AF.Silu)),
                        reads=(f"o{ei}cvo",), writes=(f"o{ei}cvo",))
                    pb.op("scalar", (lambda e: e.activation(
                        out=zt[:], in_=zt[:], func=AF.Silu)),
                        reads=(f"o{ei}zt",), writes=(f"o{ei}zt",))

                    pb.op("tensor", (lambda e: e.transpose(
                        ps[2][0:P, 0:128], dtsl[:, :], ident[:])),
                        reads=(f"o{ei}dtsl", "ident"), writes=("ps2",))
                    pb.op("scalar", (lambda e: e.activation(
                        out=tsml[:, :, :], in_=ps[2][0:P, 0:128].rearrange(
                            "p (a b) -> p a b", a=4), func=AF.Copy, bias=0.0)),
                        reads=("ps2",), writes=(btm,))
                    pb.op("tensor", (lambda e: e.transpose(
                        ps[2][0:P, 128:256], cvo[:, 16, :], ident[:])),
                        reads=(f"o{ei}cvo", "ident"), writes=("ps2",))
                    pb.op("scalar", (lambda e: e.activation(
                        out=bmT[:, :], in_=ps[2][0:P, 128:256],
                        func=AF.Copy, bias=0.0)),
                        reads=("ps2",), writes=(bbm,))
                    pb.op("tensor", (lambda e: e.matmul(
                        ps[2][0:P, 256:384], cvo[:, 16, :], cvo[:, 17, :],
                        start=True, stop=True)),
                        reads=(f"o{ei}cvo",), writes=("ps2",))
                    pb.op("scalar", (lambda e: e.activation(
                        out=cbt[:, :], in_=ps[2][0:P, 256:384],
                        func=AF.Copy, bias=0.0)),
                        reads=("ps2",), writes=(bct,))
                    for r4 in range(4):
                        bank = 6 + (r4 % 2)
                        def xtr(e, r4=r4, bank=bank):
                            for j in range(4):
                                cc = r4 * 4 + j
                                ins = e.transpose(ps[bank][0:P, j * 128:(j + 1) * 128],
                                                  cvo[:, cc, :], ident[:])
                            return ins
                        pb.op("tensor", xtr, reads=(f"o{ei}cvo", "ident"),
                              writes=(f"ps{bank}",))
                        def xev(e, r4=r4, pp=pp, bank=bank):
                            pss = ps[bank][0:P, 0:512].rearrange("p (a b) -> p a b", a=8)
                            rep = tsml[:, 0, 8 * r4:8 * r4 + 8].unsqueeze(-1) \
                                .broadcast_to((P, 8, 64))
                            e.tensor_tensor(out=xdtT[:, 0, 4 * r4:4 * r4 + 4].rearrange(
                                "p a (c b) -> p (a c) b", b=64), in0=pss, in1=rep,
                                op=Op.mult)
                            rep2 = tsml[:, 1, 8 * r4:8 * r4 + 8].unsqueeze(-1) \
                                .broadcast_to((P, 8, 64))
                            e.tensor_tensor(out=xqT[:, 4 * r4:4 * r4 + 4].rearrange(
                                "p a (c b) -> p (a c) b", b=64), in0=pss, in1=rep2,
                                op=Op.mult)
                            rep3 = dskb[:, 8 * r4:8 * r4 + 8].unsqueeze(-1) \
                                .broadcast_to((P, 8, 64))
                            return e.tensor_tensor(out=xskT[:, 4 * r4:4 * r4 + 4].rearrange(
                                "p a (c b) -> p (a c) b", b=64), in0=pss, in1=rep3,
                                op=Op.mult)
                        pb.op("vector", xev, reads=(f"ps{bank}", btm, f"o{ei}dsk"),
                              writes=(bxd, f"o{ei}xqT", f"o{ei}xskT"))
                    for r4 in range(4):
                        bank = 6 + (r4 % 2)
                        def ztr(e, r4=r4, bank=bank):
                            for j in range(4):
                                cc = r4 * 4 + j
                                ins = e.transpose(ps[bank][0:P, j * 128:(j + 1) * 128],
                                                  zt[:, cc, :], ident[:])
                            return ins
                        pb.op("tensor", ztr, reads=(f"o{ei}zt", "ident"),
                              writes=(f"ps{bank}",))
                        pb.op("scalar", (lambda e, r4=r4, bank=bank: e.activation(
                            out=szT[:, r4 * 512:(r4 + 1) * 512],
                            in_=ps[bank][0:P, 0:512],
                            func=AF.Copy, bias=0.0)),
                            reads=(f"ps{bank}",), writes=(f"o{ei}szT",))

                    for g in range(4):
                        pb.op("tensor", (lambda e, g=g: e.matmul(
                            ps[g][0:P, 0:512], cvo[:, 17, :],
                            state[:, g * 512:(g + 1) * 512], start=True, stop=True)),
                            reads=(f"o{ei}cvo", f"o{ei}st"), writes=(f"ps{g}",))
                        def yev(e, g=g, pp=pp):
                            rep = tsml[:, 2, 8 * g:8 * g + 8].unsqueeze(-1) \
                                .broadcast_to((P, 8, 64))
                            return e.tensor_tensor(
                                out=ytok[:, g * 512:(g + 1) * 512].rearrange(
                                    "p (a b) -> p a b", a=8),
                                in0=ps[g][0:P, 0:512].rearrange("p (a b) -> p a b", a=8),
                                in1=rep, op=Op.mult)
                        pb.op("vector", yev, reads=(f"ps{g}", btm),
                              writes=(f"o{ei}ytok",))
                    for g in range(4):
                        bank = 4 + (g % 2)
                        pb.op("tensor", (lambda e, g=g, bank=bank: e.matmul(
                            ps[bank][0:P, 0:512], bmT[:, :],
                            xqT[:, 4 * g:4 * g + 4, :].rearrange("p a b -> p (a b)"),
                            start=True, stop=True)),
                            reads=(bbm, f"o{ei}xqT"), writes=(f"ps{bank}",))
                        def srec1(e, g=g, pp=pp):
                            rep = decb[:, 8 * g:8 * g + 8].unsqueeze(-1) \
                                .broadcast_to((P, 8, 64))
                            return e.tensor_tensor(
                                out=state[:, g * 512:(g + 1) * 512]
                                .rearrange("p (a b) -> p a b", a=8),
                                in0=state[:, g * 512:(g + 1) * 512]
                                .rearrange("p (a b) -> p a b", a=8),
                                in1=rep, op=Op.mult)
                        pb.op("vector", srec1, reads=(bdec, f"o{ei}st"),
                              writes=(f"o{ei}st",))
                        pb.op("vector", (lambda e, g=g, bank=bank: e.tensor_tensor(
                            out=state[:, g * 512:(g + 1) * 512],
                            in0=state[:, g * 512:(g + 1) * 512],
                            in1=ps[bank][0:P, 0:512], op=Op.add)),
                            reads=(f"ps{bank}", f"o{ei}st"), writes=(f"o{ei}st",))
                    pb.op("vector", (lambda e: e.tensor_tensor(
                        out=ytok[:], in0=ytok[:],
                        in1=xskT[:, :, :].rearrange("p a b -> p (a b)"), op=Op.add)),
                        reads=(f"o{ei}xskT", f"o{ei}ytok"), writes=(f"o{ei}ytok",))

                    pb.op("vector", (lambda e: e.tensor_tensor(
                        out=csbc[:], in0=csbc[:],
                        in1=masktile[:, :].unsqueeze(1).broadcast_to((P, NH, CH)),
                        op=Op.add)),
                        reads=(f"o{ei}csbc", "masktile"), writes=(f"o{ei}csbc",))
                    for g in range(4):
                        gp = g % 2
                        bmh = f"o{ei}mh{gp}"
                        def msub(e, g=g, gp=gp):
                            rep = tsml[:, 3, 8 * g:8 * g + 8].unsqueeze(-1) \
                                .broadcast_to((P, 8, CH))
                            return e.tensor_tensor(out=mh[:, gp],
                                                   in0=csbc[:, 8 * g:8 * g + 8, :],
                                                   in1=rep, op=Op.subtract)
                        pb.op("vector", msub, reads=(f"o{ei}csbc", btm),
                              writes=(bmh,))
                        pb.op("scalar", (lambda e, gp=gp: e.activation(
                            out=mh[:, gp], in_=mh[:, gp], func=AF.Exp)),
                            reads=(bmh,), writes=(bmh,))
                        def mmul(e, gp=gp):
                            rep = cbt[:, :].unsqueeze(1).broadcast_to((P, 8, CH))
                            return e.tensor_tensor(out=mh[:, gp], in0=mh[:, gp],
                                                   in1=rep, op=Op.mult)
                        pb.op("vector", mmul, reads=(bmh, bct), writes=(bmh,))
                        def ydmm(e, g=g, gp=gp):
                            for hcol in range(8):
                                h = g * 8 + hcol
                                ins = e.matmul(
                                    ps[g][0:P, hcol * 64:(hcol + 1) * 64],
                                    mh[:, gp, hcol, :],
                                    xdtT[:, 0, h // 2, (h % 2) * 64:(h % 2) * 64 + 64],
                                    start=True, stop=True)
                            return ins
                        pb.op("tensor", ydmm, reads=(bmh, bxd), writes=(f"ps{g}",))
                    for g in range(4):
                        pb.op("vector", (lambda e, g=g: e.tensor_tensor(
                            out=ytok[:, g * 512:(g + 1) * 512],
                            in0=ytok[:, g * 512:(g + 1) * 512],
                            in1=ps[g][0:P, 0:512], op=Op.add)),
                            reads=(f"ps{g}", f"o{ei}ytok"), writes=(f"o{ei}ytok",))

                    pb.op("vector", (lambda e: e.tensor_tensor(
                        out=ytok[:], in0=ytok[:], in1=szT[:], op=Op.mult)),
                        reads=(f"o{ei}szT", f"o{ei}ytok"), writes=(f"o{ei}ytok",))
                    pb.op("scalar", (lambda e: e.activation(
                        out=xskT[:, :, :].rearrange("p a b -> p (a b)"), in_=ytok[:],
                        func=AF.Square)),
                        reads=(f"o{ei}ytok",), writes=(f"o{ei}xskT",))
                    pb.op("vector", (lambda e: e.tensor_reduce(
                        out=sscl[:, 0:1], in_=xskT[:, :, :].rearrange("p a b -> p (a b)"),
                        axis=AX.X, op=Op.add)),
                        reads=(f"o{ei}xskT",), writes=(f"o{ei}sscl",))
                    pb.op("scalar", (lambda e: e.activation(
                        out=sscl[:, 1:2], in_=sscl[:, 0:1], func=AF.Ln,
                        scale=1.0 / D_INNER, bias=epscol[0:P, 0:1])),
                        reads=(f"o{ei}sscl", "epscol"), writes=(f"o{ei}sscl",))
                    pb.op("scalar", (lambda e: e.activation(
                        out=sscl[:, 2:3], in_=sscl[:, 1:2], func=AF.Exp, scale=-0.5)),
                        reads=(f"o{ei}sscl",), writes=(f"o{ei}sscl",))
                    pb.op("vector", (lambda e, c=c: e.tensor_tensor(
                        out=sscl[:, 3:4], in0=sscl[:, 2:3],
                        in1=r_sb[:, c * E + ei:c * E + ei + 1], op=Op.mult)),
                        reads=(f"o{ei}sscl", "r_sb"), writes=(f"o{ei}sscl",))
                    pb.op("vector", (lambda e: e.tensor_scalar(
                        out=ytok[:], in0=ytok[:], scalar1=sscl[:, 3:4], scalar2=None,
                        op0=Op.mult, op1=Op.bypass)),
                        reads=(f"o{ei}ytok", f"o{ei}sscl"), writes=(f"o{ei}ytok",))
                    for r4 in range(4):
                        bank = 6 + (r4 % 2)
                        def ytr(e, r4=r4, bank=bank):
                            for j in range(4):
                                cc = r4 * 4 + j
                                ins = e.transpose(ps[bank][0:P, j * 128:(j + 1) * 128],
                                                  ytok[:, cc * 128:(cc + 1) * 128],
                                                  ident[:])
                            return ins
                        pb.op("tensor", ytr, reads=(f"o{ei}ytok", "ident"),
                              writes=(f"ps{bank}",))
                        pb.op("scalar", (lambda e, r4=r4, bank=bank, pp=pp: e.activation(
                            out=yfh[:, 4 * r4:4 * r4 + 4, pp, :],
                            in_=ps[bank][0:P, 0:512].rearrange("p (a b) -> p a b", a=4),
                            func=AF.Copy, bias=0.0)),
                            reads=(f"ps{bank}",), writes=(f"o{ei}yfh{pp}",))
                        pb.op("vector", (lambda e, r4=r4, bank=bank, pp=pp: e.tensor_tensor(
                            out=yfl[:, 4 * r4:4 * r4 + 4, pp, :],
                            in0=ps[bank][0:P, 0:512].rearrange("p (a b) -> p a b", a=4),
                            in1=yfh[:, 4 * r4:4 * r4 + 4, pp, :], op=Op.subtract)),
                            reads=(f"ps{bank}", f"o{ei}yfh{pp}"), writes=(f"o{ei}yfl{pp}",))
                    if pp == 1:
                        for oc in range(DC):
                            bank = 4 + (oc % 4) // 2
                            slot = oc % 2
                            def omm(e, oc=oc, bank=bank, slot=slot):
                                out = ps[bank][0:P, slot * 256:slot * 256 + 256]
                                mvh = yfh.rearrange("p a b c -> p a (b c)")
                                mvl = yfl.rearrange("p a b c -> p a (b c)")
                                for dic in range(DINC):
                                    e.matmul(out, owhi[:, dic, oc * 128:(oc + 1) * 128],
                                             mvh[:, dic, :], start=(dic == 0), stop=False)
                                    e.matmul(out, owhi[:, dic, oc * 128:(oc + 1) * 128],
                                             mvl[:, dic, :], start=False, stop=False)
                                    ins = e.matmul(out,
                                                   owlo[:, dic, oc * 128:(oc + 1) * 128],
                                                   mvh[:, dic, :], start=False,
                                                   stop=(dic == DINC - 1))
                                return ins
                            pb.op("tensor", omm,
                                  reads=(f"o{ei}owhi", f"o{ei}owlo", f"o{ei}yfh0",
                                         f"o{ei}yfh1", f"o{ei}yfl0", f"o{ei}yfl1"),
                                  writes=(f"ps{bank}",))
                            pb.op("vector", (lambda e, oc=oc, bank=bank, slot=slot:
                                             e.tensor_tensor(
                                                 out=xnew[:, oc, :],
                                                 in0=xnew[:, oc, :],
                                                 in1=ps[bank][0:P, slot * 256:
                                                              slot * 256 + 256],
                                                 op=Op.add)),
                                  reads=(f"ps{bank}", f"o{ei}xnew"),
                                  writes=(f"o{ei}xnew",))
                        for hf in range(2):
                            pb.op("gpsimd", (lambda e, t0=t0, hf=hf: e.dma_start(
                                out=fmv(x2)[:, :, t0 - CH + hf * CH:t0 + hf * CH],
                                in_=xnew[:, :, hf * CH:(hf + 1) * CH])),
                                reads=(f"o{ei}xnew",), writes=("dramx2",), dma=True)

                pb.barrier()

        if "F1" in STAGES:
            ffn_call("f1", xT, x1, 0.5, False)
        else:
            pb.op("sync", (lambda e: e.dma_start(out=x1[:, :], in_=xT[:, :])), dma=True)
            pb.barrier()

        src_final = x1
        if "M0" in STAGES:
            m0()
        if "M1" in STAGES:
            for ei in range(E):
                m1(ei)
                if "M2" in STAGES:
                    m2(ei)
            src_final = x2

        if "F2" in STAGES:
            ffn_call("f2", src_final, out, 0.5, True)
        else:
            pb.op("sync", (lambda e, s=src_final: e.dma_start(out=out[:, :], in_=s[:, :])),
                  dma=True, force_mark=True)
            pb.barrier()

        pb.resolve()
        semctx = {s: stack.enter_context(nc.semaphore(s)) for s in SEMS}
        with nc.Block() as block:
            @block.sync
            def _(e):
                pb.emit("sync", e, semctx)

            @block.tensor
            def _(e):
                pb.emit("tensor", e, semctx)

            @block.vector
            def _(e):
                pb.emit("vector", e, semctx)

            @block.scalar
            def _(e):
                pb.emit("scalar", e, semctx)

            @block.gpsimd
            def _(e):
                pb.emit("gpsimd", e, semctx)
                for s in SEMS:
                    if s.startswith("s_l") or s.startswith("s_g"):
                        if pb.final.get(s, 0) > 0:
                            e.wait_ge(semctx[s], pb.final[s])
    return nc


def kernel(L0_attn_x, L0_attn_router_w, L0_attn_in_w, L0_attn_conv_w,
           L0_attn_conv_b, L0_attn_dt_bias, L0_attn_A_log, L0_attn_D_skip,
           L0_attn_norm_w, L0_attn_out_w, L0_ffn_w1, L0_ffn_conv_w,
           L0_ffn_conv_b, L0_ffn_w2):
    x = np.asarray(L0_attn_x)
    shared = {
        "w1T": np.ascontiguousarray(np.asarray(L0_ffn_w1).T),
        "w2T": np.ascontiguousarray(np.asarray(L0_ffn_w2).T),
        "cwf": np.ascontiguousarray(np.asarray(L0_ffn_conv_w)[:, 0, :]),
        "cbf": np.ascontiguousarray(np.asarray(L0_ffn_conv_b)),
        "rwT": np.ascontiguousarray(np.asarray(L0_attn_router_w)),
        "inw": np.ascontiguousarray(np.asarray(L0_attn_in_w)),
        "cwm": np.ascontiguousarray(np.asarray(L0_attn_conv_w)),
        "cbm": np.ascontiguousarray(np.asarray(L0_attn_conv_b)),
        "dtb": np.ascontiguousarray(np.asarray(L0_attn_dt_bias)),
        "alog": np.ascontiguousarray(np.asarray(L0_attn_A_log)),
        "dskp": np.ascontiguousarray(np.asarray(L0_attn_D_skip)),
        "nrmw": np.ascontiguousarray(np.asarray(L0_attn_norm_w)),
        "outw": np.ascontiguousarray(np.asarray(L0_attn_out_w)),
    }
    nc = build_nc()
    in_maps = []
    for b in range(8):
        m = dict(shared)
        m["xT"] = np.ascontiguousarray(x[b].T)
        in_maps.append(m)
    res = run_bass_kernel_spmd(nc, in_maps, core_ids=list(range(8)))
    outs = [np.ascontiguousarray(r["out"].T) for r in res.results]
    return np.stack(outs, axis=0)
